# version 01
# baseline (speedup 1.0000x reference)
from collections import defaultdict


COMPUTE = ("pe", "dve", "act")
QUEUES = ("sp", "gp")
N_DMA_SLOTS = 6


class Prog:
    def __init__(self, nc):
        self.nc = nc
        self.es = None
        self.streams = {k: [] for k in COMPUTE + QUEUES}
        self.sems = {}
        self.count = defaultdict(int)
        self.slot_rr = {q: 0 for q in QUEUES}
        self.last_w = {}
        self.readers = defaultdict(dict)
        self.wm = defaultdict(int)
        self.final_waits = []

    def init_sems(self, es, nc):
        names = list(COMPUTE) + [f"{q}{i}" for q in QUEUES for i in range(N_DMA_SLOTS)]
        for n in names:
            self.sems[n] = es.enter_context(nc.semaphore(n))

    @staticmethod
    def _k(b):
        if isinstance(b, tuple):
            return tuple(Prog._k(x) for x in b)
        if isinstance(b, (str, int)):
            return b
        n = getattr(b, "name", None)
        return n if n is not None else id(b)

    def _wait(self, stream, sem_name, cnt):
        if cnt <= 0:
            return
        if self.wm[(stream, sem_name)] >= cnt:
            return
        self.wm[(stream, sem_name)] = cnt
        sem = self.sems[sem_name]
        self.streams[stream].append(lambda e, sem=sem, cnt=cnt: e.wait_ge(sem, cnt))

    def _dep_waits(self, stream, my_sem, reads, writes):
        reads = [self._k(b) for b in reads]
        writes = [self._k(b) for b in writes]
        def need(other):
            if other != my_sem:
                return True
            return my_sem != "pe"
        for b in reads:
            w = self.last_w.get(b)
            if w is not None and need(w[0]):
                self._wait(stream, w[0], w[1])
        for b in writes:
            w = self.last_w.get(b)
            if w is not None and need(w[0]):
                self._wait(stream, w[0], w[1])
            for rs, rc in self.readers.get(b, {}).items():
                if need(rs):
                    self._wait(stream, rs, rc)

    def _record(self, my_sem, cnt, reads, writes):
        reads = [self._k(b) for b in reads]
        writes = [self._k(b) for b in writes]
        for b in reads:
            self.readers[b][my_sem] = cnt
        for b in writes:
            self.last_w[b] = (my_sem, cnt)
            self.readers[b] = {}

    def op(self, stream, fn, reads=(), writes=(), inc=True):
        assert stream in COMPUTE
        my_sem = stream
        self._dep_waits(stream, my_sem, reads, writes)
        self.count[my_sem] += 1
        cnt = self.count[my_sem]
        sem = None
        def emit(e, fn=fn, sname=my_sem, cnt=cnt):
            inst = fn(e)
            inst.then_inc(self.sems[sname], 1)
        self.streams[stream].append(emit)
        self._record(my_sem, cnt, reads, writes)
        return (my_sem, cnt)

    def ops(self, stream, fns, reads=(), writes=()):
        assert stream in COMPUTE
        my_sem = stream
        self._dep_waits(stream, my_sem, reads, writes)
        self.count[my_sem] += 1
        cnt = self.count[my_sem]
        n = len(fns)
        for i, fn in enumerate(fns):
            if i == n - 1:
                def emit(e, fn=fn, sname=my_sem):
                    fn(e).then_inc(self.sems[sname], 1)
            else:
                def emit(e, fn=fn):
                    fn(e)
            self.streams[stream].append(emit)
        self._record(my_sem, cnt, reads, writes)
        return (my_sem, cnt)

    def dma(self, q, out_ap, in_ap, reads=(), writes=(), slot=None, **kw):
        return self.dma_f(q, lambda e, o=out_ap, i=in_ap: e.dma_start(o, i, **kw),
                          reads=reads, writes=writes, slot=slot)

    def dma_f(self, q, fn, reads=(), writes=(), slot=None):
        assert q in QUEUES
        if slot is None:
            slot = self.slot_rr[q]
            self.slot_rr[q] = (slot + 1) % N_DMA_SLOTS
        my_sem = f"{q}{slot}"
        self._dep_waits(q, my_sem, reads, writes)
        self.count[my_sem] += 16
        cnt = self.count[my_sem]
        def emit(e, fn=fn, sname=my_sem):
            fn(e).then_inc(self.sems[sname], 16)
        self.streams[q].append(emit)
        self._record(my_sem, cnt, reads, writes)
        return (my_sem, cnt)

    def barrier(self):
        snap = dict(self.count)
        for stream in COMPUTE + QUEUES:
            for sem_name, cnt in snap.items():
                if sem_name == stream:
                    continue
                self._wait(stream, sem_name, cnt)

    def wait_all_on(self, stream):
        snap = dict(self.count)
        for sem_name, cnt in snap.items():
            if sem_name != stream:
                self._wait(stream, sem_name, cnt)

    def finish(self, block):
        self.wait_all_on("sp")
        streams = self.streams

        @block.tensor
        def _(e):
            for f in streams["pe"]:
                f(e)

        @block.vector
        def _(e):
            for f in streams["dve"]:
                f(e)

        @block.scalar
        def _(e):
            for f in streams["act"]:
                f(e)

        @block.gpsimd
        def _(e):
            for f in streams["gp"]:
                f(e)

        @block.sync
        def _(e):
            for f in streams["sp"]:
                f(e)


"""Kernel builder: full fused megakernel (developed incrementally).

Layouts / phases per the design:
  R   : replicated global routing (all 32768 tokens)
  1aA : LN + in_proj(B,C,xh,last) + scalars + lambda scan   [exp/ln ACT set]
  1aB : theta scan + trig + silu(z)-proj + v-mix            [silu/sin/tanh set]
  P15 : scatter xn rows into expert slot array
  1b  : chunked SSD scan + out_proj -> s                    [exp set]
  P2  : MoE expert macro-tiles (gathered cached weights)    [silu set]
  P3  : gather-back + shared expert + final combine         [silu set]
"""
import os
import sys
import numpy as np

sys.path.insert(0, "/opt/trn_rl_repo")
import concourse.bass as bass
import concourse.mybir as mybir
from concourse.bass_utils import run_bass_kernel_spmd
from concourse.bass_core_rust import IndirectOffsetOnAxis
from contextlib import ExitStack
import ml_dtypes


F32 = mybir.dt.float32
BF16 = mybir.dt.bfloat16
I32 = mybir.dt.int32
U32 = mybir.dt.uint32
ALU = mybir.AluOpType
AF = mybir.ActivationFunctionType
AX = mybir.AxisListType

D = 1024
DIN = 2048
DPRJ = 4480
H = 32
PD = 64
R_ = 32
NST = 128
T = 4096
SEQ = 2048
NSEQ = 2
Q = 128
NCH = 16
NIT = 32
NTOK_G = 32768
E = 16
FF = 1408
CGLOB = 5120
MT = 512
NMT = 32
NSLOT = NMT * MT
NCORES = 8

BIGNEG = -1.0e30


def host_constants():
    bf = ml_dtypes.bfloat16
    c = {}
    c["C_ID"] = np.eye(128, dtype=np.float32).astype(bf)
    tri = np.where(np.arange(128)[None, :] >= np.arange(128)[:, None], 0.0, BIGNEG)
    c["C_TRINEG"] = tri.astype(np.float32)
    c["C_ONES1"] = np.ones((1, 128), dtype=np.float32).astype(bf)
    m = np.arange(128)
    c["C_PDUP"] = (np.arange(32)[:, None] == (m[None, :] % 32)).astype(np.float32).astype(bf)
    c["C_PR"] = ((np.arange(32)[:, None] % 2) == (m[None, :] >= 64)).astype(np.float32).astype(bf)
    c["C_HALF"] = ((np.arange(32)[:, None] // 2) == np.arange(16)[None, :]).astype(np.float32).astype(bf)
    c["C_IOTAE"] = np.tile(np.arange(16, dtype=np.float32), (128, 1))
    c["C_ECOL"] = (np.arange(128, dtype=np.float32) % 16).reshape(128, 1)
    c["C_GSEL"] = ((np.arange(128)[:, None] // 16) == np.arange(8)[None, :]).astype(np.float32)
    kk = np.arange(128)
    c["C_OFFM"] = (((kk[:, None] % 16) == (kk[None, :] % 16)) &
                   ((kk[:, None] // 16) < (kk[None, :] // 16))).astype(np.float32)
    c["C_IOTA128"] = np.arange(128, dtype=np.float32).reshape(128, 1)
    c["C_QCOL"] = np.array([[0.25]*32 + [0.0]*32 + [0.25]*32 + [0.0]*32], dtype=np.float32).T
    c["C_SHIFT"] = (np.arange(128)[:, None] == (np.arange(128)[None, :] - 1)).astype(np.float32).astype(bf)
    c["C_E127"] = (np.arange(128) == 127).astype(np.float32).astype(bf).reshape(128, 1)
    c["C_PI2"] = np.full((128, 1), 2.0 * np.pi, dtype=np.float32)
    c["C_REP8"] = ((np.arange(128)[None, :] // 16) == np.arange(8)[:, None]).astype(np.float32).astype(bf)
    c["C_LEX16"] = (np.arange(16)[:, None] < np.arange(16)[None, :]).astype(np.float32)
    c["C_IOTAT"] = np.tile(np.arange(32, dtype=np.float32), (16, 1))
    c["C_ECOL16"] = np.arange(16, dtype=np.float32).reshape(16, 1)
    c["C_ONES1F"] = np.ones((1, 128), dtype=np.float32)
    c["C_ONESC"] = np.ones((128, 1), dtype=np.float32).astype(bf)
    c["C_IDF"] = np.eye(128, dtype=np.float32)
    c["C_ONESCF"] = np.ones((128, 1), dtype=np.float32)
    c["C_ONES16F"] = np.ones((16, 1), dtype=np.float32)
    c["C_SCL"] = np.zeros((128, 8), dtype=np.float32)
    c["C_SCL"][:, 0] = 1.0 / D
    c["C_SCL"][:, 1] = -0.5
    c["C_SCL"][:, 2] = 1e-5
    c["C_SCL"][:, 3] = 1e-9
    return c


def per_core_inputs(core):
    c = {}
    sel16 = np.zeros((128, 16), dtype=np.float32)
    for j in range(16):
        sel16[core * 16 + j, j] = 1.0
    c["C_SEL16"] = sel16
    sel8 = np.zeros((8, 1), dtype=np.float32)
    sel8[core, 0] = 1.0
    c["C_SEL8"] = sel8
    c["C_BASE"] = np.array([[core * T, core, 0, 0]], dtype=np.float32)
    return c


def build(cfg):
    dbg = cfg.get("dbg", set())
    nc = bass.Bass("TRN2", target_bir_lowering=False, detect_race_conditions=False)

    inp = {}
    def EI(name, shape, dt=F32):
        inp[name] = nc.dram_tensor(name, list(shape), dt, kind="ExternalInput").ap()
        return inp[name]

    xall = EI("xall", [NTOK_G, D])
    xloc = EI("xloc", [T, D])
    w_in = EI("w_in", [DPRJ, D])
    w_out = EI("w_out", [D, DIN])
    rw = EI("rw", [D, E])
    rb = EI("rb", [1, E])
    dtb = EI("dtb", [1, H])
    Dv = EI("Dv", [1, H])
    wg = EI("wg", [E, D, FF])
    wu = EI("wu", [E, D, FF])
    wd = EI("wd", [E, FF, D])
    swg = EI("swg", [D, FF])
    swu = EI("swu", [D, FF])
    swd = EI("swd", [FF, D])
    for k, v in host_constants().items():
        EI(k, v.shape, BF16 if v.dtype == ml_dtypes.bfloat16 else F32)
    EI("C_SEL16", [128, 16]); EI("C_SEL8", [8, 1]); EI("C_BASE", [1, 4])

    OUT = nc.dram_tensor("out", [T, D], F32, kind="ExternalOutput").ap()
    dbg_out = {}
    def DBG(name, shape, dt=F32):
        if name in dbg:
            dbg_out[name] = nc.dram_tensor("dbg_" + name, list(shape), dt, kind="ExternalOutput").ap()
            return dbg_out[name]
        return None

    dbgd = cfg.get("dbg_drams", set())
    def DRAM(name, shape, dt):
        kind = "ExternalOutput" if name in dbgd else "Internal"
        return nc.dram_tensor(name, list(shape), dt, kind=kind).ap()

    d_xn16 = DRAM("d_xn16", [T, D], BF16)
    d_xh16 = DRAM("d_xh16", [T, DIN], BF16)
    d_vmix = DRAM("d_vmix", [T, DIN], BF16)
    d_zsil = DRAM("d_zsil", [T, DIN], BF16)
    d_trigF = DRAM("d_trigF", [NSEQ, 16, 128, SEQ], BF16)
    d_trigT = DRAM("d_trigT", [T, DIN], BF16)
    d_lam = DRAM("d_lam", [NIT, 4096], F32)
    d_dtf = DRAM("d_dtf", [H, T], F32)
    d_tanf = DRAM("d_tanf", [H, T], F32)
    d_s16 = DRAM("d_s16", [T, D], BF16)
    d_slotin = DRAM("d_slotin", [NSLOT, D], BF16)
    d_eout = DRAM("d_eout", [NSLOT, D], BF16)
    d_wgC = DRAM("d_wgC", [E * 128, 8 * FF], BF16)
    d_wuC = DRAM("d_wuC", [E * 128, 8 * FF], BF16)
    d_wdC = DRAM("d_wdC", [E * 128, 11 * 1024], BF16)
    d_tokid = DRAM("d_tokid", [NSLOT + 128, 8], I32)
    d_st = DRAM("d_st", [T, 2], F32)
    d_rt = DRAM("d_rt", [4, NTOK_G], F32)
    d_my = DRAM("d_my", [6, T], F32)
    d_tile_e = DRAM("d_tile_e", [1, NMT], F32)

    es = ExitStack()
    P = Prog(nc)
    P.init_sems(es, nc)

    SB = {}
    scopes = [es]
    def push_scope():
        scopes.append(ExitStack())
    def pop_scope():
        P.barrier()
        scopes.pop().close()
    def sb(name, shape, dt):
        t = scopes[-1].enter_context(nc.sbuf_tensor(name, list(shape), dt))
        SB[name] = t
        return t

    def sbroot(name, shape, dt):
        t = scopes[0].enter_context(nc.sbuf_tensor(name, list(shape), dt))
        SB[name] = t
        return t

    def ps(name, shape, dt=F32):
        t = scopes[-1].enter_context(nc.psum_tensor(name, list(shape), dt))
        SB[name] = t
        return t

    cst = {}
    for name, shape, dt in [
        ("C_ID", [128, 128], BF16), ("C_TRINEG", [128, 128], F32),
        ("C_ONES1", [1, 128], BF16), ("C_PDUP", [32, 128], BF16),
        ("C_PR", [32, 128], BF16), ("C_HALF", [32, 16], BF16),
        ("C_IOTAE", [128, 16], F32), ("C_ECOL", [128, 1], F32),
        ("C_GSEL", [128, 8], F32), ("C_OFFM", [128, 128], F32),
        ("C_IOTA128", [128, 1], F32), ("C_QCOL", [128, 1], F32),
        ("C_SHIFT", [128, 128], BF16), ("C_E127", [128, 1], BF16),
        ("C_PI2", [128, 1], F32), ("C_SCL", [128, 8], F32),
        ("C_SEL16", [128, 16], F32), ("C_SEL8", [8, 1], F32),
        ("C_REP8", [8, 128], BF16), ("C_LEX16", [16, 16], F32),
        ("C_IOTAT", [16, 32], F32), ("C_ECOL16", [16, 1], F32),
        ("C_ONES1F", [1, 128], F32), ("C_ONES16F", [16, 1], F32),
        ("C_ONESC", [128, 1], BF16), ("C_ONESCF", [128, 1], F32), ("C_IDF", [128, 128], F32),
    ]:
        cst[name] = sb("sb_" + name, shape, dt)
        P.dma("sp", cst[name][:, :], inp[name], writes=[cst[name]])

    vec = {}
    for name, shape in [("rb", [1, E]), ("dtb", [1, H]), ("Dv", [1, H])]:
        vec[name] = sb("v_" + name, shape, F32)
        P.dma("sp", vec[name][:, :], inp[name], writes=[vec[name]])

    zc128 = sbroot("zc128", [128, 1], F32)
    P.op("dve", lambda e: e.memset(zc128[:, :], 0.0), writes=[zc128])
    ret_zc = zc128
    P.barrier()
    bes = ExitStack()
    block = bes.enter_context(nc.Block())
    ret_bes = bes
    ret = {"nc": nc, "P": P, "es": es, "bes": ret_bes, "block": block, "sb": sb, "ps": ps, "SB": SB,
           "push_scope": push_scope, "pop_scope": pop_scope, "sbroot": sbroot, "zc128": ret_zc,
           "cst": cst, "vec": vec, "inp": inp, "OUT": OUT, "DBG": DBG, "dbg_out": dbg_out,
           "drams": dict(d_xn16=d_xn16, d_xh16=d_xh16, d_vmix=d_vmix, d_zsil=d_zsil,
                         d_trigF=d_trigF, d_trigT=d_trigT, d_lam=d_lam, d_dtf=d_dtf,
                         d_tanf=d_tanf, d_s16=d_s16, d_slotin=d_slotin, d_eout=d_eout,
                         d_wgC=d_wgC, d_wuC=d_wuC, d_wdC=d_wdC, d_rt=d_rt, d_my=d_my,
                         d_tile_e=d_tile_e, d_tokid=d_tokid, d_st=d_st)}
    return ret




def phase_routing(ctx):
    nc, P, sb, ps, cst, inp, DBG = (ctx["nc"], ctx["P"], ctx["sb"], ctx["ps"],
                                    ctx["cst"], ctx["inp"], ctx["DBG"])
    dr = ctx["drams"]
    xall = inp["xall"]
    ctx["push_scope"]()

    rxb = [sb(f"rxb{i}", [128, 1024], BF16) for i in range(3)]
    rxf = [sb(f"rxf{i}", [128, 1024], BF16) for i in range(2)]
    rwo = sb("rwo", [128, 8 * 17], BF16)
    sqs = [sb(f"sqs{i}", [128, 1024], BF16) for i in range(2)]
    logit = sb("logit", [128, 4096], F32)
    probs = sb("probs", [128, 4096], F32)
    musum = sb("musum", [128, 256], F32)
    ssq = sb("ssq", [128, 256], F32)
    rtA = sb("rtA", [128, 4096], F32)
    rtB = sb("rtB", [128, 4096], F32)
    small = sb("small", [128, 1536], F32)
    MU, RSTD, M1, M2, E0, E1V, RDEN = [slice(i * 256, (i + 1) * 256) for i in range(7)] if False else [
        slice(0, 256), slice(256, 512), slice(512, 768), slice(768, 1024), slice(1024, 1280), slice(1280, 1536), None]
    cs16 = sb("cs16", [128, 16], F32)
    rb16 = sb("rb16", [128, 16], F32)
    pl = [ps(f"pl{i}", [128, 160], F32) for i in range(2)]
    csrow = sb("csrow", [1, 17], F32)
    rbb = sb("rbb", [1, 16], BF16)
    prep = ps("prep", [128, 32], F32)
    pcs = ps("pcs", [1, 32], F32)

    P.op("dve", lambda e: e.memset(rwo[:, :], 0.0), writes=[rwo])
    for kc in range(8):
        P.dma("gp", rwo[:, kc * 17: kc * 17 + 16], inp["rw"][kc * 128:(kc + 1) * 128, :],
              reads=[rwo], writes=[rwo])
    P.op("dve", lambda e: e.memset(bass.AP(rwo, 16, [[rwo.shape[1], 128], [17, 8]]), 1.0),
         reads=[rwo], writes=[rwo])
    P.ops("pe", [
        (lambda e, kc=kc: e.matmul(pcs[0:1, 0:17], cst["C_ONESC"][:, :],
                                   rwo[:, kc * 17:(kc + 1) * 17],
                                   start=(kc == 0), stop=(kc == 7)))
        for kc in range(8)], reads=[rwo], writes=[pcs])
    P.op("dve", lambda e: e.tensor_copy(csrow[:, :], pcs[0:1, 0:17]), reads=[pcs], writes=[csrow])
    P.op("dve", lambda e: e.tensor_copy(rbb[:, :], ctx["vec"]["rb"][:, :]),
         reads=[ctx["vec"]["rb"]], writes=[rbb])
    csrow16 = sb("csrow16", [1, 16], BF16)
    P.op("dve", lambda e: e.tensor_copy(csrow16[:, :], csrow[0:1, 0:16]), reads=[csrow], writes=[csrow16])
    P.op("pe", lambda e: e.matmul(prep[:, 0:16], cst["C_ONES1"][0:1, :], csrow16[0:1, :],
                                  start=True, stop=True), reads=[csrow16], writes=[prep])
    P.op("pe", lambda e: e.matmul(prep[:, 16:32], cst["C_ONES1"][0:1, :], rbb[0:1, :],
                                  start=True, stop=True), reads=[rbb], writes=[prep])
    P.op("dve", lambda e: e.tensor_copy(cs16[:, :], prep[:, 0:16]), reads=[prep], writes=[cs16])
    P.op("dve", lambda e: e.tensor_copy(rb16[:, :], prep[:, 16:32]), reads=[prep], writes=[rb16])

    for t in range(256):
        b = t % 3
        f = t % 2
        g8 = t % 8
        pt = pl[(t // 8) % 2]
        P.dma("gp", rxb[b][:, :], xall[t * 128:(t + 1) * 128, :],
              writes=[rxb[b]], slot=b)
        P.dma_f("sp", lambda e, b=b, f=f: e.dma_start_transpose(
            rxf[f][:, :].rearrange("p (kc t2) -> p kc t2", kc=8),
            rxb[b][:, :].rearrange("t2 (kc d2) -> t2 kc d2", kc=8)),
            reads=[rxb[b]], writes=[rxf[f]], slot=3 + f)
        P.ops("pe", [
            (lambda e, kc=kc, f=f, pt=pt, g8=g8: e.matmul(
                pt[:, g8 * 17:(g8 + 1) * 17], rxf[f][:, kc * 128:(kc + 1) * 128],
                rwo[:, kc * 17:(kc + 1) * 17], start=(kc == 0), stop=(kc == 7)))
            for kc in range(8)], reads=[rxf[f], rwo], writes=[(pt, g8)])
        sq = sqs[t % 2]
        P.op("act", lambda e, t=t, b=b, sq=sq: e.activation(
            sq[:, :], rxb[b][:, :], AF.Square, accum_out=ssq[:, t:t + 1]),
            reads=[rxb[b]], writes=[sq, ssq])
        if g8 == 7:
            t0 = t - 7
            P.op("dve", lambda e, t0=t0, pt=pt: e.tensor_copy(
                bass.AP(logit, t0 * 16, [[logit.shape[1], 128], [17 - 1, 0]][0:1] + [[16, 8], [1, 16]]),
                bass.AP(pt, 0, [[pt.shape[1], 128], [17, 8], [1, 16]])),
                reads=[(pt, g) for g in range(8)], writes=[logit])
            P.op("dve", lambda e, t0=t0, pt=pt: e.tensor_copy(
                bass.AP(musum, t0, [[musum.shape[1], 128], [1, 8]]),
                bass.AP(pt, 16, [[pt.shape[1], 128], [17, 8]])),
                reads=[(pt, g) for g in range(8)], writes=[musum])

    sc = cst["C_SCL"]
    P.op("dve", lambda e: e.tensor_scalar(small[:, MU], musum[:, :], sc[:, 0:1], None, ALU.mult),
         reads=[musum, sc], writes=[small])
    P.op("dve", lambda e: e.tensor_scalar(small[:, M1], ssq[:, :], sc[:, 0:1], None, ALU.mult),
         reads=[ssq, sc], writes=[small])
    P.op("dve", lambda e: e.tensor_tensor(small[:, M2], small[:, MU], small[:, MU], ALU.mult),
         reads=[small], writes=[small])
    P.op("dve", lambda e: e.tensor_tensor(small[:, M1], small[:, M1], small[:, M2], ALU.subtract),
         reads=[small], writes=[small])
    P.op("dve", lambda e: e.tensor_scalar(small[:, M1], small[:, M1], sc[:, 2:3], None, ALU.add),
         reads=[small, sc], writes=[small])
    P.op("act", lambda e: e.activation(small[:, M2], small[:, M1], AF.Ln),
         reads=[small], writes=[small])
    P.op("act", lambda e: e.activation(small[:, RSTD], small[:, M2], AF.Exp, scale=sc[:, 1:2]),
         reads=[small, sc], writes=[small])
    muv = lambda: bass.AP(small, 0, [[small.shape[1], 128], [1, 256], [0, 16]])
    rstdv = lambda: bass.AP(small, 256, [[small.shape[1], 128], [1, 256], [0, 16]])
    cs16v = lambda: bass.AP(cs16, 0, [[cs16.shape[1], 128], [0, 256], [1, 16]])
    rb16v = lambda: bass.AP(rb16, 0, [[rb16.shape[1], 128], [0, 256], [1, 16]])
    P.op("dve", lambda e: e.tensor_tensor(rtA[:, :], muv(), cs16v(), ALU.mult),
         reads=[small, cs16], writes=[rtA])
    P.op("dve", lambda e: e.tensor_tensor(logit[:, :], logit[:, :], rtA[:, :], ALU.subtract),
         reads=[logit, rtA], writes=[logit])
    P.op("dve", lambda e: e.tensor_tensor(logit[:, :], logit[:, :], rstdv(), ALU.mult),
         reads=[logit, small], writes=[logit])
    P.op("dve", lambda e: e.tensor_tensor(logit[:, :], logit[:, :], rb16v(), ALU.add),
         reads=[logit, rb16], writes=[logit])
    P.op("act", lambda e: e.activation(probs[:, :], logit[:, :], AF.Exp),
         reads=[logit], writes=[probs])
    l3 = lambda t_: bass.AP(t_, 0, [[t_.shape[1], 128], [16, 256], [1, 16]])
    P.op("dve", lambda e: e.tensor_reduce(bass.AP(small, 512, [[small.shape[1], 128], [1, 256]]),
                                          l3(probs), AX.X, ALU.add),
         reads=[probs], writes=[small])
    P.op("dve", lambda e: e.reciprocal(small[:, M2], small[:, M1]), reads=[small], writes=[small])
    P.op("dve", lambda e: e.tensor_tensor(probs[:, :], probs[:, :],
                                          bass.AP(small, 768, [[small.shape[1], 128], [1, 256], [0, 16]]),
                                          ALU.mult),
         reads=[probs, small], writes=[probs])
    iotav = lambda: bass.AP(cst["C_IOTAE"], 0, [[cst["C_IOTAE"].shape[1], 128], [0, 256], [1, 16]])
    P.op("dve", lambda e: e.tensor_reduce(bass.AP(small, 512, [[small.shape[1], 128], [1, 256]]),
                                          l3(probs), AX.X, ALU.max),
         reads=[probs], writes=[small])
    m1v = lambda: bass.AP(small, 512, [[small.shape[1], 128], [1, 256], [0, 16]])
    m2v = lambda: bass.AP(small, 768, [[small.shape[1], 128], [1, 256], [0, 16]])
    P.op("dve", lambda e: e.tensor_tensor(rtA[:, :], probs[:, :], m1v(), ALU.not_equal),
         reads=[probs, small], writes=[rtA])
    P.op("dve", lambda e: e.scalar_tensor_tensor(rtA[:, :], rtA[:, :], 1e6, iotav(), ALU.mult, ALU.add),
         reads=[rtA, cst["C_IOTAE"]], writes=[rtA])
    P.op("dve", lambda e: e.tensor_reduce(bass.AP(small, 1024, [[small.shape[1], 128], [1, 256]]),
                                          l3(rtA), AX.X, ALU.min),
         reads=[rtA], writes=[small])
    e0v = lambda: bass.AP(small, 1024, [[small.shape[1], 128], [1, 256], [0, 16]])
    P.op("dve", lambda e: e.tensor_tensor(rtB[:, :], iotav(), e0v(), ALU.not_equal),
         reads=[cst["C_IOTAE"], small], writes=[rtB])
    P.op("dve", lambda e: e.tensor_tensor(rtB[:, :], rtB[:, :], probs[:, :], ALU.mult),
         reads=[rtB, probs], writes=[rtB])
    P.op("dve", lambda e: e.tensor_reduce(bass.AP(small, 768, [[small.shape[1], 128], [1, 256]]),
                                          l3(rtB), AX.X, ALU.max),
         reads=[rtB], writes=[small])
    P.op("dve", lambda e: e.tensor_tensor(rtB[:, :], rtB[:, :], m2v(), ALU.not_equal),
         reads=[rtB, small], writes=[rtB])
    P.op("dve", lambda e: e.scalar_tensor_tensor(rtB[:, :], rtB[:, :], 1e6, iotav(), ALU.mult, ALU.add),
         reads=[rtB, cst["C_IOTAE"]], writes=[rtB])
    P.op("dve", lambda e: e.tensor_reduce(bass.AP(small, 1280, [[small.shape[1], 128], [1, 256]]),
                                          l3(rtB), AX.X, ALU.min),
         reads=[rtB], writes=[small])
    vden = sb("vden", [128, 256], F32)
    P.op("dve", lambda e: e.tensor_tensor(vden[:, :], small[:, M1], small[:, M2], ALU.add),
         reads=[small], writes=[vden])
    P.op("dve", lambda e: e.tensor_scalar(vden[:, :], vden[:, :], sc[:, 3:4], None, ALU.add),
         reads=[vden, sc], writes=[vden])
    P.op("dve", lambda e: e.reciprocal(vden[:, :], vden[:, :]), reads=[vden], writes=[vden])
    v0tm = sb("v0tm", [128, 256], F32)
    v1tm = sb("v1tm", [128, 256], F32)
    P.op("dve", lambda e: e.tensor_tensor(v0tm[:, :], small[:, M1], vden[:, :], ALU.mult),
         reads=[small, vden], writes=[v0tm])
    P.op("dve", lambda e: e.tensor_tensor(v1tm[:, :], small[:, M2], vden[:, :], ALU.mult),
         reads=[small, vden], writes=[v1tm])
    ptr = ps("ptr", [128, 256], BF16)
    trf = sb("trf", [128, 256], F32)
    items = [(bass.AP(small, 1024, [[small.shape[1], 128], [1, 256]]), 0),
             (bass.AP(small, 1280, [[small.shape[1], 128], [1, 256]]), 1),
             (v0tm[:, :], 2), (v1tm[:, :], 3)]
    idb = sb("idbf", [128, 256], BF16)
    for ap, row in items:
        P.op("dve", lambda e, ap=ap: e.tensor_copy(idb[:, :], ap), reads=[small, v0tm, v1tm], writes=[idb])
        for hhh in range(2):
            P.op("pe", lambda e, hhh=hhh: e.transpose(
                bass.AP(ptr, hhh * 128, [[ptr.shape[1], 128], [1, 128]]),
                idb[:, hhh * 128:(hhh + 1) * 128], cst["C_ID"][:, :]),
                reads=[idb], writes=[(ptr, hhh)])
        P.op("dve", lambda e: e.tensor_copy(trf[:, :], ptr[:, 0:256]),
             reads=[(ptr, 0), (ptr, 1)], writes=[trf])
        P.dma("sp", bass.AP(dr["d_rt"].tensor, row * NTOK_G, [[128, 128], [16384, 2], [1, 128]]),
              trf[:, 0:256], reads=[trf], writes=[("d_rt", row)])
    ctx["pop_scope"]()
    return


def phase_routing2(ctx):
    nc, P, sb, ps, cst, inp = ctx["nc"], ctx["P"], ctx["sb"], ctx["ps"], ctx["cst"], ctx["inp"]
    dr = ctx["drams"]
    sbroot = ctx["sbroot"]
    widx = sbroot("widx", [128, NMT], I32)
    sidx_tm = [sbroot(f"sidxtm{i}", [128, 32], I32) for i in range(2)]
    val_tm = [sbroot(f"valtm{i}", [128, 32], F32) for i in range(2)]
    ctx["push_scope"]()

    W = [sb(f"W{i}", [128, 4096], F32) for i in range(7)]
    erep, oh0, oh1, sA, sB, sC, surv = W
    g8t = sb("g8t", [8, 4096], F32)
    g8bt = sb("g8bt", [8, 4096], F32)
    g8ct = sb("g8ct", [8, 4096], F32)
    g8z = g8t
    g8 = lambda: g8t[0:8, :]
    g8b = lambda: g8bt[0:8, :]
    g8c = lambda: g8ct[0:8, :]
    W8 = sb("W8", [128, 4096], F32)
    W9 = sb("W9", [128, 4096], F32)
    myw = [lambda: W8[0:1, :], lambda: W8[32:33, :]]
    mytof = lambda: W8[64:65, :]
    mysl = [lambda: W9[0:1, :], lambda: W9[32:33, :]]
    myv = lambda: W9[64:65, :]
    moh = surv

    colz = sb("colz", [128, 8], F32)
    e16 = sb("e16", [16, 96], F32)
    erow = sb("erow", [1, 64], F32)
    pg = ps("pg", [128, 512], F32)
    pg2 = ps("pg2", [128, 512], F32)
    p16 = ps("p16", [16, 64], F32)
    p1r = ps("p1r", [1, 512], F32)
    zerocol = sb("zerocol", [128, 1], F32)
    P.op("dve", lambda e: e.memset(zerocol[:, :], 0.0), writes=[zerocol])
    ecolb = lambda: cst["C_ECOL"][:, 0:1].to_broadcast([128, 4096])

    def bcast_load(dst, row):
        P.dma("gp", dst[:, :], bass.AP(dr["d_rt"].tensor, row * NTOK_G,
                                       [[4096, 8], [0, 16], [1, 4096]]),
              reads=[("d_rt", row)], writes=[dst])

    bcast_load(erep, 0)
    P.op("dve", lambda e: e.tensor_tensor(oh0[:, :], erep[:, :], ecolb(), ALU.is_equal),
         reads=[erep, cst["C_ECOL"]], writes=[oh0])
    bcast_load(erep, 1)
    P.op("dve", lambda e: e.tensor_tensor(oh1[:, :], erep[:, :], ecolb(), ALU.is_equal),
         reads=[erep, cst["C_ECOL"]], writes=[oh1])
    P.op("dve", lambda e: e.tensor_tensor(sA[:, :], oh0[:, :], oh1[:, :], ALU.add),
         reads=[oh0, oh1], writes=[sA])
    P.op("dve", lambda e: e.tensor_tensor_scan(sB[:, :], sA[:, :],
                                               zerocol[:, 0:1].to_broadcast([128, 4096]),
                                               0.0, ALU.add, ALU.add),
         reads=[sA, zerocol], writes=[sB])
    P.op("dve", lambda e: e.tensor_tensor(sC[:, :], sB[:, :], sA[:, :], ALU.subtract),
         reads=[sB, sA], writes=[sC])
    P.op("dve", lambda e: e.tensor_copy(colz[:, 0:1], sB[:, 4095:4096]), reads=[sB], writes=[colz])
    P.op("pe", lambda e: e.matmul(pg2[:, 0:1], cst["C_OFFM"][:, :], colz[:, 0:1],
                                  start=True, stop=True), reads=[colz, cst["C_OFFM"]], writes=[pg2])
    P.op("dve", lambda e: e.tensor_copy(colz[:, 1:2], pg2[:, 0:1]), reads=[pg2], writes=[colz])
    P.op("dve", lambda e: e.scalar_tensor_tensor(sA[:, :], sC[:, :], 1.0,
                                                 colz[:, 1:2].to_broadcast([128, 4096]),
                                                 ALU.mult, ALU.add),
         reads=[sC, colz], writes=[sA])

    def greduce(dst_ap, dst_key, src128, rkeys):
        for ch in range(8):
            P.ops("pe", [lambda e, ch=ch, src128=src128: e.matmul(
                pg[0:8, 0:512], cst["C_GSEL"][:, :], src128[:, ch * 512:(ch + 1) * 512],
                start=True, stop=True)], reads=rkeys + [cst["C_GSEL"]], writes=[pg])
            P.op("dve", lambda e, ch=ch, dst_ap=dst_ap: e.tensor_copy(
                dst_ap[:, ch * 512:(ch + 1) * 512], pg[0:8, 0:512]),
                reads=[pg], writes=[dst_key])

    for k, oh in ((0, oh0), (1, oh1)):
        P.op("dve", lambda e, oh=oh: e.tensor_tensor(sB[:, :], oh[:, :], sA[:, :], ALU.mult),
             reads=[oh, sA], writes=[sB])
        dstt = g8t if k == 0 else g8bt
        greduce(dstt[0:8, :], dstt, sB, [sB])
        P.op("dve", lambda e, dstt=dstt: e.tensor_scalar(dstt[0:8, :], dstt[0:8, :], float(CGLOB), None, ALU.is_lt),
             reads=[dstt], writes=[dstt])
    wb = sb("wb", [8, 4096], BF16)
    for k, oh in ((0, oh0), (1, oh1)):
        srct = g8t if k == 0 else g8bt
        P.op("dve", lambda e, srct=srct: e.tensor_copy(wb[0:8, :], srct[0:8, :]),
             reads=[srct], writes=[wb])
        for ch in range(8):
            P.ops("pe", [lambda e, ch=ch: e.matmul(
                pg2[:, 0:512], cst["C_REP8"][:, :], wb[0:8, ch * 512:(ch + 1) * 512],
                start=True, stop=True)], reads=[wb, cst["C_REP8"]], writes=[pg2])
            P.op("dve", lambda e, ch=ch: e.tensor_copy(erep[:, ch * 512:(ch + 1) * 512], pg2[:, 0:512]),
                 reads=[pg2], writes=[erep])
        if k == 0:
            P.op("dve", lambda e, oh=oh: e.tensor_tensor(surv[:, :], oh[:, :], erep[:, :], ALU.mult),
                 reads=[oh0, erep], writes=[surv])
        else:
            P.op("dve", lambda e, oh=oh: e.tensor_tensor(sB[:, :], oh[:, :], erep[:, :], ALU.mult),
                 reads=[oh1, erep], writes=[sB])
            P.op("dve", lambda e: e.tensor_tensor(surv[:, :], surv[:, :], sB[:, :], ALU.add),
                 reads=[surv, sB], writes=[surv])
    P.op("dve", lambda e: e.tensor_tensor_scan(sA[:, :], surv[:, :],
                                               zerocol[:, 0:1].to_broadcast([128, 4096]),
                                               0.0, ALU.add, ALU.add),
         reads=[surv, zerocol], writes=[sA])
    P.op("dve", lambda e: e.tensor_tensor(sC[:, :], sA[:, :], surv[:, :], ALU.subtract),
         reads=[sA, surv], writes=[sC])
    P.op("dve", lambda e: e.tensor_copy(colz[:, 2:3], sA[:, 4095:4096]), reads=[sA], writes=[colz])
    P.op("pe", lambda e: e.matmul(p16[0:16, 0:1], cst["C_SEL16"][:, :], colz[:, 2:3],
                                  start=True, stop=True), reads=[colz, cst["C_SEL16"]], writes=[p16])
    P.op("dve", lambda e: e.tensor_copy(e16[:, 0:1], p16[0:16, 0:1]), reads=[p16], writes=[e16])
    ei32 = sb("ei32", [16, 8], I32)
    P.op("dve", lambda e: e.tensor_scalar(e16[:, 4:5], e16[:, 0:1], float(1.0 / MT), None, ALU.mult),
         reads=[e16], writes=[e16])
    P.op("dve", lambda e: e.tensor_copy(ei32[:, 0:1], e16[:, 4:5]), reads=[e16], writes=[ei32])
    P.op("dve", lambda e: e.tensor_copy(e16[:, 4:5], ei32[:, 0:1]), reads=[ei32], writes=[e16])
    P.op("dve", lambda e: e.tensor_scalar(e16[:, 5:6], e16[:, 4:5], float(-MT), None, ALU.mult),
         reads=[e16], writes=[e16])
    P.op("dve", lambda e: e.tensor_tensor(e16[:, 5:6], e16[:, 0:1], e16[:, 5:6], ALU.add),
         reads=[e16], writes=[e16])
    P.op("dve", lambda e: e.tensor_scalar(e16[:, 6:7], e16[:, 5:6], 0.0, None, ALU.is_gt),
         reads=[e16], writes=[e16])
    P.op("dve", lambda e: e.tensor_tensor(e16[:, 1:2], e16[:, 4:5], e16[:, 6:7], ALU.add),
         reads=[e16], writes=[e16])
    P.op("pe", lambda e: e.matmul(p16[0:16, 1:2], cst["C_LEX16"][:, :], e16[:, 1:2],
                                  start=True, stop=True), reads=[e16, cst["C_LEX16"]], writes=[p16])
    P.op("dve", lambda e: e.tensor_copy(e16[:, 2:3], p16[0:16, 1:2]), reads=[p16], writes=[e16])
    P.op("dve", lambda e: e.tensor_tensor(e16[:, 16:48], cst["C_IOTAT"][:, :],
                                          e16[:, 2:3].to_broadcast([16, 32]), ALU.is_ge),
         reads=[e16, cst["C_IOTAT"]], writes=[e16])
    P.op("dve", lambda e: e.tensor_tensor(e16[:, 3:4], e16[:, 2:3], e16[:, 1:2], ALU.add),
         reads=[e16], writes=[e16])
    P.op("dve", lambda e: e.tensor_tensor(e16[:, 48:80], cst["C_IOTAT"][:, :],
                                          e16[:, 3:4].to_broadcast([16, 32]), ALU.is_lt),
         reads=[e16, cst["C_IOTAT"]], writes=[e16])
    P.op("dve", lambda e: e.tensor_tensor(e16[:, 16:48], e16[:, 16:48], e16[:, 48:80], ALU.mult),
         reads=[e16], writes=[e16])
    P.op("pe", lambda e: e.matmul(p1r[0:1, 0:32], cst["C_ECOL16"][:, :], e16[:, 16:48],
                                  start=True, stop=True), reads=[e16, cst["C_ECOL16"]], writes=[p1r])
    P.op("dve", lambda e: e.tensor_copy(erow[:, 0:32], p1r[0:1, 0:32]), reads=[p1r], writes=[erow])
    P.op("pe", lambda e: e.matmul(p1r[0:1, 32:64], cst["C_ONES16F"][:, :], e16[:, 16:48],
                                  start=True, stop=True), reads=[e16, cst["C_ONES16F"]], writes=[p1r])
    P.op("dve", lambda e: e.tensor_copy(erow[:, 32:64], p1r[0:1, 32:64]), reads=[p1r], writes=[erow])
    P.op("dve", lambda e: e.tensor_scalar(erow[:, 32:64], erow[:, 32:64], -1.0, -1e6, ALU.add, ALU.mult),
         reads=[erow], writes=[erow])
    P.op("dve", lambda e: e.tensor_tensor(erow[:, 0:32], erow[:, 0:32], erow[:, 32:64], ALU.add),
         reads=[erow], writes=[erow])
    P.dma("sp", dr["d_tile_e"][0:1, :], erow[0:1, 0:NMT], reads=[erow], writes=[("d_tile_e",)])
    widxf = sb("widxf", [128, NMT], F32)
    P.op("pe", lambda e: e.matmul(pg2[:, 0:NMT], cst["C_ONES1F"][0:1, :], erow[0:1, 0:NMT],
                                  start=True, stop=True), reads=[erow, cst["C_ONES1F"]], writes=[pg2])
    P.op("dve", lambda e: e.scalar_tensor_tensor(widxf[:, :], pg2[:, 0:NMT], 128.0,
                                                 cst["C_IOTA128"][:, 0:1].to_broadcast([128, NMT]),
                                                 ALU.mult, ALU.add),
         reads=[pg2, cst["C_IOTA128"]], writes=[widxf])
    P.op("dve", lambda e: e.tensor_copy(widx[:, :], widxf[:, :]), reads=[widxf], writes=[widx])
    ctx["widx"] = widx

    basef = sb("basef", [1, 4], F32)
    bidx = sb("bidx", [1, 4], I32)
    P.dma("sp", basef[:, :], inp["C_BASE"], writes=[basef])
    P.op("dve", lambda e: e.tensor_copy(bidx[:, :], basef[:, :]), reads=[basef], writes=[bidx])
    b32f = sb("b32f", [128, 1], F32)
    bidx32 = sb("bidx32", [128, 1], I32)
    pb32 = ps("pb32", [128, 1], F32)
    P.op("pe", lambda e: e.matmul(pb32[:, 0:1], cst["C_ONES1F"][0:1, :], basef[0:1, 0:1],
                                  start=True, stop=True), reads=[basef, cst["C_ONES1F"]], writes=[pb32])
    P.op("dve", lambda e: e.scalar_tensor_tensor(b32f[:, :], pb32[:, 0:1], 1.0 / 32,
                                                 cst["C_IOTA128"][:, 0:1], ALU.mult, ALU.add),
         reads=[pb32, cst["C_IOTA128"]], writes=[b32f])
    P.op("dve", lambda e: e.tensor_copy(bidx32[:, :], b32f[:, :]), reads=[b32f], writes=[bidx32])
    ctx["bidx32"] = bidx32
    ctx["b32f"] = b32f

    def sel8_row(dst_ap, dst_key, srcg_ap, rkeys):
        for ch in range(8):
            P.ops("pe", [lambda e, ch=ch, srcg_ap=srcg_ap: e.matmul(
                p1r[0:1, 0:512], cst["C_SEL8"][:, :], srcg_ap[:, ch * 512:(ch + 1) * 512],
                start=True, stop=True)], reads=rkeys + [cst["C_SEL8"]], writes=[p1r])
            P.op("dve", lambda e, ch=ch, dst_ap=dst_ap: e.tensor_copy(
                dst_ap[:, ch * 512:(ch + 1) * 512], p1r[0:1, 0:512]),
                reads=[p1r], writes=[dst_key])

    myw = W8[0:1, :]
    mysl = W9[0:1, :]
    mytof = erep[0:1, :]
    for k, oh in ((0, oh0), (1, oh1)):
        sel8_row(myw, (W8, "r"), g8() if k == 0 else g8b(), [g8t if k == 0 else g8bt])
        P.op("dve", lambda e, oh=oh: e.tensor_tensor(sB[:, :], oh[:, :], sC[:, :], ALU.mult),
             reads=[oh, sC], writes=[sB])
        greduce(g8c(), g8ct, sB, [sB])
        sel8_row(mysl, (W9, "r"), g8c(), [g8ct])
        for ch in range(8):
            P.ops("pe", [lambda e, ch=ch, oh=oh: e.matmul(
                pg2[0:16, 0:512], cst["C_SEL16"][:, :], oh[:, ch * 512:(ch + 1) * 512],
                start=True, stop=True)], reads=[oh, cst["C_SEL16"]], writes=[pg2])
            P.op("dve", lambda e, ch=ch: e.tensor_copy(moh[0:16, ch * 512:(ch + 1) * 512],
                                                       pg2[0:16, 0:512]),
                 reads=[pg2], writes=[moh])
        P.op("dve", lambda e: e.tensor_tensor(moh[0:16, :], moh[0:16, :],
                                              e16[:, 2:3].to_broadcast([16, 4096]), ALU.mult),
             reads=[moh, e16], writes=[moh])
        for ch in range(8):
            P.ops("pe", [lambda e, ch=ch: e.matmul(
                p1r[0:1, 0:512], cst["C_ONES16F"][:, :], moh[0:16, ch * 512:(ch + 1) * 512],
                start=True, stop=True)], reads=[moh, cst["C_ONES16F"]], writes=[p1r])
            P.op("dve", lambda e, ch=ch: e.tensor_copy(mytof[:, ch * 512:(ch + 1) * 512],
                                                       p1r[0:1, 0:512]),
                 reads=[p1r], writes=[(erep, "r")])
        P.op("dve", lambda e: e.scalar_tensor_tensor(mysl, mytof, float(MT), mysl, ALU.mult, ALU.add),
             reads=[(erep, "r"), (W9, "r")], writes=[(W9, "r")])
        P.op("dve", lambda e: e.tensor_scalar(mytof, myw, -1.0, -1e7, ALU.add, ALU.mult),
             reads=[(W8, "r")], writes=[(erep, "r")])
        P.op("dve", lambda e: e.tensor_tensor(mysl, mysl, mytof, ALU.add),
             reads=[(W9, "r"), (erep, "r")], writes=[(W9, "r")])
        P.dma("sp", bass.AP(dr["d_my"].tensor, k * T, [[T, 1], [1, T]]), mysl,
              reads=[(W9, "r")], writes=[("d_my", k)])
        vblk = sb(f"vblk{k}", [128, 32], F32)
        vbi = sb(f"vbi{k}", [128, 1], I32)
        P.op("dve", lambda e, k=k, vbi=vbi: e.tensor_scalar(vbi[:, :], ctx["b32f"][:, :],
                                                            float((2 + k) * (NTOK_G // 32)), None, ALU.add),
             reads=[ctx["b32f"]], writes=[vbi])
        P.dma_f("gp", lambda e, vblk=vblk, vbi=vbi: e.indirect_dma_start(
            vblk[:, :], None,
            bass.AP(dr["d_rt"].tensor, 0, [[32, 4 * (NTOK_G // 32)], [1, 32]]),
            IndirectOffsetOnAxis(vbi[:, 0:1], 0)),
            reads=[("d_rt", 2), ("d_rt", 3), vbi], writes=[vblk])
        P.dma("sp", bass.AP(dr["d_my"].tensor, (2 + k) * T, [[32, 128], [1, 32]]), vblk[:, :],
              reads=[vblk], writes=[("d_my", 2 + k)])
        P.dma("sp", bass.AP(dr["d_my"].tensor, (4 + k) * T, [[T, 1], [1, T]]), myw,
              reads=[(W8, "r")], writes=[("d_my", 4 + k)])

    t2f = sb("t2f", [32, 128], F32)
    pt2 = ps("pt2", [128, 32], F32)
    wtm = sb("wtm", [128, 32], F32)
    def tr_row(srcrow, dst, dkey, extra=None):
        P.dma("sp", t2f[:, :], bass.AP(dr["d_my"].tensor, srcrow * T, [[128, 32], [1, 128]]),
              reads=[("d_my", srcrow)], writes=[t2f])
        P.op("pe", lambda e: e.transpose(pt2[:, :], t2f[:, :], cst["C_IDF"][0:32, 0:32]),
             reads=[t2f, cst["C_IDF"]], writes=[pt2])
        P.op("dve", lambda e, dst=dst: e.tensor_copy(dst[:, :], pt2[:, :]),
             reads=[pt2], writes=[dkey])
    for k in range(2):
        tr_row(k, sidx_tm[k][:, :], sidx_tm[k])
        tr_row(2 + k, val_tm[k][:, :], val_tm[k])
        tr_row(4 + k, wtm[:, :], wtm)
        P.op("dve", lambda e, k=k: e.tensor_tensor(val_tm[k][:, :], val_tm[k][:, :], wtm[:, :], ALU.mult),
             reads=[val_tm[k], wtm], writes=[val_tm[k]])
    ctx["sidx_tm"] = sidx_tm
    ctx["val_tm"] = val_tm
    if "d_dbg16" in ctx:
        P.dma("sp", ctx["d_dbg16"][0:16, 0:7], e16[:, 0:7], reads=[e16], writes=["d_dbg16"])
    ctx["pop_scope"]()


def finish(ctx):
    ctx["P"].finish(ctx["block"])
    ctx["bes"].close()


def make_inputs(inputs_np, cfg):
    consts = host_constants()
    xall = np.ascontiguousarray(inputs_np["L0_attn_u"].reshape(NTOK_G, D))
    shared = {
        "xall": xall,
        "w_in": inputs_np["L0_attn_w_in"], "w_out": inputs_np["L0_attn_w_out"],
        "rw": inputs_np["L0_ffn_router_w"], "rb": inputs_np["L0_ffn_router_b"].reshape(1, E),
        "dtb": inputs_np["L0_attn_dt_bias"].reshape(1, H), "Dv": inputs_np["L0_attn_D"].reshape(1, H),
        "wg": inputs_np["L0_ffn_w_gate"], "wu": inputs_np["L0_ffn_w_up"], "wd": inputs_np["L0_ffn_w_down"],
        "swg": inputs_np["L0_ffn_sw_gate"], "swu": inputs_np["L0_ffn_sw_up"], "swd": inputs_np["L0_ffn_sw_down"],
    }
    shared.update(consts)
    in_maps = []
    for c in range(NCORES):
        m = dict(shared)
        m["xloc"] = xall[c * T:(c + 1) * T]
        m.update(per_core_inputs(c))
        in_maps.append(m)
    return in_maps


def phase_mamba_a(ctx):
    nc, P, sb, ps, cst, inp = ctx["nc"], ctx["P"], ctx["sb"], ctx["ps"], ctx["cst"], ctx["inp"]
    dr = ctx["drams"]
    sc = cst["C_SCL"]

    ctx["push_scope"]()
    Bfm = sb("Bfm", [128, 4096], BF16)
    Btm = sb("Btm", [128, 4096], BF16)
    Cfm = sb("Cfm", [128, 4096], BF16)
    Bfu = sb("Bfu", [128, 4096], BF16)
    Cfu = sb("Cfu", [128, 4096], BF16)
    P.op("dve", lambda e: e.memset(Bfu[:, :], 0.0), writes=[Bfu])
    P.op("dve", lambda e: e.memset(Cfu[:, :], 0.0), writes=[Cfu])
    ctx.update(Bfu=Bfu, Cfu=Cfu)
    lam_tm = sb("lam_tm", [128, 1024], F32)
    E1tm = sb("E1tm", [128, 1024], BF16)
    Ettm = sb("Ettm", [128, 1024], BF16)
    exlam = sb("exlam", [32, 32], F32)
    Ddup = sb("Ddup", [128, 2048], BF16)
    dt_tm = sb("dt_tm", [128, 1024], F32)
    ctx.update(Bfm=Bfm, Btm=Btm, Cfm=Cfm, lam_tm=lam_tm, E1tm=E1tm, Ettm=Ettm,
               exlam=exlam, Ddup=Ddup, dt_tm=dt_tm)

    dtb128 = sb("dtb128", [128, 32], F32)
    ctx["push_scope"]()
    pD = ps("pD", [128, 512], F32)
    Dvb = sb("Dvb", [1, 2048], BF16)
    P.op("dve", lambda e: e.tensor_copy(Dvb[0:1, :],
                                        bass.AP(ctx["vec"]["Dv"], 0, [[H, 1], [1, H], [0, PD]])),
         reads=[ctx["vec"]["Dv"]], writes=[Dvb])
    for q4 in range(4):
        P.op("pe", lambda e, q4=q4: e.matmul(pD[:, 0:512], cst["C_ONES1"][0:1, :],
                                             Dvb[0:1, q4 * 512:(q4 + 1) * 512],
                                             start=True, stop=True), reads=[Dvb], writes=[pD])
        P.op("dve", lambda e, q4=q4: e.tensor_copy(Ddup[:, q4 * 512:(q4 + 1) * 512], pD[:, 0:512]),
             reads=[pD], writes=[Ddup])
    P.op("pe", lambda e: e.matmul(pD[:, 0:32], cst["C_ONES1F"][0:1, :], ctx["vec"]["dtb"][0:1, :],
                                  start=True, stop=True), reads=[ctx["vec"]["dtb"], cst["C_ONES1F"]],
         writes=[pD])
    P.op("dve", lambda e: e.tensor_copy(dtb128[:, :], pD[:, 0:32]), reads=[pD], writes=[dtb128])

    beta_tm = sb("beta_tm", [128, 1024], BF16)
    gam_tm = sb("gam_tm", [128, 1024], BF16)
    angles_tm = sb("angles_tm", [128, 1024], F32)
    ctx.update(beta_tm=beta_tm, gam_tm=gam_tm, angles_tm=angles_tm)

    ctx["push_scope"]()
    xnfm = sb("xnfm", [128, 8 * 4096], BF16)
    ctx["xnfm"] = xnfm
    ctx["push_scope"]()
    last_tm = sb("last_tm", [128, 4096], F32)
    ctx["push_scope"]()
    NT_A = 19
    w_inT = sb("w_inT", [128, 8 * NT_A * 128], BF16)
    wstg = [sb(f"wstg{i}", [128, 1024], BF16) for i in range(2)]
    pw = [ps(f"pw{i}", [128, 128], BF16) for i in range(2)]
    jt_list = list(range(16, 35))
    for i, jt in enumerate(jt_list):
        st = wstg[i % 2]
        P.dma("gp", st[:, :], inp["w_in"][jt * 128:(jt + 1) * 128, :],
              reads=[st], writes=[st], slot=i % 2)
        for kc in range(8):
            pj = pw[kc % 2]
            P.op("pe", lambda e, st=st, kc=kc, pj=pj: e.transpose(
                pj[:, :], st[:, kc * 128:(kc + 1) * 128], cst["C_ID"][:, :]),
                reads=[st, cst["C_ID"], pj], writes=[pj])
            P.op("dve", lambda e, i=i, kc=kc, pj=pj: e.tensor_copy(
                w_inT[:, (kc * NT_A + i) * 128:(kc * NT_A + i + 1) * 128], pj[:, :]),
                reads=[pj], writes=[w_inT])
    WA = lambda jt_i, kc: w_inT[:, (kc * NT_A + jt_i) * 128:(kc * NT_A + jt_i + 1) * 128]

    xt = sb("xt", [128, 1024], F32)
    sqv = sb("sqv", [128, 1024], BF16)
    stat = sb("stat", [128, 8], F32)
    xnb = sb("xnb", [128, 1024], BF16)
    c1 = sb("c1", [128, 1], F32)
    P.op("dve", lambda e: e.memset(c1[:, :], 1.0), writes=[c1])
    pxn = ps("pxn", [128, 128], BF16)
    pp = [ps(f"pp{i}", [128, 512], F32) for i in range(4)]
    xh_tm = sb("xh_tm", [128, 2048], BF16)

    for ch in range(32):
        g2 = ch % 2
        P.dma("sp", xt[:, :], inp["xloc"][ch * 128:(ch + 1) * 128, :],
              reads=[xt], writes=[xt], slot=g2)
        P.op("dve", lambda e: e.tensor_reduce(stat[:, 0:1], xt[:, :], AX.X, ALU.add),
             reads=[xt], writes=[stat])
        P.op("act", lambda e: e.activation(sqv[:, :], xt[:, :], AF.Square, accum_out=stat[:, 1:2]),
             reads=[xt], writes=[sqv, stat])
        P.op("dve", lambda e: e.tensor_scalar(stat[:, 0:1], stat[:, 0:1], sc[:, 0:1], None, ALU.mult),
             reads=[stat, sc], writes=[stat])
        P.op("dve", lambda e: e.tensor_scalar(stat[:, 1:2], stat[:, 1:2], sc[:, 0:1], None, ALU.mult),
             reads=[stat, sc], writes=[stat])
        P.op("dve", lambda e: e.tensor_tensor(stat[:, 2:3], stat[:, 0:1], stat[:, 0:1], ALU.mult),
             reads=[stat], writes=[stat])
        P.op("dve", lambda e: e.tensor_tensor(stat[:, 1:2], stat[:, 1:2], stat[:, 2:3], ALU.subtract),
             reads=[stat], writes=[stat])
        P.op("dve", lambda e: e.tensor_scalar(stat[:, 1:2], stat[:, 1:2], sc[:, 2:3], None, ALU.add),
             reads=[stat, sc], writes=[stat])
        P.op("act", lambda e: e.activation(stat[:, 2:3], stat[:, 1:2], AF.Ln),
             reads=[stat], writes=[stat])
        P.op("act", lambda e: e.activation(stat[:, 3:4], stat[:, 2:3], AF.Exp, scale=sc[:, 1:2]),
             reads=[stat, sc], writes=[stat])
        P.op("dve", lambda e: e.tensor_scalar(xnb[:, :], xt[:, :], stat[:, 0:1], stat[:, 3:4],
                                              ALU.subtract, ALU.mult),
             reads=[xt, stat], writes=[xnb])
        P.op("dve", lambda e: e.tensor_copy(stat[:, 4:5], stat[:, 0:1]), reads=[stat], writes=[stat])
        P.dma("sp", bass.AP(dr["d_st"].tensor, ch * 128 * 2, [[2, 128], [1, 2]]), stat[:, 3:5],
              reads=[stat], writes=[("d_st", ch)], slot=2 + g2)
        P.dma("sp", dr["d_xn16"][ch * 128:(ch + 1) * 128, :], xnb[:, :],
              reads=[xnb], writes=[("d_xn16", ch)], slot=2 + g2)
        for kc in range(8):
            pj = pw[kc % 2]
            P.op("pe", lambda e, kc=kc, pj=pj: e.transpose(
                pj[:, :], xnb[:, kc * 128:(kc + 1) * 128], cst["C_ID"][:, :]),
                reads=[xnb, cst["C_ID"], pj], writes=[pj])
            P.op("dve", lambda e, ch=ch, kc=kc, pj=pj: e.tensor_copy(
                xnfm[:, kc * 4096 + ch * 128: kc * 4096 + (ch + 1) * 128], pj[:, :]),
                reads=[pj], writes=[xnfm])
        for q4 in range(4):
            P.ops("pe", [
                (lambda e, kc=kc, q4=q4, ch=ch: e.matmul(
                    pp[q4][:, :], xnfm[:, kc * 4096 + ch * 128: kc * 4096 + (ch + 1) * 128],
                    bass.AP(w_inT.ap().tensor if False else w_inT, (kc * NT_A + q4 * 4) * 128,
                            [[8 * NT_A * 128, 128], [128, 4], [1, 128]]),
                    start=(kc == 0), stop=(kc == 7)))
                for kc in range(8)], reads=[xnfm, w_inT], writes=[pp[q4]])
            P.op("dve", lambda e, q4=q4: e.tensor_copy(xh_tm[:, q4 * 512:(q4 + 1) * 512], pp[q4][:, :]),
                 reads=[pp[q4]], writes=[xh_tm])
        P.dma("sp", dr["d_xh16"][ch * 128:(ch + 1) * 128, :], xh_tm[:, :],
              reads=[xh_tm], writes=[("d_xh16", ch)], slot=2 + g2)
        P.ops("pe", [
            (lambda e, kc=kc, ch=ch: e.matmul(
                pp[0][:, 0:128], xnfm[:, kc * 4096 + ch * 128: kc * 4096 + (ch + 1) * 128],
                WA(16, kc), start=(kc == 0), stop=(kc == 7)))
            for kc in range(8)], reads=[xnfm, w_inT], writes=[pp[0]])
        P.op("dve", lambda e, ch=ch: e.tensor_copy(Btm[:, ch * 128:(ch + 1) * 128], pp[0][:, 0:128]),
             reads=[pp[0]], writes=[Btm])
        P.ops("pe", [
            (lambda e, kc=kc, ch=ch: e.matmul(
                pp[1][:, 0:128], xnfm[:, kc * 4096 + ch * 128: kc * 4096 + (ch + 1) * 128],
                WA(18, kc), start=(kc == 0), stop=(kc == 7)))
            for kc in range(8)], reads=[xnfm, w_inT], writes=[pp[1]])
        P.op("dve", lambda e, ch=ch: e.tensor_copy(last_tm[:, ch * 128:(ch + 1) * 128], pp[1][:, 0:128]),
             reads=[pp[1]], writes=[last_tm])
        P.ops("pe", [
            (lambda e, kc=kc, ch=ch: e.matmul(
                pp[2][:, 0:128], WA(16, kc), xnfm[:, kc * 4096 + ch * 128: kc * 4096 + (ch + 1) * 128],
                start=(kc == 0), stop=(kc == 7)))
            for kc in range(8)], reads=[xnfm, w_inT], writes=[pp[2]])
        P.op("dve", lambda e, ch=ch: e.tensor_copy(Bfm[:, ch * 128:(ch + 1) * 128], pp[2][:, 0:128]),
             reads=[pp[2]], writes=[Bfm])
        P.ops("pe", [
            (lambda e, kc=kc, ch=ch: e.matmul(
                pp[3][:, 0:128], WA(17, kc), xnfm[:, kc * 4096 + ch * 128: kc * 4096 + (ch + 1) * 128],
                start=(kc == 0), stop=(kc == 7)))
            for kc in range(8)], reads=[xnfm, w_inT], writes=[pp[3]])
        P.op("dve", lambda e, ch=ch: e.tensor_copy(Cfm[:, ch * 128:(ch + 1) * 128], pp[3][:, 0:128]),
             reads=[pp[3]], writes=[Cfm])
        P.ops("pe", [
            (lambda e, kc=kc, ch=ch: e.matmul(
                pp[2][0:96, 128:256], bass.AP(w_inT, (kc * NT_A + 16) * 128 + 32, [[8 * NT_A * 128, 128], [1, 96]]),
                xnfm[:, kc * 4096 + ch * 128: kc * 4096 + (ch + 1) * 128],
                start=(kc == 0), stop=(kc == 7)))
            for kc in range(8)], reads=[xnfm, w_inT], writes=[(pp[2], "u")])
        P.op("dve", lambda e, ch=ch: e.tensor_copy(Bfu[0:64, ch * 128:(ch + 1) * 128],
                                                   pp[2][0:64, 128:256]),
             reads=[(pp[2], "u")], writes=[Bfu])
        P.op("dve", lambda e, ch=ch: e.tensor_copy(Bfu[64:96, ch * 128:(ch + 1) * 128],
                                                   pp[2][64:96, 128:256]),
             reads=[(pp[2], "u")], writes=[Bfu])
        P.ops("pe", [
            (lambda e, kc=kc, ch=ch: e.matmul(
                pp[3][0:96, 128:256], bass.AP(w_inT, (kc * NT_A + 17) * 128 + 32, [[8 * NT_A * 128, 128], [1, 96]]),
                xnfm[:, kc * 4096 + ch * 128: kc * 4096 + (ch + 1) * 128],
                start=(kc == 0), stop=(kc == 7)))
            for kc in range(8)], reads=[xnfm, w_inT], writes=[(pp[3], "u")])
        P.op("dve", lambda e, ch=ch: e.tensor_copy(Cfu[0:64, ch * 128:(ch + 1) * 128],
                                                   pp[3][0:64, 128:256]),
             reads=[(pp[3], "u")], writes=[Cfu])
        P.op("dve", lambda e, ch=ch: e.tensor_copy(Cfu[64:96, ch * 128:(ch + 1) * 128],
                                                   pp[3][64:96, 128:256]),
             reads=[(pp[3], "u")], writes=[Cfu])

    ctx["pop_scope"]()
    ctx["push_scope"]()
    def lview(c0):
        return bass.AP(last_tm, c0, [[4096, 128], [128, 32], [1, 32]])
    sca = sb("sca", [128, 1024], F32)
    scb = sb("scb", [128, 1024], F32)
    scc = sb("scc", [128, 1024], F32)
    a_tm = sb("a_tm", [128, 1024], F32)
    c1b = sb("c1a", [128, 1], F32)
    P.op("dve", lambda e: e.memset(c1b[:, :], 1.0), writes=[c1b])
    dtbv = lambda: bass.AP(dtb128, 0, [[32, 128], [0, 32], [1, 32]])

    def softplus(dst, dkey, src_ap, rkeys, tmp1, tmp2):
        P.op("act", lambda e: e.activation(tmp1[:, :], src_ap, AF.Abs), reads=rkeys, writes=[tmp1])
        P.op("act", lambda e: e.activation(tmp1[:, :], tmp1[:, :], AF.Exp, scale=-1.0),
             reads=[tmp1], writes=[tmp1])
        P.op("act", lambda e: e.activation(tmp1[:, :], tmp1[:, :], AF.Ln, bias=c1b[:, 0:1]),
             reads=[tmp1, c1b], writes=[tmp1])
        P.op("dve", lambda e: e.tensor_scalar(tmp2[:, :], src_ap, 0.0, None, ALU.max),
             reads=rkeys, writes=[tmp2])
        P.op("dve", lambda e: e.tensor_tensor(dst, tmp2[:, :], tmp1[:, :], ALU.add),
             reads=[tmp1, tmp2], writes=[dkey])

    P.op("dve", lambda e: e.tensor_tensor(sca[:, :], lview(0), dtbv(), ALU.add),
         reads=[last_tm, dtb128], writes=[sca])
    softplus(dt_tm[:, :], dt_tm, sca[:, :], [sca], scb, scc)
    softplus(sca[:, :], sca, lview(32), [last_tm], scb, scc)
    P.op("dve", lambda e: e.tensor_scalar(sca[:, :], sca[:, :], 1e-4, -1.0, ALU.max, ALU.mult),
         reads=[sca], writes=[sca])
    P.op("dve", lambda e: e.tensor_tensor(a_tm[:, :], sca[:, :], dt_tm[:, :], ALU.mult),
         reads=[sca, dt_tm], writes=[a_tm])
    P.op("act", lambda e: e.activation(scb[:, :], a_tm[:, :], AF.Exp), reads=[a_tm], writes=[scb])
    P.op("act", lambda e: e.activation(scc[:, :], lview(64), AF.Exp, scale=-1.0),
         reads=[last_tm], writes=[scc])
    P.op("dve", lambda e: e.tensor_scalar(scc[:, :], scc[:, :], 1.0, None, ALU.add),
         reads=[scc], writes=[scc])
    P.op("dve", lambda e: e.reciprocal(scc[:, :], scc[:, :]), reads=[scc], writes=[scc])
    P.op("dve", lambda e: e.tensor_tensor(gam_tm[:, :], scc[:, :], dt_tm[:, :], ALU.mult),
         reads=[scc, dt_tm], writes=[gam_tm])
    P.op("dve", lambda e: e.tensor_scalar(scc[:, :], scc[:, :], -1.0, -1.0, ALU.add, ALU.mult),
         reads=[scc], writes=[scc])
    P.op("dve", lambda e: e.tensor_tensor(scc[:, :], scc[:, :], dt_tm[:, :], ALU.mult),
         reads=[scc, dt_tm], writes=[scc])
    P.op("dve", lambda e: e.tensor_tensor(beta_tm[:, :], scc[:, :], scb[:, :], ALU.mult),
         reads=[scc, scb], writes=[beta_tm])
    P.op("dve", lambda e: e.tensor_copy(angles_tm[:, :], lview(96)), reads=[last_tm], writes=[angles_tm])
    if "d_dbgsc" in ctx:
        P.dma("gp", ctx["d_dbgsc"][0:128, 0:1024], beta_tm[:, :], reads=[beta_tm], writes=["dbgsc"])
        P.dma("gp", ctx["d_dbgsc"][0:128, 1024:2048], gam_tm[:, :], reads=[gam_tm], writes=["dbgsc"])
        P.dma("sp", ctx["d_dbgsc"][0:128, 2048:3072], dt_tm[:, :], reads=[dt_tm], writes=["dbgsc"])
        P.dma("sp", ctx["d_dbgsc"][0:128, 3072:4096], a_tm[:, :], reads=[a_tm], writes=["dbgsc"])

    a_fm = sb("a_fm", [32, 4096], F32)
    lamf = sb("lamf", [32, 4096], F32)
    p32 = ps("p32", [32, 128], F32)
    p32b = ps("p32b", [128, 32], F32)
    for ch in range(32):
        P.op("pe", lambda e, ch=ch: e.transpose(p32[:, :], a_tm[:, ch * 32:(ch + 1) * 32],
                                                cst["C_IDF"][:, :]),
             reads=[a_tm, cst["C_IDF"], p32], writes=[p32])
        P.op("dve", lambda e, ch=ch: e.tensor_copy(a_fm[:, ch * 128:(ch + 1) * 128], p32[:, :]),
             reads=[p32], writes=[a_fm])
    zc32 = sb("zc32", [32, 1], F32)
    P.op("dve", lambda e: e.memset(zc32[:, :], 0.0), writes=[zc32])
    for s in range(2):
        P.op("dve", lambda e, s=s: e.tensor_tensor_scan(
            lamf[:, s * 2048:(s + 1) * 2048], a_fm[:, s * 2048:(s + 1) * 2048],
            zc32[:, 0:1].to_broadcast([32, 2048]), 0.0, ALU.add, ALU.add),
            reads=[a_fm, zc32], writes=[lamf])
    for it in reversed(range(NIT)):
        if it % 16 != 0:
            P.op("dve", lambda e, it=it: e.tensor_scalar(
                lamf[:, it * 128:(it + 1) * 128], lamf[:, it * 128:(it + 1) * 128],
                bass.AP(lamf, it * 128 - 1, [[4096, 32], [1, 1]]), None, ALU.subtract),
                reads=[lamf], writes=[lamf])


    for it in range(NIT):
        P.dma("sp", bass.AP(dr["d_lam"].tensor, it * 4096, [[128, 32], [1, 128]]),
              lamf[:, it * 128:(it + 1) * 128], reads=[lamf], writes=[("d_lam", it)],
              slot=it % 2)
    for it in range(NIT):
        P.op("pe", lambda e, it=it: e.transpose(p32b[:, :], lamf[:, it * 128:(it + 1) * 128],
                                                cst["C_IDF"][0:32, 0:32]),
             reads=[lamf, cst["C_IDF"], p32b], writes=[p32b])
        P.op("dve", lambda e, it=it: e.tensor_copy(lam_tm[:, it * 32:(it + 1) * 32], p32b[:, :]),
             reads=[p32b], writes=[lam_tm])
    P.op("act", lambda e: e.activation(Ettm[:, :], lam_tm[:, :], AF.Exp),
         reads=[lam_tm], writes=[Ettm])
    e1f = a_fm
    P.op("dve", lambda e: e.tensor_tensor(
        e1f[:, :], bass.AP(lamf, 127, [[4096, 32], [128, 32], [0, 128]]), lamf[:, :],
        ALU.subtract), reads=[lamf], writes=[e1f])
    P.op("act", lambda e: e.activation(e1f[:, :], e1f[:, :], AF.Exp), reads=[e1f], writes=[e1f])
    for it in range(NIT):
        P.op("pe", lambda e, it=it: e.transpose(p32b[:, :], e1f[:, it * 128:(it + 1) * 128],
                                                cst["C_IDF"][0:32, 0:32]),
             reads=[e1f, cst["C_IDF"], p32b], writes=[p32b])
        P.op("dve", lambda e, it=it: e.tensor_copy(E1tm[:, it * 32:(it + 1) * 32], p32b[:, :]),
             reads=[p32b], writes=[E1tm])
    P.op("act", lambda e: e.activation(exlam[:, :],
                                       bass.AP(lamf, 127, [[4096, 32], [128, 32]]), AF.Exp),
         reads=[lamf], writes=[exlam])
    P.op("dve", lambda e: e.tensor_scalar(sca[:, :], dt_tm[:, :], 0.5, None, ALU.mult),
         reads=[dt_tm], writes=[sca])
    for ch in range(32):
        P.op("pe", lambda e, ch=ch: e.transpose(p32[:, :], sca[:, ch * 32:(ch + 1) * 32],
                                                cst["C_IDF"][:, :]),
             reads=[sca, cst["C_IDF"], p32], writes=[p32])
        P.op("dve", lambda e, ch=ch: e.tensor_copy(a_fm[:, ch * 128:(ch + 1) * 128], p32[:, :]),
             reads=[p32], writes=[a_fm])
    P.dma("sp", dr["d_dtf"][:, :], a_fm[:, :], reads=[a_fm], writes=[("d_dtf",)])


    ctx["pop_scope"]()
    ctx["pop_scope"]()


def phase_mamba_b(ctx):
    nc, P, sb, ps, cst, inp = ctx["nc"], ctx["P"], ctx["sb"], ctx["ps"], ctx["cst"], ctx["inp"]
    dr = ctx["drams"]
    xnfm = ctx["xnfm"]
    ctx["push_scope"]()
    ctx["push_scope"]()

    tant = sb("tant", [128, 1024], F32)
    tanf = sb("tanf", [32, 4096], F32)
    p32 = ps("p32b_", [32, 128], F32)
    P.op("act", lambda e: e.activation(tant[:, :], ctx["angles_tm"][:, :], AF.Tanh),
         reads=[ctx["angles_tm"]], writes=[tant])
    for ch in range(32):
        P.op("pe", lambda e, ch=ch: e.transpose(p32[:, :], tant[:, ch * 32:(ch + 1) * 32],
                                                cst["C_IDF"][:, :]),
             reads=[tant, cst["C_IDF"], p32], writes=[p32])
        P.op("dve", lambda e, ch=ch: e.tensor_copy(tanf[:, ch * 128:(ch + 1) * 128], p32[:, :]),
             reads=[p32], writes=[tanf])
    P.dma("sp", dr["d_tanf"][:, :], tanf[:, :], reads=[tanf], writes=[("d_tanf",)])
    ctx["pop_scope"]()
    ctx["push_scope"]()

    dtrep = sb("dtrep", [128, 2048], F32)
    tanrep = sb("tanrep", [128, 2048], F32)
    fsc = sb("fsc", [128, 2048], F32)
    f1 = sb("f1", [128, 2048], F32)
    qi = sb("qi", [128, 2048], I32)
    qf = sb("qf", [128, 2048], F32)
    trg = sb("trg", [128, 2048], BF16)
    for pair in range(16):
        for s in range(2):
            P.dma("gp", dtrep[:, :],
                  bass.AP(dr["d_dtf"].tensor, pair * 2 * T + s * 2048,
                          [[T, 2], [0, 64], [1, 2048]]),
                  reads=[("d_dtf",), dtrep], writes=[dtrep], slot=0)
            P.dma("gp", tanrep[:, :],
                  bass.AP(dr["d_tanf"].tensor, s * 2048,
                          [[0, 4], [T, 32], [1, 2048]]),
                  reads=[("d_tanf",), tanrep], writes=[tanrep], slot=1)
            P.op("dve", lambda e: e.tensor_tensor(f1[:, :], dtrep[:, :], tanrep[:, :], ALU.mult),
                 reads=[dtrep, tanrep], writes=[f1])
            P.op("dve", lambda e: e.tensor_tensor_scan(
                fsc[:, :], f1[:, :], ctx["zc128"][:, 0:1].to_broadcast([128, 2048]),
                0.0, ALU.add, ALU.add), reads=[f1, ctx["zc128"]], writes=[fsc])
            P.op("dve", lambda e: e.tensor_scalar(fsc[:, :], fsc[:, :], cst["C_QCOL"][:, 0:1],
                                                  None, ALU.add),
                 reads=[fsc, cst["C_QCOL"]], writes=[fsc])
            P.op("dve", lambda e: e.tensor_copy(qi[:, :], fsc[:, :]), reads=[fsc], writes=[qi])
            P.op("dve", lambda e: e.tensor_copy(qf[:, :], qi[:, :]), reads=[qi], writes=[qf])
            P.op("dve", lambda e: e.tensor_tensor(f1[:, :], fsc[:, :], qf[:, :], ALU.subtract),
                 reads=[fsc, qf], writes=[f1])
            P.op("dve", lambda e: e.tensor_scalar(qf[:, :], f1[:, :], 0.5, None, ALU.is_ge),
                 reads=[f1], writes=[qf])
            P.op("dve", lambda e: e.tensor_tensor(f1[:, :], f1[:, :], qf[:, :], ALU.subtract),
                 reads=[f1, qf], writes=[f1])
            P.op("dve", lambda e: e.tensor_scalar(qf[:, :], f1[:, :], -0.5, None, ALU.is_lt),
                 reads=[f1], writes=[qf])
            P.op("dve", lambda e: e.tensor_tensor(f1[:, :], f1[:, :], qf[:, :], ALU.add),
                 reads=[f1, qf], writes=[f1])
            P.op("act", lambda e: e.activation(trg[:, :], f1[:, :], AF.Sin, scale=cst["C_PI2"][:, 0:1]),
                 reads=[f1, cst["C_PI2"]], writes=[trg])
            P.dma("sp", dr["d_trigF"][s, pair], trg[:, :],
                  reads=[trg], writes=[("d_trigF", s, pair)], slot=2 + s)

    ctx["pop_scope"]()
    ctx["push_scope"]()
    NT_Z = 16
    w_inT_z = sb("w_inT_z", [128, 8 * NT_Z * 128], BF16)
    wstg = [sb(f"wstgz{i}", [128, 1024], BF16) for i in range(2)]
    pwz = [ps(f"pwz{i}", [128, 128], BF16) for i in range(2)]
    for i in range(NT_Z):
        st = wstg[i % 2]
        P.dma("gp", st[:, :], inp["w_in"][i * 128:(i + 1) * 128, :],
              reads=[st], writes=[st], slot=i % 2)
        for kc in range(8):
            pj = pwz[kc % 2]
            P.op("pe", lambda e, st=st, kc=kc, pj=pj: e.transpose(
                pj[:, :], st[:, kc * 128:(kc + 1) * 128], cst["C_ID"][:, :]),
                reads=[st, cst["C_ID"], pj], writes=[pj])
            P.op("dve", lambda e, i=i, kc=kc, pj=pj: e.tensor_copy(
                w_inT_z[:, (kc * NT_Z + i) * 128:(kc * NT_Z + i + 1) * 128], pj[:, :]),
                reads=[pj], writes=[w_inT_z])
    ppz = [ps(f"ppz{i}", [128, 512], F32) for i in range(2)]
    zsil = sb("zsil", [128, 2048], BF16)
    xh_t = sb("xh_t", [128, 2048], BF16)
    xhp_t = sb("xhp_t", [128, 2048], BF16)
    vmix = sb("vmix", [128, 2048], BF16)
    psh = ps("psh", [128, 512], F32)
    for ch in range(32):
        for q4 in range(4):
            pz = ppz[q4 % 2]
            P.ops("pe", [
                (lambda e, kc=kc, q4=q4, ch=ch, pz=pz: e.matmul(
                    pz[:, :], xnfm[:, kc * 4096 + ch * 128: kc * 4096 + (ch + 1) * 128],
                    bass.AP(w_inT_z, (kc * NT_Z + q4 * 4) * 128,
                            [[8 * NT_Z * 128, 128], [128, 4], [1, 128]]),
                    start=(kc == 0), stop=(kc == 7)))
                for kc in range(8)], reads=[xnfm, w_inT_z, pz], writes=[pz])
            P.op("act", lambda e, q4=q4, pz=pz: e.activation(zsil[:, q4 * 512:(q4 + 1) * 512],
                                                             pz[:, :], AF.Silu),
                 reads=[pz], writes=[zsil])
        P.dma("sp", dr["d_zsil"][ch * 128:(ch + 1) * 128, :], zsil[:, :],
              reads=[zsil], writes=[("d_zsil", ch)], slot=2 + ch % 2)
        P.dma("gp", xh_t[:, :], dr["d_xh16"][ch * 128:(ch + 1) * 128, :],
              reads=[("d_xh16", ch), xh_t], writes=[xh_t], slot=2)
        if ch % 16 == 0:
            for q4 in range(4):
                P.ops("pe", [(lambda e, q4=q4: e.matmul(
                    psh[:, :], cst["C_SHIFT"][:, :], xh_t[:, q4 * 512:(q4 + 1) * 512],
                    start=True, stop=True))], reads=[cst["C_SHIFT"], xh_t, psh], writes=[psh])
                P.op("dve", lambda e, q4=q4: e.tensor_copy(xhp_t[:, q4 * 512:(q4 + 1) * 512],
                                                           psh[:, :]),
                     reads=[psh], writes=[xhp_t])
        else:
            P.dma("gp", xhp_t[:, :],
                  bass.AP(dr["d_xh16"].tensor, (ch * 128 - 1) * DIN, [[DIN, 128], [1, DIN]]),
                  reads=[("d_xh16", ch), ("d_xh16", ch - 1), xhp_t], writes=[xhp_t], slot=3)
        bview = lambda ch=ch: bass.AP(ctx["beta_tm"], ch * 32, [[1024, 128], [1, 32], [0, 64]])
        gview = lambda ch=ch: bass.AP(ctx["gam_tm"], ch * 32, [[1024, 128], [1, 32], [0, 64]])
        P.op("dve", lambda e, ch=ch: e.tensor_tensor(xhp_t[:, :], xhp_t[:, :], bview(ch), ALU.mult),
             reads=[xhp_t, ctx["beta_tm"]], writes=[xhp_t])
        P.op("dve", lambda e, ch=ch: e.tensor_tensor(vmix[:, :], xh_t[:, :], gview(ch), ALU.mult),
             reads=[xh_t, ctx["gam_tm"]], writes=[vmix])
        P.op("dve", lambda e: e.tensor_tensor(vmix[:, :], vmix[:, :], xhp_t[:, :], ALU.add),
             reads=[vmix, xhp_t], writes=[vmix])
        P.dma("sp", dr["d_vmix"][ch * 128:(ch + 1) * 128, :], vmix[:, :],
              reads=[vmix], writes=[("d_vmix", ch)], slot=4)
    ctx["pop_scope"]()
    ctx["pop_scope"]()
    ctx["pop_scope"]()


def phase_scan(ctx):
    nc, P, sb, ps, cst, inp = ctx["nc"], ctx["P"], ctx["sb"], ctx["ps"], ctx["cst"], ctx["inp"]
    dr = ctx["drams"]
    Bfm, Btm, Cfm = ctx["Bfm"], ctx["Btm"], ctx["Cfm"]
    Bfu, Cfu = ctx["Bfu"], ctx["Cfu"]
    lam_tm, E1tm, Ettm, Ddup = ctx["lam_tm"], ctx["E1tm"], ctx["Ettm"], ctx["Ddup"]
    ctx["push_scope"]()

    w_outT = sb("w_outT", [128, 16 * 1024], BF16)
    ctx["push_scope"]()
    wstg = [sb(f"wstgo{i}", [128, 2048], BF16) for i in range(2)]
    pwo = [ps(f"pwo{i}", [128, 128], BF16) for i in range(2)]
    for i in range(8):
        st = wstg[i % 2]
        P.dma("gp", st[:, :], inp["w_out"][i * 128:(i + 1) * 128, :],
              reads=[st], writes=[st], slot=i % 2)
        for fc in range(16):
            pj = pwo[fc % 2]
            P.op("pe", lambda e, st=st, fc=fc, pj=pj: e.transpose(
                pj[:, :], st[:, fc * 128:(fc + 1) * 128], cst["C_ID"][:, :]),
                reads=[st, cst["C_ID"], pj], writes=[pj])
            P.op("dve", lambda e, i=i, fc=fc, pj=pj: e.tensor_copy(
                w_outT[:, fc * 1024 + i * 128: fc * 1024 + (i + 1) * 128], pj[:, :]),
                reads=[pj], writes=[w_outT])

    ctx["pop_scope"]()
    S_cs = sb("S_cs", [128, 1024], BF16)
    Sun_t = sb("Sun_t", [128, 2048], BF16)
    P.op("dve", lambda e: e.memset(S_cs[:, :], 0.0), writes=[S_cs])
    P.op("dve", lambda e: e.memset(Sun_t[:, :], 0.0), writes=[Sun_t])

    trg = [sb(f"s_trg{i}", [128, 2048], BF16) for i in range(2)]
    vm = [sb(f"s_vm{i}", [128, 2048], BF16) for i in range(2)]
    zs = [sb("s_zs0", [128, 2048], BF16)] * 2
    xh = [sb("s_xh0", [128, 2048], BF16)] * 2
    lrep = [sb("s_lrep0", [128, 4096], F32)] * 2
    exrow = [sb(f"s_exrow{i}", [1, 32], F32) for i in range(2)]
    ttm = sb("s_ttm", [128, 2048], BF16)
    kcs = sb("s_kcs", [128, 2048], BF16)
    qcs = sb("s_qcs", [128, 2048], BF16)
    ktc = sb("s_ktc", [128, 2048], BF16)
    ktu = sb("s_ktu", [128, 4096], BF16)
    P.op("dve", lambda e: e.memset(ktu[:, :], 0.0), writes=[ktu])
    dmm = sb("s_dmm", [128, 4096], BF16)
    dm1 = dmm
    gm = sb("s_gm", [128, 4096], BF16)
    bdup = sb("s_bdup", [128, 128], BF16)
    cdup = sb("s_cdup", [128, 128], BF16)
    exh = sb("s_exh", [128, 32], BF16)
    expair = sb("s_expair", [128, 16], BF16)
    xmask = sb("s_xmask", [32, 16], BF16)
    ygt = ttm
    ygf = kcs
    scics = sb("s_scics", [128, 2048], BF16)
    sciun = sb("s_sciun", [128, 4096], BF16)
    sot = sb("s_sot", [128, 1024], BF16)
    exl32 = sb("s_exl32", [32, 1], F32)
    ptrig = ps("s_ptrig", [128, 1024], BF16)
    pG = ps("s_pG", [128, 1024], F32)
    pyA = ps("s_pyA", [128, 512], F32)
    pyB = ps("s_pyB", [128, 512], F32)
    pSc = ps("s_pSc", [128, 256], F32)
    pSu = ps("s_pSu", [128, 512], F32)

    d_exl = nc.dram_tensor("d_exl", [NIT, 32], F32, kind="Internal").ap()
    exl_f = ctx["exlam"]
    P.op("pe", lambda e: e.transpose(pSc[0:32, 0:32], exl_f[:, :], cst["C_IDF"][0:32, 0:32]),
         reads=[exl_f, cst["C_IDF"], pSc], writes=[pSc])
    exlT = sb("s_exlT", [32, 32], F32)
    P.op("dve", lambda e: e.tensor_copy(exlT[:, :], pSc[0:32, 0:32]), reads=[pSc], writes=[exlT])
    P.dma("sp", d_exl[:, :], exlT[:, :], reads=[exlT], writes=["d_exl"])

    for it in range(NIT):
        sq = it // 16
        c = it % 16
        bb = it % 2
        tk = slice(it * 128, (it + 1) * 128)
        if it == 16:
            P.op("dve", lambda e: e.memset(S_cs[:, :], 0.0), reads=[S_cs], writes=[S_cs])
            P.op("dve", lambda e: e.memset(Sun_t[:, :], 0.0), reads=[Sun_t], writes=[Sun_t])
        P.dma("sp", trg[bb][:, :].rearrange("p (pr t) -> p pr t", pr=16),
              bass.AP(dr["d_trigF"].tensor, sq * 16 * 128 * SEQ + c * 128,
                      [[SEQ, 128], [128 * SEQ, 16], [1, 128]]),
              reads=[("d_trigF", sq), trg[bb]], writes=[trg[bb]], slot=0)
        P.dma("sp", vm[bb][:, :], dr["d_vmix"][it * 128:(it + 1) * 128, :],
              reads=[("d_vmix", it), vm[bb]], writes=[vm[bb]], slot=1)
        P.dma("sp", zs[bb][:, :], dr["d_zsil"][it * 128:(it + 1) * 128, :],
              reads=[("d_zsil", it), zs[bb]], writes=[zs[bb]], slot=2)
        P.dma("sp", xh[bb][:, :], dr["d_xh16"][it * 128:(it + 1) * 128, :],
              reads=[("d_xh16", it), xh[bb]], writes=[xh[bb]], slot=3)
        P.dma("gp", lrep[bb][:, :],
              bass.AP(dr["d_lam"].tensor, it * 4096, [[0, 128], [1, 4096]]),
              reads=[("d_lam", it), lrep[bb]], writes=[lrep[bb]], slot=4)
        P.dma("sp", exrow[bb][:, :], d_exl[it:it + 1, :],
              reads=["d_exl", exrow[bb]], writes=[exrow[bb]], slot=4)
        for hf in range(2):
            for p8 in range(8):
                p = hf * 8 + p8
                P.op("pe", lambda e, p=p, p8=p8, bb=bb: e.transpose(
                    ptrig[:, p8 * 128:(p8 + 1) * 128], trg[bb][:, p * 128:(p + 1) * 128],
                    cst["C_ID"][:, :]),
                    reads=[trg[bb], cst["C_ID"], ptrig], writes=[ptrig])
            for q2 in range(2):
                P.op("act", lambda e, hf=hf, q2=q2: e.activation(
                    ttm[:, hf * 1024 + q2 * 512: hf * 1024 + (q2 + 1) * 512],
                    ptrig[:, q2 * 512:(q2 + 1) * 512], AF.Copy),
                    reads=[ptrig], writes=[ttm])
        P.op("pe", lambda e, tk=tk: e.matmul(pG[:, 0:128], cst["C_PDUP"][:, :], Bfm[0:32, tk],
                                      start=True, stop=True), reads=[Bfm, cst["C_PDUP"], pG], writes=[pG])
        P.op("pe", lambda e, tk=tk: e.matmul(pG[:, 128:256], cst["C_PDUP"][:, :], Cfm[0:32, tk],
                                      start=True, stop=True), reads=[Cfm, cst["C_PDUP"], pG], writes=[pG])
        P.op("dve", lambda e: e.tensor_copy(bdup[:, :], pG[:, 0:128]), reads=[pG], writes=[bdup])
        P.op("dve", lambda e: e.tensor_copy(cdup[:, :], pG[:, 128:256]), reads=[pG], writes=[cdup])
        P.op("dve", lambda e, bb=bb: e.tensor_copy(exl32[:, :],
                                                   exrow[bb][0:1, :].transpose([1, 0]) if False else exrow[bb][0:1, :]),
             reads=[exrow[bb]], writes=[exl32]) if False else None
        P.op("pe", lambda e, bb=bb: e.matmul(pyA[:, 0:32], cst["C_ONES1F"][0:1, :], exrow[bb][0:1, :],
                                             start=True, stop=True),
             reads=[exrow[bb], cst["C_ONES1F"], pyA], writes=[pyA])
        P.op("dve", lambda e: e.tensor_copy(exh[:, :], pyA[:, 0:32]), reads=[pyA], writes=[exh])
        P.op("dve", lambda e: e.tensor_tensor(xmask[:, :], cst["C_HALF"][:, :],
                                              bass.AP(exh, 0, [[32, 32], [0, 16]]) if False else exh[0:32, 0:16],
                                              ALU.mult),
             reads=[cst["C_HALF"], exh], writes=[xmask]) if False else None
        P.op("pe", lambda e, bb=bb: e.transpose(pSc[0:32, 0:1], exrow[bb][0:1, :],
                                                cst["C_IDF"][0:1, 0:1]),
             reads=[exrow[bb], cst["C_IDF"], pSc], writes=[pSc])
        P.op("dve", lambda e: e.tensor_copy(exl32[:, :], pSc[0:32, 0:1]), reads=[pSc], writes=[exl32])
        exl32b = exl32
        P.op("dve", lambda e: e.tensor_tensor(xmask[:, :], cst["C_HALF"][:, :],
                                              exl32b[:, 0:1].to_broadcast([32, 16]), ALU.mult),
             reads=[cst["C_HALF"], exl32], writes=[xmask])
        P.op("pe", lambda e: e.matmul(pyB[:, 0:16], cst["C_PR"][:, :], xmask[:, :],
                                      start=True, stop=True), reads=[cst["C_PR"], xmask, pyB], writes=[pyB])
        P.op("dve", lambda e: e.tensor_copy(expair[:, :], pyB[:, 0:16]), reads=[pyB], writes=[expair])
        P.op("dve", lambda e: e.tensor_tensor(
            scics[:, :], bass.AP(cst["C_ID"], 0, [[128, 128], [0, 16], [1, 128]]),
            bass.AP(expair, 0, [[16, 128], [1, 16], [0, 128]]), ALU.mult),
            reads=[cst["C_ID"], expair], writes=[scics])
        P.op("dve", lambda e: e.tensor_tensor(
            sciun[:, :], bass.AP(cst["C_ID"], 0, [[128, 128], [0, 32], [1, 128]]),
            bass.AP(exh, 0, [[32, 128], [1, 32], [0, 128]]), ALU.mult),
            reads=[cst["C_ID"], exh], writes=[sciun])
        P.op("dve", lambda e, bb=bb: e.tensor_tensor(
            kcs[:, :], trg[bb][:, :], bass.AP(bdup, 0, [[128, 128], [0, 16], [1, 128]]), ALU.mult),
            reads=[trg[bb], bdup], writes=[kcs])
        P.op("dve", lambda e, bb=bb: e.tensor_tensor(
            qcs[:, :], trg[bb][:, :], bass.AP(cdup, 0, [[128, 128], [0, 16], [1, 128]]), ALU.mult),
            reads=[trg[bb], cdup], writes=[qcs])
        P.op("dve", lambda e, it=it: e.tensor_tensor(
            ktc[:, :], ttm[:, :], bass.AP(Btm, it * 128, [[4096, 128], [0, 64], [1, 32]]), ALU.mult),
            reads=[ttm, Btm], writes=[ktc])
        P.op("dve", lambda e, it=it: e.tensor_tensor(
            ktc[:, :], ktc[:, :], bass.AP(E1tm, it * 32, [[1024, 128], [1, 32], [0, 64]]), ALU.mult),
            reads=[ktc, E1tm], writes=[ktc])
        P.op("dve", lambda e, it=it: e.tensor_tensor(
            bass.AP(ktu, 0, [[4096, 128], [128, 32], [1, 96]]),
            bass.AP(Btm, it * 128 + 32, [[4096, 128], [0, 32], [1, 96]]),
            bass.AP(E1tm, it * 32, [[1024, 128], [1, 32], [0, 96]]), ALU.mult),
            reads=[Btm, E1tm], writes=[ktu])
        P.op("dve", lambda e, bb=bb, it=it: e.tensor_tensor(
            dm1[:, :], lrep[bb][:, :], bass.AP(lam_tm, it * 32, [[1024, 128], [1, 32], [0, 128]]),
            ALU.subtract), reads=[lrep[bb], lam_tm], writes=[dm1])
        P.op("dve", lambda e: e.tensor_tensor(
            dmm[:, :], dm1[:, :], bass.AP(cst["C_TRINEG"], 0, [[128, 128], [0, 32], [1, 128]]),
            ALU.add), reads=[dm1, cst["C_TRINEG"]], writes=[dmm])
        P.op("act", lambda e: e.activation(dmm[:, :], dmm[:, :], AF.Exp), reads=[dmm], writes=[dmm])
        if it == 2 and "d_dbgs" in ctx:
            P.dma("gp", ctx["d_dbgs2"][0:128, 0:1024], S_cs[:, :], reads=[S_cs], writes=["dbgs2"])
            P.dma("gp", ctx["d_dbgs2"][0:128, 1024:3072], Sun_t[:, :], reads=[Sun_t], writes=["dbgs2"])
            P.dma("gp", ctx["d_dbgs2"][0:128, 3072:5120], ktc[:, :], reads=[ktc], writes=["dbgs2"])
        if it == 1 and "d_dbgs" in ctx:
            P.dma("gp", ctx["d_dbgs"][0:128, 0:16], expair[:, :], reads=[expair], writes=["dbgs1"])
            P.dma("gp", ctx["d_dbgs"][0:128, 16:2064], scics[:, :], reads=[scics], writes=["dbgs1"])
            P.dma("gp", ctx["d_dbgs"][0:128, 2064:2096], exh[:, :], reads=[exh], writes=["dbgs1"])
            P.dma("sp", ctx["d_dbgs"][0:128, 2096:3120], lrep[bb][0:1, 0:1024].to_broadcast([128, 1024]) if False else lrep[bb][:, 0:1024], reads=[lrep[bb]], writes=["dbgs1"])
        vmb = vm[it % 2]
        for b in range(4):
            hs = range(b * 8, (b + 1) * 8)
            gsl = slice(b * 1024, (b + 1) * 1024)
            fns = []
            for j, h in enumerate(hs):
                pr = h // 2
                h2 = h % 2
                fns.append((lambda e, j=j, h=h, pr=pr, h2=h2: e.matmul(
                    pG[:, j * 128:(j + 1) * 128],
                    kcs[h2 * 64:(h2 + 1) * 64, pr * 128:(pr + 1) * 128],
                    qcs[h2 * 64:(h2 + 1) * 64, pr * 128:(pr + 1) * 128],
                    start=True, stop=False)))
                fns.append((lambda e, j=j, it=it: e.matmul(
                    pG[:, j * 128:(j + 1) * 128], Bfu[:, it * 128:(it + 1) * 128],
                    Cfu[:, it * 128:(it + 1) * 128], start=False, stop=True)))
            P.ops("pe", fns, reads=[kcs, qcs, Bfu, Cfu, pG], writes=[pG])
            P.op("act", lambda e, gsl=gsl: e.activation(gm[:, gsl], pG[:, 0:1024], AF.Copy),
                 reads=[pG], writes=[gm])
            P.op("dve", lambda e, gsl=gsl: e.tensor_tensor(gm[:, gsl], gm[:, gsl], dmm[:, gsl],
                                                           ALU.mult),
                 reads=[gm, dmm], writes=[gm])
            fnsA, fnsB = [], []
            for j, h in enumerate(hs):
                pr = h // 2
                h2 = h % 2
                fnsA.append((lambda e, j=j, h=h, gsl=gsl, vmb=vmb: e.matmul(
                    pyA[:, j * 64:(j + 1) * 64], gm[:, gsl.start + j * 128: gsl.start + (j + 1) * 128],
                    vmb[:, h * 64:(h + 1) * 64], start=True, stop=True)))
                fnsB.append((lambda e, j=j, h=h, pr=pr, h2=h2: e.matmul(
                    pyB[:, j * 64:(j + 1) * 64],
                    qcs[h2 * 64:(h2 + 1) * 64, pr * 128:(pr + 1) * 128],
                    S_cs[h2 * 64:(h2 + 1) * 64, pr * 64:(pr + 1) * 64],
                    start=True, stop=False)))
                fnsB.append((lambda e, j=j, h=h, it=it: e.matmul(
                    pyB[:, j * 64:(j + 1) * 64], Cfu[:, it * 128:(it + 1) * 128],
                    Sun_t[:, h * 64:(h + 1) * 64], start=False, stop=True)))
            P.ops("pe", fnsA, reads=[gm, vmb, pyA], writes=[pyA])
            P.ops("pe", fnsB, reads=[qcs, Cfu, S_cs, Sun_t, pyB], writes=[pyB])
            fnsS = []
            for j, h in enumerate(hs):
                pr = h // 2
                h2 = h % 2
                fnsS.append((lambda e, h=h, pr=pr, h2=h2: e.matmul(
                    pSc[h2 * 64:(h2 + 1) * 64, (pr % 4) * 64:(pr % 4 + 1) * 64],
                    scics[:, pr * 128 + h2 * 64: pr * 128 + (h2 + 1) * 64],
                    S_cs[:, pr * 64:(pr + 1) * 64],
                    start=True, stop=False)))
                fnsS.append((lambda e, h=h, pr=pr, h2=h2, vmb=vmb: e.matmul(
                    pSc[h2 * 64:(h2 + 1) * 64, (pr % 4) * 64:(pr % 4 + 1) * 64],
                    ktc[:, h * 64:(h + 1) * 64], vmb[:, h * 64:(h + 1) * 64],
                    start=False, stop=True)))
            P.ops("pe", fnsS, reads=[scics, S_cs, ktc, vmb, pSc], writes=[pSc])
            fnsU = []
            for j, h in enumerate(hs):
                fnsU.append((lambda e, j=j, h=h: e.matmul(
                    pSu[:, j * 64:(j + 1) * 64], sciun[:, h * 128:(h + 1) * 128],
                    Sun_t[:, h * 64:(h + 1) * 64], start=True, stop=False)))
                fnsU.append((lambda e, j=j, h=h, vmb=vmb: e.matmul(
                    pSu[:, j * 64:(j + 1) * 64], ktu[:, h * 128:(h + 1) * 128],
                    vmb[:, h * 64:(h + 1) * 64], start=False, stop=True)))
            P.ops("pe", fnsU, reads=[sciun, Sun_t, ktu, vmb, pSu], writes=[pSu])
            P.op("act", lambda e, b=b: e.activation(S_cs[:, b * 256:(b + 1) * 256], pSc[:, 0:256],
                                                    AF.Copy),
                 reads=[pSc], writes=[S_cs])
            P.op("act", lambda e, b=b: e.activation(Sun_t[:, b * 512:(b + 1) * 512],
                                                    pSu[:, 0:512], AF.Copy),
                 reads=[pSu], writes=[Sun_t])
            ysl = slice(b * 512, (b + 1) * 512)
            P.op("dve", lambda e, b=b, it=it, ysl=ysl: e.tensor_tensor(
                ygt[:, ysl], pyB[:, 0:512],
                bass.AP(Ettm, it * 32 + b * 8, [[1024, 128], [1, 8], [0, 64]]), ALU.mult),
                reads=[pyB, Ettm], writes=[ygt])
            P.op("dve", lambda e, ysl=ysl: e.tensor_tensor(ygt[:, ysl], ygt[:, ysl], pyA[:, 0:512],
                                                           ALU.add),
                 reads=[ygt, pyA], writes=[ygt])
        P.op("dve", lambda e, bb=bb: e.tensor_tensor(ygf[:, :], xh[bb][:, :], Ddup[:, :], ALU.mult),
             reads=[xh[bb], Ddup], writes=[ygf])
        P.op("dve", lambda e: e.tensor_tensor(ygt[:, :], ygt[:, :], ygf[:, :], ALU.add),
             reads=[ygt, ygf], writes=[ygt])
        P.op("dve", lambda e, bb=bb: e.tensor_tensor(ygt[:, :], ygt[:, :], zs[bb][:, :], ALU.mult),
             reads=[ygt, zs[bb]], writes=[ygt])
        for hf in range(2):
            for p8 in range(8):
                p = hf * 8 + p8
                P.op("pe", lambda e, p=p, p8=p8: e.transpose(
                    ptrig[:, p8 * 128:(p8 + 1) * 128], ygt[:, p * 128:(p + 1) * 128],
                    cst["C_ID"][:, :]),
                    reads=[ygt, cst["C_ID"], ptrig], writes=[ptrig])
            for q2 in range(2):
                P.op("dve", lambda e, hf=hf, q2=q2: e.tensor_copy(
                    ygf[:, hf * 1024 + q2 * 512: hf * 1024 + (q2 + 1) * 512],
                    ptrig[:, q2 * 512:(q2 + 1) * 512]),
                    reads=[ptrig], writes=[ygf])
        fnsO = []
        for half in range(2):
            for fc in range(16):
                fnsO.append((lambda e, fc=fc, half=half: e.matmul(
                    pG[:, half * 512:(half + 1) * 512], ygf[:, fc * 128:(fc + 1) * 128],
                    w_outT[:, fc * 1024 + half * 512: fc * 1024 + (half + 1) * 512],
                    start=(fc == 0), stop=(fc == 15))))
        P.ops("pe", fnsO, reads=[ygf, w_outT, pG], writes=[pG])
        P.op("dve", lambda e: e.tensor_copy(sot[:, :], pG[:, 0:1024]), reads=[pG], writes=[sot])
        P.dma("sp", dr["d_s16"][it * 128:(it + 1) * 128, :], sot[:, :],
              reads=[sot], writes=[("d_s16", it)], slot=5)
    ctx["pop_scope"]()
    ctx["pop_scope"]()
    ctx["pop_scope"]()


def phase_scatter(ctx):
    nc, P, cst = ctx["nc"], ctx["P"], ctx["cst"]
    sbroot = ctx["sbroot"]
    dr = ctx["drams"]
    zti = sbroot("zti", [128, 8], I32)
    tki8 = sbroot("tki8", [128, 256], I32)
    sidx_sc = [sbroot(f"sidxsc{i}", [128, 32], I32) for i in range(2)]
    sidx_gb = [sbroot(f"sidxgb{i}", [128, 32], I32) for i in range(2)]
    ncl = sbroot("ncl", [128, 1], F32)
    P.op("dve", lambda e: e.tensor_scalar(ncl[:, :], cst["C_IOTA128"][:, 0:1], float(NSLOT),
                                          None, ALU.add),
         reads=[cst["C_IOTA128"]], writes=[ncl])
    for k in range(2):
        P.op("dve", lambda e, k=k: e.tensor_scalar(sidx_sc[k][:, :], ctx["sidx_tm"][k][:, :],
                                                   ncl[:, 0:1], None, ALU.min),
             reads=[ctx["sidx_tm"][k], ncl], writes=[sidx_sc[k]])
        P.op("dve", lambda e, k=k: e.tensor_scalar(sidx_gb[k][:, :], ctx["sidx_tm"][k][:, :],
                                                   float(NSLOT - 1), None, ALU.min),
             reads=[ctx["sidx_tm"][k]], writes=[sidx_gb[k]])
    ctx["sidx_sc"] = sidx_sc
    ctx["sidx_gb"] = sidx_gb
    P.op("dve", lambda e: e.memset(zti[:, :], 0.0), writes=[zti])
    P.dma("sp", bass.AP(dr["d_tokid"].tensor, 0, [[8, NSLOT], [1, 8]]),
          bass.AP(zti, 0, [[8, 128], [0, 128], [1, 8]]),
          reads=[zti], writes=[("d_tokid", "z")], slot=4)
    for j in range(32):
        P.op("dve", lambda e, j=j: e.tensor_scalar(
            tki8[:, j * 8:(j + 1) * 8], cst["C_IOTA128"][:, 0:1].to_broadcast([128, 8]),
            float(j * 128), None, ALU.add),
            reads=[cst["C_IOTA128"]], writes=[tki8])
    for j in range(32):
        for k in range(2):
            P.dma_f("gp", lambda e, j=j, k=k: e.indirect_dma_start(
                dr["d_tokid"], IndirectOffsetOnAxis(ctx["sidx_sc"][k][:, j:j + 1], 0),
                tki8[:, j * 8:(j + 1) * 8], None),
                reads=[tki8, ctx["sidx_sc"][k], ("d_tokid", "z")],
                writes=[("d_tokid", k)], slot=2 + k)


def phase_moe(ctx):
    nc, P, sb, ps, cst, inp = ctx["nc"], ctx["P"], ctx["sb"], ctx["ps"], ctx["cst"], ctx["inp"]
    dr = ctx["drams"]
    widx = ctx["widx"]
    TOKD = [("d_tokid", "z"), ("d_tokid", 0), ("d_tokid", 1)]
    XNKS = [("d_xn16", c) for c in range(32)]
    ctx["push_scope"]()
    wfall = sb("m_wfall", [128, NMT], F32)
    itmp = sb("m_itmp", [128, 2], F32)
    w8 = sb("m_w8", [128, NMT], F32)
    w11 = sb("m_w11", [128, NMT], F32)
    widxp_all = sb("m_widxp", [128, NMT * 8], I32)
    widxd_all = sb("m_widxd", [128, NMT * 11], I32)
    P.op("dve", lambda e: e.tensor_copy(wfall[:, :], widx[:, :]), reads=[widx], writes=[wfall])
    P.op("dve", lambda e: e.tensor_scalar(wfall[:, :], wfall[:, :], float(E * 128 - 1), None,
                                          ALU.min),
         reads=[wfall], writes=[wfall])
    P.op("dve", lambda e: e.tensor_scalar(itmp[:, 0:1], cst["C_IOTA128"][:, 0:1], 7.0, None, ALU.mult),
         reads=[cst["C_IOTA128"]], writes=[itmp])
    P.op("dve", lambda e: e.tensor_scalar(itmp[:, 1:2], cst["C_IOTA128"][:, 0:1], 10.0, None, ALU.mult),
         reads=[cst["C_IOTA128"]], writes=[itmp])
    P.op("dve", lambda e: e.scalar_tensor_tensor(w8[:, :], wfall[:, :], 8.0,
                                                 itmp[:, 0:1].to_broadcast([128, NMT]),
                                                 ALU.mult, ALU.subtract),
         reads=[wfall, itmp], writes=[w8])
    P.op("dve", lambda e: e.scalar_tensor_tensor(w11[:, :], wfall[:, :], 11.0,
                                                 itmp[:, 1:2].to_broadcast([128, NMT]),
                                                 ALU.mult, ALU.subtract),
         reads=[wfall, itmp], writes=[w11])
    for kc in range(8):
        P.op("dve", lambda e, kc=kc: e.tensor_scalar(
            bass.AP(widxp_all, kc, [[NMT * 8, 128], [8, NMT]]), w8[:, :],
            float(kc * 128), None, ALU.add),
            reads=[w8], writes=[widxp_all])
    for fc in range(11):
        P.op("dve", lambda e, fc=fc: e.tensor_scalar(
            bass.AP(widxd_all, fc, [[NMT * 11, 128], [11, NMT]]), w11[:, :],
            float(fc * 128), None, ALU.add),
            reads=[w11], writes=[widxd_all])

    wgs = [sb(f"m_wg{i}", [128, 8 * FF], BF16) for i in range(2)]
    wus = [sb(f"m_wu{i}", [128, 8 * FF], BF16) for i in range(2)]
    wds = [sb(f"m_wd{i}", [128, 11 * 1024], BF16) for i in range(2)]
    xin = [sb(f"m_xin{i}", [128, 1024], BF16) for i in range(2)]
    tkld = [sb(f"m_tk{i}", [128, 8], I32) for i in range(2)]
    xgfm = sb("m_xgfm", [128, 8 * 512], BF16)
    hg = sb("m_hg", [128, 512], BF16)
    hfm = sb("m_hfm", [128, 11 * 512], BF16)
    eo = sb("m_eo", [128, 1024], BF16)
    ptile = ps("m_ptile", [128, 1024], BF16)
    pga = [ps(f"m_pga{i}", [128, 512], F32) for i in range(2)]
    pup = [ps(f"m_pup{i}", [128, 512], F32) for i in range(2)]
    pdn = ps("m_pdn", [128, 1024], F32)
    wgR = inp["wg"].rearrange("e d f -> (e d) f")
    wuR = inp["wu"].rearrange("e d f -> (e d) f")
    wdR = inp["wd"].rearrange("e f d -> (e f) d")

    for t in range(NMT):
        b = t % 2
        wg_t, wu_t, wd_t = wgs[b], wus[b], wds[b]
        for kc in range(8):
            P.dma_f("gp", lambda e, t=t, kc=kc, wg_t=wg_t: e.indirect_dma_start(
                wg_t[:, kc * FF:(kc + 1) * FF], None, wgR,
                IndirectOffsetOnAxis(widxp_all[:, t * 8 + kc: t * 8 + kc + 1], 0)),
                reads=[widxp_all], writes=[wg_t], slot=0)
            P.dma_f("gp", lambda e, t=t, kc=kc, wu_t=wu_t: e.indirect_dma_start(
                wu_t[:, kc * FF:(kc + 1) * FF], None, wuR,
                IndirectOffsetOnAxis(widxp_all[:, t * 8 + kc: t * 8 + kc + 1], 0)),
                reads=[widxp_all], writes=[wu_t], slot=1)
        for fc in range(11):
            P.dma_f("gp", lambda e, t=t, fc=fc, wd_t=wd_t: e.indirect_dma_start(
                wd_t[:, fc * 1024:(fc + 1) * 1024], None, wdR,
                IndirectOffsetOnAxis(widxd_all[:, t * 11 + fc: t * 11 + fc + 1], 0)),
                reads=[widxd_all], writes=[wd_t], slot=2)
        for j in range(4):
            xi = xin[j % 2]
            tk = tkld[j % 2]
            P.dma("sp", tk[:, :],
                  bass.AP(dr["d_tokid"].tensor, (t * MT + j * 128) * 8, [[8, 128], [1, 8]]),
                  reads=TOKD, writes=[tk], slot=j % 2)
            P.dma_f("gp", lambda e, xi=xi, tk=tk: e.indirect_dma_start(
                xi[:, :], None, dr["d_xn16"], IndirectOffsetOnAxis(tk[:, 0:1], 0)),
                reads=[tk] + XNKS, writes=[xi], slot=3)
            for kc in range(8):
                P.op("pe", lambda e, kc=kc, xi=xi: e.transpose(
                    ptile[:, kc * 128:(kc + 1) * 128], xi[:, kc * 128:(kc + 1) * 128],
                    cst["C_ID"][:, :]),
                    reads=[xi, cst["C_ID"], ptile], writes=[ptile])
            P.op("dve", lambda e, j=j: e.tensor_copy(
                bass.AP(xgfm, j * 128, [[4096, 128], [512, 8], [1, 128]]),
                bass.AP(ptile, 0, [[1024, 128], [128, 8], [1, 128]])),
                reads=[ptile], writes=[xgfm])
        for fc in range(11):
            pa = pga[fc % 2]
            pu = pup[fc % 2]
            P.ops("pe", [(lambda e, kc=kc, fc=fc, wg_t=wg_t, pa=pa: e.matmul(
                pa[:, :], wg_t[:, kc * FF + fc * 128: kc * FF + (fc + 1) * 128],
                xgfm[:, kc * 512:(kc + 1) * 512], start=(kc == 0), stop=(kc == 7)))
                for kc in range(8)], reads=[wg_t, xgfm, pa], writes=[pa])
            P.op("act", lambda e, pa=pa: e.activation(hg[:, :], pa[:, :], AF.Silu),
                 reads=[pa], writes=[hg])
            P.ops("pe", [(lambda e, kc=kc, fc=fc, wu_t=wu_t, pu=pu: e.matmul(
                pu[:, :], wu_t[:, kc * FF + fc * 128: kc * FF + (fc + 1) * 128],
                xgfm[:, kc * 512:(kc + 1) * 512], start=(kc == 0), stop=(kc == 7)))
                for kc in range(8)], reads=[wu_t, xgfm, pu], writes=[pu])
            P.op("dve", lambda e, fc=fc, pu=pu: e.tensor_tensor(
                hfm[:, fc * 512:(fc + 1) * 512], hg[:, :], pu[:, :], ALU.mult),
                reads=[hg, pu], writes=[hfm])
        for j in range(4):
            fnsD = []
            for half in range(2):
                for fc in range(11):
                    fnsD.append((lambda e, j=j, fc=fc, half=half, wd_t=wd_t: e.matmul(
                        pdn[:, half * 512:(half + 1) * 512],
                        hfm[:, fc * 512 + j * 128: fc * 512 + (j + 1) * 128],
                        wd_t[:, fc * 1024 + half * 512: fc * 1024 + (half + 1) * 512],
                        start=(fc == 0), stop=(fc == 10))))
            P.ops("pe", fnsD, reads=[hfm, wd_t, pdn], writes=[pdn])
            P.op("dve", lambda e: e.tensor_copy(eo[:, :], pdn[:, 0:1024]), reads=[pdn], writes=[eo])
            P.dma("sp", dr["d_eout"][t * MT + j * 128: t * MT + (j + 1) * 128, :], eo[:, :],
                  reads=[eo], writes=["d_eout"], slot=4)
    ctx["pop_scope"]()


def phase_final(ctx):
    nc, P, sb, ps, cst, inp = ctx["nc"], ctx["P"], ctx["sb"], ctx["ps"], ctx["cst"], ctx["inp"]
    dr = ctx["drams"]
    OUT = ctx["OUT"]
    ctx["push_scope"]()
    sg = sb("f_sg", [128, 8 * FF], BF16)
    su = sb("f_su", [128, 8 * FF], BF16)
    sd = sb("f_sd", [128, 11 * 1024], BF16)
    P.dma("gp", sg[:, :], inp["swg"].rearrange("(kc p) f -> p kc f", p=128), writes=[sg], slot=0)
    P.dma("gp", su[:, :], inp["swu"].rearrange("(kc p) f -> p kc f", p=128), writes=[su], slot=1)
    P.dma("gp", sd[:, :], inp["swd"].rearrange("(fc p) d -> p fc d", p=128), writes=[sd], slot=2)

    xnt = [sb(f"f_xn{i}", [128, 2048], BF16) for i in range(2)]
    xlt = [sb(f"f_xl{i}", [128, 2048], F32) for i in range(2)]
    st2 = [sb(f"f_st{i}", [128, 4], F32) for i in range(2)]
    s_t = [sb(f"f_s{i}", [128, 2048], BF16) for i in range(2)]
    g0 = sb("f_g0", [128, 2048], BF16)
    g1 = sb("f_g1", [128, 2048], BF16)
    xn32 = sb("f_xn32", [128, 2048], F32)
    xgfm = sb("f_xgfm", [128, 8 * 256], BF16)
    hg = sb("f_hg", [128, 256], BF16)
    hfm = sb("f_hfm", [128, 11 * 256], BF16)
    ffn = sb("f_ffn", [128, 2048], F32)
    outt = sb("f_out", [128, 2048], F32)
    ptile = ps("f_ptile", [128, 1024], BF16)
    pga = [ps(f"f_pga{i}", [128, 256], F32) for i in range(2)]
    pup = [ps(f"f_pup{i}", [128, 256], F32) for i in range(2)]
    pdn = ps("f_pdn", [128, 1024], F32)

    for pr in range(16):
        b = pr % 2
        xn2, xl2, stp, stt = xnt[b], xlt[b], st2[b], s_t[b]
        P.dma("sp", xn2[:, :],
              bass.AP(dr["d_xn16"].tensor, pr * 256 * D, [[D, 128], [128 * D, 2], [1, D]]),
              reads=[("d_xn16", 2 * pr), ("d_xn16", 2 * pr + 1)], writes=[xn2], slot=0)
        P.dma("sp", xl2[:, :],
              bass.AP(inp["xloc"].tensor, pr * 256 * D, [[D, 128], [128 * D, 2], [1, D]]),
              writes=[xl2], slot=1)
        P.dma("sp", stp[:, :],
              bass.AP(dr["d_st"].tensor, pr * 256 * 2, [[2, 128], [256, 2], [1, 2]]),
              reads=[("d_st", 2 * pr), ("d_st", 2 * pr + 1)], writes=[stp], slot=2)
        P.dma("sp", stt[:, :],
              bass.AP(dr["d_s16"].tensor, pr * 256 * D, [[D, 128], [128 * D, 2], [1, D]]),
              reads=[("d_s16", 2 * pr), ("d_s16", 2 * pr + 1)], writes=[stt], slot=3)
        for jj in range(2):
            j = pr * 2 + jj
            for k in range(2):
                gb = g0 if k == 0 else g1
                P.dma_f("gp", lambda e, j=j, jj=jj, k=k, gb=gb: e.indirect_dma_start(
                    gb[:, jj * 1024:(jj + 1) * 1024], None, dr["d_eout"],
                    IndirectOffsetOnAxis(ctx["sidx_gb"][k][:, j:j + 1], 0)),
                    reads=[ctx["sidx_gb"][k], "d_eout"], writes=[gb], slot=3 + k)
        for jj in range(2):
            for kc in range(8):
                P.op("pe", lambda e, jj=jj, kc=kc, xn2=xn2: e.transpose(
                    ptile[:, kc * 128:(kc + 1) * 128],
                    xn2[:, jj * 1024 + kc * 128: jj * 1024 + (kc + 1) * 128],
                    cst["C_ID"][:, :]),
                    reads=[xn2, cst["C_ID"], ptile], writes=[ptile])
            P.op("dve", lambda e, jj=jj: e.tensor_copy(
                bass.AP(xgfm, jj * 128, [[2048, 128], [256, 8], [1, 128]]),
                bass.AP(ptile, 0, [[1024, 128], [128, 8], [1, 128]])),
                reads=[ptile], writes=[xgfm])
        for fc in range(11):
            pa = pga[fc % 2]
            pu = pup[fc % 2]
            P.ops("pe", [(lambda e, kc=kc, fc=fc, pa=pa: e.matmul(
                pa[:, :], sg[:, kc * FF + fc * 128: kc * FF + (fc + 1) * 128],
                xgfm[:, kc * 256:(kc + 1) * 256], start=(kc == 0), stop=(kc == 7)))
                for kc in range(8)], reads=[sg, xgfm, pa], writes=[pa])
            P.op("act", lambda e, pa=pa: e.activation(hg[:, :], pa[:, :], AF.Silu),
                 reads=[pa], writes=[hg])
            P.ops("pe", [(lambda e, kc=kc, fc=fc, pu=pu: e.matmul(
                pu[:, :], su[:, kc * FF + fc * 128: kc * FF + (fc + 1) * 128],
                xgfm[:, kc * 256:(kc + 1) * 256], start=(kc == 0), stop=(kc == 7)))
                for kc in range(8)], reads=[su, xgfm, pu], writes=[pu])
            P.op("dve", lambda e, fc=fc, pu=pu: e.tensor_tensor(
                hfm[:, fc * 256:(fc + 1) * 256], hg[:, :], pu[:, :], ALU.mult),
                reads=[hg, pu], writes=[hfm])
        for jj in range(2):
            j = pr * 2 + jj
            sl = slice(jj * 1024, (jj + 1) * 1024)
            fnsD = []
            for half in range(2):
                for fc in range(11):
                    fnsD.append((lambda e, jj=jj, fc=fc, half=half: e.matmul(
                        pdn[:, half * 512:(half + 1) * 512],
                        hfm[:, fc * 256 + jj * 128: fc * 256 + (jj + 1) * 128],
                        sd[:, fc * 1024 + half * 512: fc * 1024 + (half + 1) * 512],
                        start=(fc == 0), stop=(fc == 10))))
            P.ops("pe", fnsD, reads=[hfm, sd, pdn], writes=[pdn])
            P.op("dve", lambda e, sl=sl, jj=jj, xl2=xl2, stp=stp: e.tensor_scalar(
                xn32[:, sl], xl2[:, sl], stp[:, jj * 2 + 1: jj * 2 + 2],
                stp[:, jj * 2: jj * 2 + 1], ALU.subtract, ALU.mult),
                reads=[xl2, stp], writes=[xn32])
            P.op("dve", lambda e, j=j, sl=sl: e.tensor_scalar(
                ffn[:, sl], g0[:, sl], ctx["val_tm"][0][:, j:j + 1], None, ALU.mult),
                reads=[g0, ctx["val_tm"][0]], writes=[ffn])
            P.op("dve", lambda e, j=j, sl=sl: e.tensor_scalar(
                outt[:, sl], g1[:, sl], ctx["val_tm"][1][:, j:j + 1], None, ALU.mult),
                reads=[g1, ctx["val_tm"][1]], writes=[outt])
            P.op("dve", lambda e, sl=sl: e.tensor_tensor(ffn[:, sl], ffn[:, sl], outt[:, sl], ALU.add),
                 reads=[ffn, outt], writes=[ffn])
            P.op("dve", lambda e, sl=sl: e.tensor_tensor(ffn[:, sl], ffn[:, sl], pdn[:, 0:1024],
                                                         ALU.add),
                 reads=[ffn, pdn], writes=[ffn])
            P.op("dve", lambda e, sl=sl: e.tensor_tensor(ffn[:, sl], ffn[:, sl], xn32[:, sl],
                                                         ALU.add),
                 reads=[ffn, xn32], writes=[ffn])
        P.op("act", lambda e: e.activation(ffn[:, :], ffn[:, :], AF.Silu), reads=[ffn], writes=[ffn])
        P.op("dve", lambda e, stt=stt: e.tensor_tensor(outt[:, :], ffn[:, :], stt[:, :], ALU.mult),
             reads=[ffn, stt], writes=[outt])
        P.op("dve", lambda e, xl2=xl2: e.tensor_tensor(outt[:, :], outt[:, :], xl2[:, :], ALU.add),
             reads=[outt, xl2], writes=[outt])
        P.dma("sp", bass.AP(OUT.tensor, pr * 256 * D, [[D, 128], [128 * D, 2], [1, D]]),
              outt[:, :], reads=[outt], writes=["OUT"], slot=5)
    ctx["pop_scope"]()


def kernel(**inputs):
    cfg = {}
    ctx = build(cfg)
    phase_routing(ctx)
    phase_routing2(ctx)
    phase_mamba_a(ctx)
    phase_mamba_b(ctx)
    phase_scan(ctx)
    phase_scatter(ctx)
    phase_moe(ctx)
    phase_final(ctx)
    finish(ctx)
    in_maps = make_inputs(inputs, cfg)
    res = run_bass_kernel_spmd(ctx["nc"], in_maps, core_ids=list(range(NCORES)))
    out = np.concatenate([np.asarray(res.results[c]["out"]) for c in range(NCORES)], 0)
    return out.reshape(16, 2048, 1024).astype(np.float32)


# version 03
# speedup vs baseline: 1.0018x; 1.0018x over previous
from collections import defaultdict


COMPUTE = ("pe", "dve", "act")
QUEUES = ("sp", "gp")
N_DMA_SLOTS = 6


class Prog:
    def __init__(self, nc):
        self.nc = nc
        self.es = None
        self.streams = {k: [] for k in COMPUTE + QUEUES}
        self.sems = {}
        self.count = defaultdict(int)
        self.slot_rr = {q: 0 for q in QUEUES}
        self.last_w = {}
        self.readers = defaultdict(dict)
        self.wm = defaultdict(int)
        self.final_waits = []

    def init_sems(self, es, nc):
        names = list(COMPUTE) + [f"{q}{i}" for q in QUEUES for i in range(N_DMA_SLOTS)]
        for n in names:
            self.sems[n] = es.enter_context(nc.semaphore(n))

    @staticmethod
    def _k(b):
        if isinstance(b, tuple):
            return tuple(Prog._k(x) for x in b)
        if isinstance(b, (str, int)):
            return b
        n = getattr(b, "name", None)
        return n if n is not None else id(b)

    def _wait(self, stream, sem_name, cnt):
        if cnt <= 0:
            return
        if self.wm[(stream, sem_name)] >= cnt:
            return
        self.wm[(stream, sem_name)] = cnt
        sem = self.sems[sem_name]
        self.streams[stream].append(lambda e, sem=sem, cnt=cnt: e.wait_ge(sem, cnt))

    def _dep_waits(self, stream, my_sem, reads, writes):
        reads = [self._k(b) for b in reads]
        writes = [self._k(b) for b in writes]
        def need(other):
            if other != my_sem:
                return True
            return my_sem != "pe"
        for b in reads:
            w = self.last_w.get(b)
            if w is not None and need(w[0]):
                self._wait(stream, w[0], w[1])
        for b in writes:
            w = self.last_w.get(b)
            if w is not None and need(w[0]):
                self._wait(stream, w[0], w[1])
            for rs, rc in self.readers.get(b, {}).items():
                if need(rs):
                    self._wait(stream, rs, rc)

    def _record(self, my_sem, cnt, reads, writes):
        reads = [self._k(b) for b in reads]
        writes = [self._k(b) for b in writes]
        for b in reads:
            self.readers[b][my_sem] = cnt
        for b in writes:
            self.last_w[b] = (my_sem, cnt)
            self.readers[b] = {}

    def op(self, stream, fn, reads=(), writes=(), inc=True):
        assert stream in COMPUTE
        my_sem = stream
        self._dep_waits(stream, my_sem, reads, writes)
        self.count[my_sem] += 1
        cnt = self.count[my_sem]
        sem = None
        def emit(e, fn=fn, sname=my_sem, cnt=cnt):
            inst = fn(e)
            inst.then_inc(self.sems[sname], 1)
        self.streams[stream].append(emit)
        self._record(my_sem, cnt, reads, writes)
        return (my_sem, cnt)

    def ops(self, stream, fns, reads=(), writes=()):
        assert stream in COMPUTE
        my_sem = stream
        self._dep_waits(stream, my_sem, reads, writes)
        self.count[my_sem] += 1
        cnt = self.count[my_sem]
        n = len(fns)
        for i, fn in enumerate(fns):
            if i == n - 1:
                def emit(e, fn=fn, sname=my_sem):
                    fn(e).then_inc(self.sems[sname], 1)
            else:
                def emit(e, fn=fn):
                    fn(e)
            self.streams[stream].append(emit)
        self._record(my_sem, cnt, reads, writes)
        return (my_sem, cnt)

    def dma(self, q, out_ap, in_ap, reads=(), writes=(), slot=None, **kw):
        return self.dma_f(q, lambda e, o=out_ap, i=in_ap: e.dma_start(o, i, **kw),
                          reads=reads, writes=writes, slot=slot)

    def dma_f(self, q, fn, reads=(), writes=(), slot=None):
        assert q in QUEUES
        if slot is None:
            slot = self.slot_rr[q]
            self.slot_rr[q] = (slot + 1) % N_DMA_SLOTS
        my_sem = f"{q}{slot}"
        self._dep_waits(q, my_sem, reads, writes)
        self.count[my_sem] += 16
        cnt = self.count[my_sem]
        def emit(e, fn=fn, sname=my_sem):
            fn(e).then_inc(self.sems[sname], 16)
        self.streams[q].append(emit)
        self._record(my_sem, cnt, reads, writes)
        return (my_sem, cnt)

    def barrier(self):
        snap = dict(self.count)
        for stream in COMPUTE + QUEUES:
            for sem_name, cnt in snap.items():
                if sem_name == stream:
                    continue
                self._wait(stream, sem_name, cnt)

    def wait_all_on(self, stream):
        snap = dict(self.count)
        for sem_name, cnt in snap.items():
            if sem_name != stream:
                self._wait(stream, sem_name, cnt)

    def finish(self, block):
        self.wait_all_on("sp")
        streams = self.streams

        @block.tensor
        def _(e):
            for f in streams["pe"]:
                f(e)

        @block.vector
        def _(e):
            for f in streams["dve"]:
                f(e)

        @block.scalar
        def _(e):
            for f in streams["act"]:
                f(e)

        @block.gpsimd
        def _(e):
            for f in streams["gp"]:
                f(e)

        @block.sync
        def _(e):
            for f in streams["sp"]:
                f(e)


"""Kernel builder: full fused megakernel (developed incrementally).

Layouts / phases per the design:
  R   : replicated global routing (all 32768 tokens)
  1aA : LN + in_proj(B,C,xh,last) + scalars + lambda scan   [exp/ln ACT set]
  1aB : theta scan + trig + silu(z)-proj + v-mix            [silu/sin/tanh set]
  P15 : scatter xn rows into expert slot array
  1b  : chunked SSD scan + out_proj -> s                    [exp set]
  P2  : MoE expert macro-tiles (gathered cached weights)    [silu set]
  P3  : gather-back + shared expert + final combine         [silu set]
"""
import os
import sys
import numpy as np

sys.path.insert(0, "/opt/trn_rl_repo")
import concourse.bass as bass
import concourse.mybir as mybir
from concourse.bass_utils import run_bass_kernel_spmd
from concourse.bass_core_rust import IndirectOffsetOnAxis
from contextlib import ExitStack
import ml_dtypes


F32 = mybir.dt.float32
BF16 = mybir.dt.bfloat16
I32 = mybir.dt.int32
U32 = mybir.dt.uint32
ALU = mybir.AluOpType
AF = mybir.ActivationFunctionType
AX = mybir.AxisListType

D = 1024
DIN = 2048
DPRJ = 4480
H = 32
PD = 64
R_ = 32
NST = 128
T = 4096
SEQ = 2048
NSEQ = 2
Q = 128
NCH = 16
NIT = 32
NTOK_G = 32768
E = 16
FF = 1408
CGLOB = 5120
MT = 512
NMT = 32
NSLOT = NMT * MT
NCORES = 8

BIGNEG = -1.0e30


def host_constants():
    bf = ml_dtypes.bfloat16
    c = {}
    c["C_ID"] = np.eye(128, dtype=np.float32).astype(bf)
    tri = np.where(np.arange(128)[None, :] >= np.arange(128)[:, None], 0.0, BIGNEG)
    c["C_TRINEG"] = tri.astype(np.float32)
    c["C_ONES1"] = np.ones((1, 128), dtype=np.float32).astype(bf)
    m = np.arange(128)
    c["C_PDUP"] = (np.arange(32)[:, None] == (m[None, :] % 32)).astype(np.float32).astype(bf)
    c["C_PR"] = ((np.arange(32)[:, None] % 2) == (m[None, :] >= 64)).astype(np.float32).astype(bf)
    c["C_HALF"] = ((np.arange(32)[:, None] // 2) == np.arange(16)[None, :]).astype(np.float32).astype(bf)
    c["C_IOTAE"] = np.tile(np.arange(16, dtype=np.float32), (128, 1))
    c["C_ECOL"] = (np.arange(128, dtype=np.float32) % 16).reshape(128, 1)
    c["C_GSEL"] = ((np.arange(128)[:, None] // 16) == np.arange(8)[None, :]).astype(np.float32)
    kk = np.arange(128)
    c["C_OFFM"] = (((kk[:, None] % 16) == (kk[None, :] % 16)) &
                   ((kk[:, None] // 16) < (kk[None, :] // 16))).astype(np.float32)
    c["C_IOTA128"] = np.arange(128, dtype=np.float32).reshape(128, 1)
    c["C_QCOL"] = np.array([[0.25]*32 + [0.0]*32 + [0.25]*32 + [0.0]*32], dtype=np.float32).T
    c["C_SHIFT"] = (np.arange(128)[:, None] == (np.arange(128)[None, :] - 1)).astype(np.float32).astype(bf)
    c["C_E127"] = (np.arange(128) == 127).astype(np.float32).astype(bf).reshape(128, 1)
    c["C_PI2"] = np.full((128, 1), np.nextafter(np.float32(2.0 * np.pi), np.float32(0)),
                         dtype=np.float32)
    c["C_REP8"] = ((np.arange(128)[None, :] // 16) == np.arange(8)[:, None]).astype(np.float32).astype(bf)
    c["C_LEX16"] = (np.arange(16)[:, None] < np.arange(16)[None, :]).astype(np.float32)
    c["C_IOTAT"] = np.tile(np.arange(32, dtype=np.float32), (16, 1))
    c["C_ECOL16"] = np.arange(16, dtype=np.float32).reshape(16, 1)
    c["C_ONES1F"] = np.ones((1, 128), dtype=np.float32)
    c["C_ONESC"] = np.ones((128, 1), dtype=np.float32).astype(bf)
    c["C_IDF"] = np.eye(128, dtype=np.float32)
    c["C_ONESCF"] = np.ones((128, 1), dtype=np.float32)
    c["C_ONES16F"] = np.ones((16, 1), dtype=np.float32)
    c["C_SCL"] = np.zeros((128, 8), dtype=np.float32)
    c["C_SCL"][:, 0] = 1.0 / D
    c["C_SCL"][:, 1] = -0.5
    c["C_SCL"][:, 2] = 1e-5
    c["C_SCL"][:, 3] = 1e-9
    return c


def per_core_inputs(core):
    c = {}
    sel16 = np.zeros((128, 16), dtype=np.float32)
    for j in range(16):
        sel16[core * 16 + j, j] = 1.0
    c["C_SEL16"] = sel16
    sel8 = np.zeros((8, 1), dtype=np.float32)
    sel8[core, 0] = 1.0
    c["C_SEL8"] = sel8
    c["C_BASE"] = np.array([[core * T, core, 0, 0]], dtype=np.float32)
    return c


def build(cfg):
    dbg = cfg.get("dbg", set())
    nc = bass.Bass("TRN2", target_bir_lowering=False, detect_race_conditions=False)

    inp = {}
    def EI(name, shape, dt=F32):
        inp[name] = nc.dram_tensor(name, list(shape), dt, kind="ExternalInput").ap()
        return inp[name]

    xall = EI("xall", [NTOK_G, D])
    xloc = EI("xloc", [T, D])
    w_in = EI("w_in", [DPRJ, D])
    w_out = EI("w_out", [D, DIN])
    rw = EI("rw", [D, E])
    rb = EI("rb", [1, E])
    dtb = EI("dtb", [1, H])
    Dv = EI("Dv", [1, H])
    wg = EI("wg", [E, D, FF])
    wu = EI("wu", [E, D, FF])
    wd = EI("wd", [E, FF, D])
    swg = EI("swg", [D, FF])
    swu = EI("swu", [D, FF])
    swd = EI("swd", [FF, D])
    for k, v in host_constants().items():
        EI(k, v.shape, BF16 if v.dtype == ml_dtypes.bfloat16 else F32)
    EI("C_SEL16", [128, 16]); EI("C_SEL8", [8, 1]); EI("C_BASE", [1, 4])

    OUT = nc.dram_tensor("out", [T, D], F32, kind="ExternalOutput").ap()
    dbg_out = {}
    def DBG(name, shape, dt=F32):
        if name in dbg:
            dbg_out[name] = nc.dram_tensor("dbg_" + name, list(shape), dt, kind="ExternalOutput").ap()
            return dbg_out[name]
        return None

    dbgd = cfg.get("dbg_drams", set())
    def DRAM(name, shape, dt):
        kind = "ExternalOutput" if name in dbgd else "Internal"
        return nc.dram_tensor(name, list(shape), dt, kind=kind).ap()

    d_xn16 = DRAM("d_xn16", [T, D], BF16)
    d_xh16 = DRAM("d_xh16", [T, DIN], BF16)
    d_vmix = DRAM("d_vmix", [T, DIN], BF16)
    d_zsil = DRAM("d_zsil", [T, DIN], BF16)
    d_trigF = DRAM("d_trigF", [NSEQ, 16, 128, SEQ], BF16)
    d_trigT = DRAM("d_trigT", [T, DIN], BF16)
    d_lam = DRAM("d_lam", [NIT, 4096], F32)
    d_dtf = DRAM("d_dtf", [H, T], F32)
    d_tanf = DRAM("d_tanf", [H, T], F32)
    d_s16 = DRAM("d_s16", [T, D], BF16)
    d_slotin = DRAM("d_slotin", [NSLOT, D], BF16)
    d_eout = DRAM("d_eout", [NSLOT, D], BF16)
    d_wgC = DRAM("d_wgC", [E * 128, 8 * FF], BF16)
    d_wuC = DRAM("d_wuC", [E * 128, 8 * FF], BF16)
    d_wdC = DRAM("d_wdC", [E * 128, 11 * 1024], BF16)
    d_tokid = DRAM("d_tokid", [NSLOT + 128, 8], I32)
    d_st = DRAM("d_st", [T, 2], F32)
    d_rt = DRAM("d_rt", [4, NTOK_G], F32)
    d_my = DRAM("d_my", [6, T], F32)
    d_tile_e = DRAM("d_tile_e", [1, NMT], F32)

    es = ExitStack()
    P = Prog(nc)
    P.init_sems(es, nc)

    SB = {}
    scopes = [es]
    def push_scope():
        scopes.append(ExitStack())
    def pop_scope():
        P.barrier()
        scopes.pop().close()
    def sb(name, shape, dt):
        t = scopes[-1].enter_context(nc.sbuf_tensor(name, list(shape), dt))
        SB[name] = t
        return t

    def sbroot(name, shape, dt):
        t = scopes[0].enter_context(nc.sbuf_tensor(name, list(shape), dt))
        SB[name] = t
        return t

    def ps(name, shape, dt=F32):
        t = scopes[-1].enter_context(nc.psum_tensor(name, list(shape), dt))
        SB[name] = t
        return t

    cst = {}
    for name, shape, dt in [
        ("C_ID", [128, 128], BF16), ("C_TRINEG", [128, 128], F32),
        ("C_ONES1", [1, 128], BF16), ("C_PDUP", [32, 128], BF16),
        ("C_PR", [32, 128], BF16), ("C_HALF", [32, 16], BF16),
        ("C_IOTAE", [128, 16], F32), ("C_ECOL", [128, 1], F32),
        ("C_GSEL", [128, 8], F32), ("C_OFFM", [128, 128], F32),
        ("C_IOTA128", [128, 1], F32), ("C_QCOL", [128, 1], F32),
        ("C_SHIFT", [128, 128], BF16), ("C_E127", [128, 1], BF16),
        ("C_PI2", [128, 1], F32), ("C_SCL", [128, 8], F32),
        ("C_SEL16", [128, 16], F32), ("C_SEL8", [8, 1], F32),
        ("C_REP8", [8, 128], BF16), ("C_LEX16", [16, 16], F32),
        ("C_IOTAT", [16, 32], F32), ("C_ECOL16", [16, 1], F32),
        ("C_ONES1F", [1, 128], F32), ("C_ONES16F", [16, 1], F32),
        ("C_ONESC", [128, 1], BF16), ("C_ONESCF", [128, 1], F32), ("C_IDF", [128, 128], F32),
    ]:
        cst[name] = sb("sb_" + name, shape, dt)
        P.dma("sp", cst[name][:, :], inp[name], writes=[cst[name]])

    vec = {}
    for name, shape in [("rb", [1, E]), ("dtb", [1, H]), ("Dv", [1, H])]:
        vec[name] = sb("v_" + name, shape, F32)
        P.dma("sp", vec[name][:, :], inp[name], writes=[vec[name]])

    zc128 = sbroot("zc128", [128, 1], F32)
    P.op("dve", lambda e: e.memset(zc128[:, :], 0.0), writes=[zc128])
    ret_zc = zc128
    P.barrier()
    bes = ExitStack()
    block = bes.enter_context(nc.Block())
    ret_bes = bes
    ret = {"nc": nc, "P": P, "es": es, "bes": ret_bes, "block": block, "sb": sb, "ps": ps, "SB": SB,
           "push_scope": push_scope, "pop_scope": pop_scope, "sbroot": sbroot, "zc128": ret_zc,
           "cst": cst, "vec": vec, "inp": inp, "OUT": OUT, "DBG": DBG, "dbg_out": dbg_out,
           "drams": dict(d_xn16=d_xn16, d_xh16=d_xh16, d_vmix=d_vmix, d_zsil=d_zsil,
                         d_trigF=d_trigF, d_trigT=d_trigT, d_lam=d_lam, d_dtf=d_dtf,
                         d_tanf=d_tanf, d_s16=d_s16, d_slotin=d_slotin, d_eout=d_eout,
                         d_wgC=d_wgC, d_wuC=d_wuC, d_wdC=d_wdC, d_rt=d_rt, d_my=d_my,
                         d_tile_e=d_tile_e, d_tokid=d_tokid, d_st=d_st)}
    return ret




def phase_routing(ctx):
    nc, P, sb, ps, cst, inp, DBG = (ctx["nc"], ctx["P"], ctx["sb"], ctx["ps"],
                                    ctx["cst"], ctx["inp"], ctx["DBG"])
    dr = ctx["drams"]
    xall = inp["xall"]
    ctx["push_scope"]()

    rxb = [sb(f"rxb{i}", [128, 1024], BF16) for i in range(3)]
    rxf = [sb(f"rxf{i}", [128, 1024], BF16) for i in range(2)]
    rwo = sb("rwo", [128, 8 * 17], BF16)
    sqs = [sb(f"sqs{i}", [128, 1024], BF16) for i in range(2)]
    logit = sb("logit", [128, 4096], F32)
    probs = sb("probs", [128, 4096], F32)
    musum = sb("musum", [128, 256], F32)
    ssq = sb("ssq", [128, 256], F32)
    rtA = sb("rtA", [128, 4096], F32)
    rtB = sb("rtB", [128, 4096], F32)
    small = sb("small", [128, 1536], F32)
    MU, RSTD, M1, M2, E0, E1V, RDEN = [slice(i * 256, (i + 1) * 256) for i in range(7)] if False else [
        slice(0, 256), slice(256, 512), slice(512, 768), slice(768, 1024), slice(1024, 1280), slice(1280, 1536), None]
    cs16 = sb("cs16", [128, 16], F32)
    rb16 = sb("rb16", [128, 16], F32)
    pl = [ps(f"pl{i}", [128, 160], F32) for i in range(2)]
    csrow = sb("csrow", [1, 17], F32)
    rbb = sb("rbb", [1, 16], BF16)
    prep = ps("prep", [128, 32], F32)
    pcs = ps("pcs", [1, 32], F32)

    P.op("dve", lambda e: e.memset(rwo[:, :], 0.0), writes=[rwo])
    for kc in range(8):
        P.dma("gp", rwo[:, kc * 17: kc * 17 + 16], inp["rw"][kc * 128:(kc + 1) * 128, :],
              reads=[rwo], writes=[rwo])
    P.op("dve", lambda e: e.memset(bass.AP(rwo, 16, [[rwo.shape[1], 128], [17, 8]]), 1.0),
         reads=[rwo], writes=[rwo])
    P.ops("pe", [
        (lambda e, kc=kc: e.matmul(pcs[0:1, 0:17], cst["C_ONESC"][:, :],
                                   rwo[:, kc * 17:(kc + 1) * 17],
                                   start=(kc == 0), stop=(kc == 7)))
        for kc in range(8)], reads=[rwo], writes=[pcs])
    P.op("dve", lambda e: e.tensor_copy(csrow[:, :], pcs[0:1, 0:17]), reads=[pcs], writes=[csrow])
    P.op("dve", lambda e: e.tensor_copy(rbb[:, :], ctx["vec"]["rb"][:, :]),
         reads=[ctx["vec"]["rb"]], writes=[rbb])
    csrow16 = sb("csrow16", [1, 16], BF16)
    P.op("dve", lambda e: e.tensor_copy(csrow16[:, :], csrow[0:1, 0:16]), reads=[csrow], writes=[csrow16])
    P.op("pe", lambda e: e.matmul(prep[:, 0:16], cst["C_ONES1"][0:1, :], csrow16[0:1, :],
                                  start=True, stop=True), reads=[csrow16], writes=[prep])
    P.op("pe", lambda e: e.matmul(prep[:, 16:32], cst["C_ONES1"][0:1, :], rbb[0:1, :],
                                  start=True, stop=True), reads=[rbb], writes=[prep])
    P.op("dve", lambda e: e.tensor_copy(cs16[:, :], prep[:, 0:16]), reads=[prep], writes=[cs16])
    P.op("dve", lambda e: e.tensor_copy(rb16[:, :], prep[:, 16:32]), reads=[prep], writes=[rb16])

    for t in range(256):
        b = t % 3
        f = t % 2
        g8 = t % 8
        pt = pl[(t // 8) % 2]
        P.dma("gp", rxb[b][:, :], xall[t * 128:(t + 1) * 128, :],
              writes=[rxb[b]], slot=b)
        P.dma_f("sp", lambda e, b=b, f=f: e.dma_start_transpose(
            rxf[f][:, :].rearrange("p (kc t2) -> p kc t2", kc=8),
            rxb[b][:, :].rearrange("t2 (kc d2) -> t2 kc d2", kc=8)),
            reads=[rxb[b]], writes=[rxf[f]], slot=3 + f)
        P.ops("pe", [
            (lambda e, kc=kc, f=f, pt=pt, g8=g8: e.matmul(
                pt[:, g8 * 17:(g8 + 1) * 17], rxf[f][:, kc * 128:(kc + 1) * 128],
                rwo[:, kc * 17:(kc + 1) * 17], start=(kc == 0), stop=(kc == 7)))
            for kc in range(8)], reads=[rxf[f], rwo], writes=[(pt, g8)])
        sq = sqs[t % 2]
        P.op("act", lambda e, t=t, b=b, sq=sq: e.activation(
            sq[:, :], rxb[b][:, :], AF.Square, accum_out=ssq[:, t:t + 1]),
            reads=[rxb[b]], writes=[sq, ssq])
        if g8 == 7:
            t0 = t - 7
            P.op("dve", lambda e, t0=t0, pt=pt: e.tensor_copy(
                bass.AP(logit, t0 * 16, [[logit.shape[1], 128], [17 - 1, 0]][0:1] + [[16, 8], [1, 16]]),
                bass.AP(pt, 0, [[pt.shape[1], 128], [17, 8], [1, 16]])),
                reads=[(pt, g) for g in range(8)], writes=[logit])
            P.op("dve", lambda e, t0=t0, pt=pt: e.tensor_copy(
                bass.AP(musum, t0, [[musum.shape[1], 128], [1, 8]]),
                bass.AP(pt, 16, [[pt.shape[1], 128], [17, 8]])),
                reads=[(pt, g) for g in range(8)], writes=[musum])

    sc = cst["C_SCL"]
    P.op("dve", lambda e: e.tensor_scalar(small[:, MU], musum[:, :], sc[:, 0:1], None, ALU.mult),
         reads=[musum, sc], writes=[small])
    P.op("dve", lambda e: e.tensor_scalar(small[:, M1], ssq[:, :], sc[:, 0:1], None, ALU.mult),
         reads=[ssq, sc], writes=[small])
    P.op("dve", lambda e: e.tensor_tensor(small[:, M2], small[:, MU], small[:, MU], ALU.mult),
         reads=[small], writes=[small])
    P.op("dve", lambda e: e.tensor_tensor(small[:, M1], small[:, M1], small[:, M2], ALU.subtract),
         reads=[small], writes=[small])
    P.op("dve", lambda e: e.tensor_scalar(small[:, M1], small[:, M1], sc[:, 2:3], None, ALU.add),
         reads=[small, sc], writes=[small])
    P.op("act", lambda e: e.activation(small[:, M2], small[:, M1], AF.Ln),
         reads=[small], writes=[small])
    P.op("act", lambda e: e.activation(small[:, RSTD], small[:, M2], AF.Exp, scale=sc[:, 1:2]),
         reads=[small, sc], writes=[small])
    muv = lambda: bass.AP(small, 0, [[small.shape[1], 128], [1, 256], [0, 16]])
    rstdv = lambda: bass.AP(small, 256, [[small.shape[1], 128], [1, 256], [0, 16]])
    cs16v = lambda: bass.AP(cs16, 0, [[cs16.shape[1], 128], [0, 256], [1, 16]])
    rb16v = lambda: bass.AP(rb16, 0, [[rb16.shape[1], 128], [0, 256], [1, 16]])
    P.op("dve", lambda e: e.tensor_tensor(rtA[:, :], muv(), cs16v(), ALU.mult),
         reads=[small, cs16], writes=[rtA])
    P.op("dve", lambda e: e.tensor_tensor(logit[:, :], logit[:, :], rtA[:, :], ALU.subtract),
         reads=[logit, rtA], writes=[logit])
    P.op("dve", lambda e: e.tensor_tensor(logit[:, :], logit[:, :], rstdv(), ALU.mult),
         reads=[logit, small], writes=[logit])
    P.op("dve", lambda e: e.tensor_tensor(logit[:, :], logit[:, :], rb16v(), ALU.add),
         reads=[logit, rb16], writes=[logit])
    P.op("act", lambda e: e.activation(probs[:, :], logit[:, :], AF.Exp),
         reads=[logit], writes=[probs])
    l3 = lambda t_: bass.AP(t_, 0, [[t_.shape[1], 128], [16, 256], [1, 16]])
    P.op("dve", lambda e: e.tensor_reduce(bass.AP(small, 512, [[small.shape[1], 128], [1, 256]]),
                                          l3(probs), AX.X, ALU.add),
         reads=[probs], writes=[small])
    P.op("dve", lambda e: e.reciprocal(small[:, M2], small[:, M1]), reads=[small], writes=[small])
    P.op("dve", lambda e: e.tensor_tensor(probs[:, :], probs[:, :],
                                          bass.AP(small, 768, [[small.shape[1], 128], [1, 256], [0, 16]]),
                                          ALU.mult),
         reads=[probs, small], writes=[probs])
    iotav = lambda: bass.AP(cst["C_IOTAE"], 0, [[cst["C_IOTAE"].shape[1], 128], [0, 256], [1, 16]])
    P.op("dve", lambda e: e.tensor_reduce(bass.AP(small, 512, [[small.shape[1], 128], [1, 256]]),
                                          l3(probs), AX.X, ALU.max),
         reads=[probs], writes=[small])
    m1v = lambda: bass.AP(small, 512, [[small.shape[1], 128], [1, 256], [0, 16]])
    m2v = lambda: bass.AP(small, 768, [[small.shape[1], 128], [1, 256], [0, 16]])
    P.op("dve", lambda e: e.tensor_tensor(rtA[:, :], probs[:, :], m1v(), ALU.not_equal),
         reads=[probs, small], writes=[rtA])
    P.op("dve", lambda e: e.scalar_tensor_tensor(rtA[:, :], rtA[:, :], 1e6, iotav(), ALU.mult, ALU.add),
         reads=[rtA, cst["C_IOTAE"]], writes=[rtA])
    P.op("dve", lambda e: e.tensor_reduce(bass.AP(small, 1024, [[small.shape[1], 128], [1, 256]]),
                                          l3(rtA), AX.X, ALU.min),
         reads=[rtA], writes=[small])
    e0v = lambda: bass.AP(small, 1024, [[small.shape[1], 128], [1, 256], [0, 16]])
    P.op("dve", lambda e: e.tensor_tensor(rtB[:, :], iotav(), e0v(), ALU.not_equal),
         reads=[cst["C_IOTAE"], small], writes=[rtB])
    P.op("dve", lambda e: e.tensor_tensor(rtB[:, :], rtB[:, :], probs[:, :], ALU.mult),
         reads=[rtB, probs], writes=[rtB])
    P.op("dve", lambda e: e.tensor_reduce(bass.AP(small, 768, [[small.shape[1], 128], [1, 256]]),
                                          l3(rtB), AX.X, ALU.max),
         reads=[rtB], writes=[small])
    P.op("dve", lambda e: e.tensor_tensor(rtB[:, :], rtB[:, :], m2v(), ALU.not_equal),
         reads=[rtB, small], writes=[rtB])
    P.op("dve", lambda e: e.scalar_tensor_tensor(rtB[:, :], rtB[:, :], 1e6, iotav(), ALU.mult, ALU.add),
         reads=[rtB, cst["C_IOTAE"]], writes=[rtB])
    P.op("dve", lambda e: e.tensor_reduce(bass.AP(small, 1280, [[small.shape[1], 128], [1, 256]]),
                                          l3(rtB), AX.X, ALU.min),
         reads=[rtB], writes=[small])
    vden = sb("vden", [128, 256], F32)
    P.op("dve", lambda e: e.tensor_tensor(vden[:, :], small[:, M1], small[:, M2], ALU.add),
         reads=[small], writes=[vden])
    P.op("dve", lambda e: e.tensor_scalar(vden[:, :], vden[:, :], sc[:, 3:4], None, ALU.add),
         reads=[vden, sc], writes=[vden])
    P.op("dve", lambda e: e.reciprocal(vden[:, :], vden[:, :]), reads=[vden], writes=[vden])
    v0tm = sb("v0tm", [128, 256], F32)
    v1tm = sb("v1tm", [128, 256], F32)
    P.op("dve", lambda e: e.tensor_tensor(v0tm[:, :], small[:, M1], vden[:, :], ALU.mult),
         reads=[small, vden], writes=[v0tm])
    P.op("dve", lambda e: e.tensor_tensor(v1tm[:, :], small[:, M2], vden[:, :], ALU.mult),
         reads=[small, vden], writes=[v1tm])
    ptr = ps("ptr", [128, 256], BF16)
    trf = sb("trf", [128, 256], F32)
    items = [(bass.AP(small, 1024, [[small.shape[1], 128], [1, 256]]), 0),
             (bass.AP(small, 1280, [[small.shape[1], 128], [1, 256]]), 1),
             (v0tm[:, :], 2), (v1tm[:, :], 3)]
    idb = sb("idbf", [128, 256], BF16)
    for ap, row in items:
        P.op("dve", lambda e, ap=ap: e.tensor_copy(idb[:, :], ap), reads=[small, v0tm, v1tm], writes=[idb])
        for hhh in range(2):
            P.op("pe", lambda e, hhh=hhh: e.transpose(
                bass.AP(ptr, hhh * 128, [[ptr.shape[1], 128], [1, 128]]),
                idb[:, hhh * 128:(hhh + 1) * 128], cst["C_ID"][:, :]),
                reads=[idb], writes=[(ptr, hhh)])
        P.op("dve", lambda e: e.tensor_copy(trf[:, :], ptr[:, 0:256]),
             reads=[(ptr, 0), (ptr, 1)], writes=[trf])
        P.dma("sp", bass.AP(dr["d_rt"].tensor, row * NTOK_G, [[128, 128], [16384, 2], [1, 128]]),
              trf[:, 0:256], reads=[trf], writes=[("d_rt", row)])
    ctx["pop_scope"]()
    return


def phase_routing2(ctx):
    nc, P, sb, ps, cst, inp = ctx["nc"], ctx["P"], ctx["sb"], ctx["ps"], ctx["cst"], ctx["inp"]
    dr = ctx["drams"]
    sbroot = ctx["sbroot"]
    widx = sbroot("widx", [128, NMT], I32)
    sidx_tm = [sbroot(f"sidxtm{i}", [128, 32], I32) for i in range(2)]
    val_tm = [sbroot(f"valtm{i}", [128, 32], F32) for i in range(2)]
    ctx["push_scope"]()

    W = [sb(f"W{i}", [128, 4096], F32) for i in range(7)]
    erep, oh0, oh1, sA, sB, sC, surv = W
    g8t = sb("g8t", [8, 4096], F32)
    g8bt = sb("g8bt", [8, 4096], F32)
    g8ct = sb("g8ct", [8, 4096], F32)
    g8z = g8t
    g8 = lambda: g8t[0:8, :]
    g8b = lambda: g8bt[0:8, :]
    g8c = lambda: g8ct[0:8, :]
    W8 = sb("W8", [128, 4096], F32)
    W9 = sb("W9", [128, 4096], F32)
    myw = [lambda: W8[0:1, :], lambda: W8[32:33, :]]
    mytof = lambda: W8[64:65, :]
    mysl = [lambda: W9[0:1, :], lambda: W9[32:33, :]]
    myv = lambda: W9[64:65, :]
    moh = surv

    colz = sb("colz", [128, 8], F32)
    e16 = sb("e16", [16, 96], F32)
    erow = sb("erow", [1, 64], F32)
    pg = ps("pg", [128, 512], F32)
    pg2 = ps("pg2", [128, 512], F32)
    p16 = ps("p16", [16, 64], F32)
    p1r = ps("p1r", [1, 512], F32)
    zerocol = sb("zerocol", [128, 1], F32)
    P.op("dve", lambda e: e.memset(zerocol[:, :], 0.0), writes=[zerocol])
    ecolb = lambda: cst["C_ECOL"][:, 0:1].to_broadcast([128, 4096])

    def bcast_load(dst, row):
        P.dma("gp", dst[:, :], bass.AP(dr["d_rt"].tensor, row * NTOK_G,
                                       [[4096, 8], [0, 16], [1, 4096]]),
              reads=[("d_rt", row)], writes=[dst])

    bcast_load(erep, 0)
    P.op("dve", lambda e: e.tensor_tensor(oh0[:, :], erep[:, :], ecolb(), ALU.is_equal),
         reads=[erep, cst["C_ECOL"]], writes=[oh0])
    bcast_load(erep, 1)
    P.op("dve", lambda e: e.tensor_tensor(oh1[:, :], erep[:, :], ecolb(), ALU.is_equal),
         reads=[erep, cst["C_ECOL"]], writes=[oh1])
    P.op("dve", lambda e: e.tensor_tensor(sA[:, :], oh0[:, :], oh1[:, :], ALU.add),
         reads=[oh0, oh1], writes=[sA])
    P.op("dve", lambda e: e.tensor_tensor_scan(sB[:, :], sA[:, :],
                                               zerocol[:, 0:1].to_broadcast([128, 4096]),
                                               0.0, ALU.add, ALU.add),
         reads=[sA, zerocol], writes=[sB])
    P.op("dve", lambda e: e.tensor_tensor(sC[:, :], sB[:, :], sA[:, :], ALU.subtract),
         reads=[sB, sA], writes=[sC])
    P.op("dve", lambda e: e.tensor_copy(colz[:, 0:1], sB[:, 4095:4096]), reads=[sB], writes=[colz])
    P.op("pe", lambda e: e.matmul(pg2[:, 0:1], cst["C_OFFM"][:, :], colz[:, 0:1],
                                  start=True, stop=True), reads=[colz, cst["C_OFFM"]], writes=[pg2])
    P.op("dve", lambda e: e.tensor_copy(colz[:, 1:2], pg2[:, 0:1]), reads=[pg2], writes=[colz])
    P.op("dve", lambda e: e.scalar_tensor_tensor(sA[:, :], sC[:, :], 1.0,
                                                 colz[:, 1:2].to_broadcast([128, 4096]),
                                                 ALU.mult, ALU.add),
         reads=[sC, colz], writes=[sA])

    def greduce(dst_ap, dst_key, src128, rkeys):
        for ch in range(8):
            P.ops("pe", [lambda e, ch=ch, src128=src128: e.matmul(
                pg[0:8, 0:512], cst["C_GSEL"][:, :], src128[:, ch * 512:(ch + 1) * 512],
                start=True, stop=True)], reads=rkeys + [cst["C_GSEL"]], writes=[pg])
            P.op("dve", lambda e, ch=ch, dst_ap=dst_ap: e.tensor_copy(
                dst_ap[:, ch * 512:(ch + 1) * 512], pg[0:8, 0:512]),
                reads=[pg], writes=[dst_key])

    for k, oh in ((0, oh0), (1, oh1)):
        P.op("dve", lambda e, oh=oh: e.tensor_tensor(sB[:, :], oh[:, :], sA[:, :], ALU.mult),
             reads=[oh, sA], writes=[sB])
        dstt = g8t if k == 0 else g8bt
        greduce(dstt[0:8, :], dstt, sB, [sB])
        P.op("dve", lambda e, dstt=dstt: e.tensor_scalar(dstt[0:8, :], dstt[0:8, :], float(CGLOB), None, ALU.is_lt),
             reads=[dstt], writes=[dstt])
    wb = sb("wb", [8, 4096], BF16)
    for k, oh in ((0, oh0), (1, oh1)):
        srct = g8t if k == 0 else g8bt
        P.op("dve", lambda e, srct=srct: e.tensor_copy(wb[0:8, :], srct[0:8, :]),
             reads=[srct], writes=[wb])
        for ch in range(8):
            P.ops("pe", [lambda e, ch=ch: e.matmul(
                pg2[:, 0:512], cst["C_REP8"][:, :], wb[0:8, ch * 512:(ch + 1) * 512],
                start=True, stop=True)], reads=[wb, cst["C_REP8"]], writes=[pg2])
            P.op("dve", lambda e, ch=ch: e.tensor_copy(erep[:, ch * 512:(ch + 1) * 512], pg2[:, 0:512]),
                 reads=[pg2], writes=[erep])
        if k == 0:
            P.op("dve", lambda e, oh=oh: e.tensor_tensor(surv[:, :], oh[:, :], erep[:, :], ALU.mult),
                 reads=[oh0, erep], writes=[surv])
        else:
            P.op("dve", lambda e, oh=oh: e.tensor_tensor(sB[:, :], oh[:, :], erep[:, :], ALU.mult),
                 reads=[oh1, erep], writes=[sB])
            P.op("dve", lambda e: e.tensor_tensor(surv[:, :], surv[:, :], sB[:, :], ALU.add),
                 reads=[surv, sB], writes=[surv])
    P.op("dve", lambda e: e.tensor_tensor_scan(sA[:, :], surv[:, :],
                                               zerocol[:, 0:1].to_broadcast([128, 4096]),
                                               0.0, ALU.add, ALU.add),
         reads=[surv, zerocol], writes=[sA])
    P.op("dve", lambda e: e.tensor_tensor(sC[:, :], sA[:, :], surv[:, :], ALU.subtract),
         reads=[sA, surv], writes=[sC])
    P.op("dve", lambda e: e.tensor_copy(colz[:, 2:3], sA[:, 4095:4096]), reads=[sA], writes=[colz])
    P.op("pe", lambda e: e.matmul(p16[0:16, 0:1], cst["C_SEL16"][:, :], colz[:, 2:3],
                                  start=True, stop=True), reads=[colz, cst["C_SEL16"]], writes=[p16])
    P.op("dve", lambda e: e.tensor_copy(e16[:, 0:1], p16[0:16, 0:1]), reads=[p16], writes=[e16])
    ei32 = sb("ei32", [16, 8], I32)
    P.op("dve", lambda e: e.tensor_scalar(e16[:, 4:5], e16[:, 0:1], float(1.0 / MT), None, ALU.mult),
         reads=[e16], writes=[e16])
    P.op("dve", lambda e: e.tensor_copy(ei32[:, 0:1], e16[:, 4:5]), reads=[e16], writes=[ei32])
    P.op("dve", lambda e: e.tensor_copy(e16[:, 4:5], ei32[:, 0:1]), reads=[ei32], writes=[e16])
    P.op("dve", lambda e: e.tensor_scalar(e16[:, 5:6], e16[:, 4:5], float(-MT), None, ALU.mult),
         reads=[e16], writes=[e16])
    P.op("dve", lambda e: e.tensor_tensor(e16[:, 5:6], e16[:, 0:1], e16[:, 5:6], ALU.add),
         reads=[e16], writes=[e16])
    P.op("dve", lambda e: e.tensor_scalar(e16[:, 6:7], e16[:, 5:6], 0.0, None, ALU.is_gt),
         reads=[e16], writes=[e16])
    P.op("dve", lambda e: e.tensor_tensor(e16[:, 1:2], e16[:, 4:5], e16[:, 6:7], ALU.add),
         reads=[e16], writes=[e16])
    P.op("pe", lambda e: e.matmul(p16[0:16, 1:2], cst["C_LEX16"][:, :], e16[:, 1:2],
                                  start=True, stop=True), reads=[e16, cst["C_LEX16"]], writes=[p16])
    P.op("dve", lambda e: e.tensor_copy(e16[:, 2:3], p16[0:16, 1:2]), reads=[p16], writes=[e16])
    P.op("dve", lambda e: e.tensor_tensor(e16[:, 16:48], cst["C_IOTAT"][:, :],
                                          e16[:, 2:3].to_broadcast([16, 32]), ALU.is_ge),
         reads=[e16, cst["C_IOTAT"]], writes=[e16])
    P.op("dve", lambda e: e.tensor_tensor(e16[:, 3:4], e16[:, 2:3], e16[:, 1:2], ALU.add),
         reads=[e16], writes=[e16])
    P.op("dve", lambda e: e.tensor_tensor(e16[:, 48:80], cst["C_IOTAT"][:, :],
                                          e16[:, 3:4].to_broadcast([16, 32]), ALU.is_lt),
         reads=[e16, cst["C_IOTAT"]], writes=[e16])
    P.op("dve", lambda e: e.tensor_tensor(e16[:, 16:48], e16[:, 16:48], e16[:, 48:80], ALU.mult),
         reads=[e16], writes=[e16])
    P.op("pe", lambda e: e.matmul(p1r[0:1, 0:32], cst["C_ECOL16"][:, :], e16[:, 16:48],
                                  start=True, stop=True), reads=[e16, cst["C_ECOL16"]], writes=[p1r])
    P.op("dve", lambda e: e.tensor_copy(erow[:, 0:32], p1r[0:1, 0:32]), reads=[p1r], writes=[erow])
    P.op("pe", lambda e: e.matmul(p1r[0:1, 32:64], cst["C_ONES16F"][:, :], e16[:, 16:48],
                                  start=True, stop=True), reads=[e16, cst["C_ONES16F"]], writes=[p1r])
    P.op("dve", lambda e: e.tensor_copy(erow[:, 32:64], p1r[0:1, 32:64]), reads=[p1r], writes=[erow])
    P.op("dve", lambda e: e.tensor_scalar(erow[:, 32:64], erow[:, 32:64], -1.0, -1e6, ALU.add, ALU.mult),
         reads=[erow], writes=[erow])
    P.op("dve", lambda e: e.tensor_tensor(erow[:, 0:32], erow[:, 0:32], erow[:, 32:64], ALU.add),
         reads=[erow], writes=[erow])
    P.dma("sp", dr["d_tile_e"][0:1, :], erow[0:1, 0:NMT], reads=[erow], writes=[("d_tile_e",)])
    widxf = sb("widxf", [128, NMT], F32)
    P.op("pe", lambda e: e.matmul(pg2[:, 0:NMT], cst["C_ONES1F"][0:1, :], erow[0:1, 0:NMT],
                                  start=True, stop=True), reads=[erow, cst["C_ONES1F"]], writes=[pg2])
    P.op("dve", lambda e: e.scalar_tensor_tensor(widxf[:, :], pg2[:, 0:NMT], 128.0,
                                                 cst["C_IOTA128"][:, 0:1].to_broadcast([128, NMT]),
                                                 ALU.mult, ALU.add),
         reads=[pg2, cst["C_IOTA128"]], writes=[widxf])
    P.op("dve", lambda e: e.tensor_copy(widx[:, :], widxf[:, :]), reads=[widxf], writes=[widx])
    ctx["widx"] = widx

    basef = sb("basef", [1, 4], F32)
    bidx = sb("bidx", [1, 4], I32)
    P.dma("sp", basef[:, :], inp["C_BASE"], writes=[basef])
    P.op("dve", lambda e: e.tensor_copy(bidx[:, :], basef[:, :]), reads=[basef], writes=[bidx])
    b32f = sb("b32f", [128, 1], F32)
    bidx32 = sb("bidx32", [128, 1], I32)
    pb32 = ps("pb32", [128, 1], F32)
    P.op("pe", lambda e: e.matmul(pb32[:, 0:1], cst["C_ONES1F"][0:1, :], basef[0:1, 0:1],
                                  start=True, stop=True), reads=[basef, cst["C_ONES1F"]], writes=[pb32])
    P.op("dve", lambda e: e.scalar_tensor_tensor(b32f[:, :], pb32[:, 0:1], 1.0 / 32,
                                                 cst["C_IOTA128"][:, 0:1], ALU.mult, ALU.add),
         reads=[pb32, cst["C_IOTA128"]], writes=[b32f])
    P.op("dve", lambda e: e.tensor_copy(bidx32[:, :], b32f[:, :]), reads=[b32f], writes=[bidx32])
    ctx["bidx32"] = bidx32
    ctx["b32f"] = b32f

    def sel8_row(dst_ap, dst_key, srcg_ap, rkeys):
        for ch in range(8):
            P.ops("pe", [lambda e, ch=ch, srcg_ap=srcg_ap: e.matmul(
                p1r[0:1, 0:512], cst["C_SEL8"][:, :], srcg_ap[:, ch * 512:(ch + 1) * 512],
                start=True, stop=True)], reads=rkeys + [cst["C_SEL8"]], writes=[p1r])
            P.op("dve", lambda e, ch=ch, dst_ap=dst_ap: e.tensor_copy(
                dst_ap[:, ch * 512:(ch + 1) * 512], p1r[0:1, 0:512]),
                reads=[p1r], writes=[dst_key])

    myw = W8[0:1, :]
    mysl = W9[0:1, :]
    mytof = erep[0:1, :]
    for k, oh in ((0, oh0), (1, oh1)):
        sel8_row(myw, (W8, "r"), g8() if k == 0 else g8b(), [g8t if k == 0 else g8bt])
        P.op("dve", lambda e, oh=oh: e.tensor_tensor(sB[:, :], oh[:, :], sC[:, :], ALU.mult),
             reads=[oh, sC], writes=[sB])
        greduce(g8c(), g8ct, sB, [sB])
        sel8_row(mysl, (W9, "r"), g8c(), [g8ct])
        for ch in range(8):
            P.ops("pe", [lambda e, ch=ch, oh=oh: e.matmul(
                pg2[0:16, 0:512], cst["C_SEL16"][:, :], oh[:, ch * 512:(ch + 1) * 512],
                start=True, stop=True)], reads=[oh, cst["C_SEL16"]], writes=[pg2])
            P.op("dve", lambda e, ch=ch: e.tensor_copy(moh[0:16, ch * 512:(ch + 1) * 512],
                                                       pg2[0:16, 0:512]),
                 reads=[pg2], writes=[moh])
        P.op("dve", lambda e: e.tensor_tensor(moh[0:16, :], moh[0:16, :],
                                              e16[:, 2:3].to_broadcast([16, 4096]), ALU.mult),
             reads=[moh, e16], writes=[moh])
        for ch in range(8):
            P.ops("pe", [lambda e, ch=ch: e.matmul(
                p1r[0:1, 0:512], cst["C_ONES16F"][:, :], moh[0:16, ch * 512:(ch + 1) * 512],
                start=True, stop=True)], reads=[moh, cst["C_ONES16F"]], writes=[p1r])
            P.op("dve", lambda e, ch=ch: e.tensor_copy(mytof[:, ch * 512:(ch + 1) * 512],
                                                       p1r[0:1, 0:512]),
                 reads=[p1r], writes=[(erep, "r")])
        P.op("dve", lambda e: e.scalar_tensor_tensor(mysl, mytof, float(MT), mysl, ALU.mult, ALU.add),
             reads=[(erep, "r"), (W9, "r")], writes=[(W9, "r")])
        P.op("dve", lambda e: e.tensor_scalar(mytof, myw, -1.0, -1e7, ALU.add, ALU.mult),
             reads=[(W8, "r")], writes=[(erep, "r")])
        P.op("dve", lambda e: e.tensor_tensor(mysl, mysl, mytof, ALU.add),
             reads=[(W9, "r"), (erep, "r")], writes=[(W9, "r")])
        P.dma("sp", bass.AP(dr["d_my"].tensor, k * T, [[T, 1], [1, T]]), mysl,
              reads=[(W9, "r")], writes=[("d_my", k)])
        vblk = sb(f"vblk{k}", [128, 32], F32)
        vbi = sb(f"vbi{k}", [128, 1], I32)
        P.op("dve", lambda e, k=k, vbi=vbi: e.tensor_scalar(vbi[:, :], ctx["b32f"][:, :],
                                                            float((2 + k) * (NTOK_G // 32)), None, ALU.add),
             reads=[ctx["b32f"]], writes=[vbi])
        P.dma_f("gp", lambda e, vblk=vblk, vbi=vbi: e.indirect_dma_start(
            vblk[:, :], None,
            bass.AP(dr["d_rt"].tensor, 0, [[32, 4 * (NTOK_G // 32)], [1, 32]]),
            IndirectOffsetOnAxis(vbi[:, 0:1], 0)),
            reads=[("d_rt", 2), ("d_rt", 3), vbi], writes=[vblk])
        P.dma("sp", bass.AP(dr["d_my"].tensor, (2 + k) * T, [[32, 128], [1, 32]]), vblk[:, :],
              reads=[vblk], writes=[("d_my", 2 + k)])
        P.dma("sp", bass.AP(dr["d_my"].tensor, (4 + k) * T, [[T, 1], [1, T]]), myw,
              reads=[(W8, "r")], writes=[("d_my", 4 + k)])

    t2f = sb("t2f", [32, 128], F32)
    pt2 = ps("pt2", [128, 32], F32)
    wtm = sb("wtm", [128, 32], F32)
    def tr_row(srcrow, dst, dkey, extra=None):
        P.dma("sp", t2f[:, :], bass.AP(dr["d_my"].tensor, srcrow * T, [[128, 32], [1, 128]]),
              reads=[("d_my", srcrow)], writes=[t2f])
        P.op("pe", lambda e: e.transpose(pt2[:, :], t2f[:, :], cst["C_IDF"][0:32, 0:32]),
             reads=[t2f, cst["C_IDF"]], writes=[pt2])
        P.op("dve", lambda e, dst=dst: e.tensor_copy(dst[:, :], pt2[:, :]),
             reads=[pt2], writes=[dkey])
    for k in range(2):
        tr_row(k, sidx_tm[k][:, :], sidx_tm[k])
        tr_row(2 + k, val_tm[k][:, :], val_tm[k])
        tr_row(4 + k, wtm[:, :], wtm)
        P.op("dve", lambda e, k=k: e.tensor_tensor(val_tm[k][:, :], val_tm[k][:, :], wtm[:, :], ALU.mult),
             reads=[val_tm[k], wtm], writes=[val_tm[k]])
    ctx["sidx_tm"] = sidx_tm
    ctx["val_tm"] = val_tm
    if "d_dbg16" in ctx:
        P.dma("sp", ctx["d_dbg16"][0:16, 0:7], e16[:, 0:7], reads=[e16], writes=["d_dbg16"])
    ctx["pop_scope"]()


def finish(ctx):
    ctx["P"].finish(ctx["block"])
    ctx["bes"].close()


def make_inputs(inputs_np, cfg):
    consts = host_constants()
    xall = np.ascontiguousarray(inputs_np["L0_attn_u"].reshape(NTOK_G, D))
    shared = {
        "xall": xall,
        "w_in": inputs_np["L0_attn_w_in"], "w_out": inputs_np["L0_attn_w_out"],
        "rw": inputs_np["L0_ffn_router_w"], "rb": inputs_np["L0_ffn_router_b"].reshape(1, E),
        "dtb": inputs_np["L0_attn_dt_bias"].reshape(1, H), "Dv": inputs_np["L0_attn_D"].reshape(1, H),
        "wg": inputs_np["L0_ffn_w_gate"], "wu": inputs_np["L0_ffn_w_up"], "wd": inputs_np["L0_ffn_w_down"],
        "swg": inputs_np["L0_ffn_sw_gate"], "swu": inputs_np["L0_ffn_sw_up"], "swd": inputs_np["L0_ffn_sw_down"],
    }
    shared.update(consts)
    in_maps = []
    for c in range(NCORES):
        m = dict(shared)
        m["xloc"] = xall[c * T:(c + 1) * T]
        m.update(per_core_inputs(c))
        in_maps.append(m)
    return in_maps


def phase_mamba_a(ctx):
    nc, P, sb, ps, cst, inp = ctx["nc"], ctx["P"], ctx["sb"], ctx["ps"], ctx["cst"], ctx["inp"]
    dr = ctx["drams"]
    sc = cst["C_SCL"]

    ctx["push_scope"]()
    Bfm = sb("Bfm", [128, 4096], BF16)
    Btm = sb("Btm", [128, 4096], BF16)
    Cfm = sb("Cfm", [128, 4096], BF16)
    Bfu = sb("Bfu", [128, 4096], BF16)
    Cfu = sb("Cfu", [128, 4096], BF16)
    P.op("dve", lambda e: e.memset(Bfu[:, :], 0.0), writes=[Bfu])
    P.op("dve", lambda e: e.memset(Cfu[:, :], 0.0), writes=[Cfu])
    ctx.update(Bfu=Bfu, Cfu=Cfu)
    lam_tm = sb("lam_tm", [128, 1024], F32)
    E1tm = sb("E1tm", [128, 1024], BF16)
    Ettm = sb("Ettm", [128, 1024], BF16)
    exlam = sb("exlam", [32, 32], F32)
    Ddup = sb("Ddup", [128, 2048], BF16)
    dt_tm = sb("dt_tm", [128, 1024], F32)
    ctx.update(Bfm=Bfm, Btm=Btm, Cfm=Cfm, lam_tm=lam_tm, E1tm=E1tm, Ettm=Ettm,
               exlam=exlam, Ddup=Ddup, dt_tm=dt_tm)

    dtb128 = sb("dtb128", [128, 32], F32)
    ctx["push_scope"]()
    pD = ps("pD", [128, 512], F32)
    Dvb = sb("Dvb", [1, 2048], BF16)
    P.op("dve", lambda e: e.tensor_copy(Dvb[0:1, :],
                                        bass.AP(ctx["vec"]["Dv"], 0, [[H, 1], [1, H], [0, PD]])),
         reads=[ctx["vec"]["Dv"]], writes=[Dvb])
    for q4 in range(4):
        P.op("pe", lambda e, q4=q4: e.matmul(pD[:, 0:512], cst["C_ONES1"][0:1, :],
                                             Dvb[0:1, q4 * 512:(q4 + 1) * 512],
                                             start=True, stop=True), reads=[Dvb], writes=[pD])
        P.op("dve", lambda e, q4=q4: e.tensor_copy(Ddup[:, q4 * 512:(q4 + 1) * 512], pD[:, 0:512]),
             reads=[pD], writes=[Ddup])
    P.op("pe", lambda e: e.matmul(pD[:, 0:32], cst["C_ONES1F"][0:1, :], ctx["vec"]["dtb"][0:1, :],
                                  start=True, stop=True), reads=[ctx["vec"]["dtb"], cst["C_ONES1F"]],
         writes=[pD])
    P.op("dve", lambda e: e.tensor_copy(dtb128[:, :], pD[:, 0:32]), reads=[pD], writes=[dtb128])

    beta_tm = sb("beta_tm", [128, 1024], BF16)
    gam_tm = sb("gam_tm", [128, 1024], BF16)
    angles_tm = sb("angles_tm", [128, 1024], F32)
    ctx.update(beta_tm=beta_tm, gam_tm=gam_tm, angles_tm=angles_tm)

    ctx["push_scope"]()
    xnfm = sb("xnfm", [128, 8 * 4096], BF16)
    ctx["xnfm"] = xnfm
    ctx["push_scope"]()
    last_tm = sb("last_tm", [128, 4096], F32)
    ctx["push_scope"]()
    NT_A = 19
    w_inT = sb("w_inT", [128, 8 * NT_A * 128], BF16)
    wstg = [sb(f"wstg{i}", [128, 1024], BF16) for i in range(2)]
    pw = [ps(f"pw{i}", [128, 128], BF16) for i in range(2)]
    jt_list = list(range(16, 35))
    for i, jt in enumerate(jt_list):
        st = wstg[i % 2]
        P.dma("gp", st[:, :], inp["w_in"][jt * 128:(jt + 1) * 128, :],
              reads=[st], writes=[st], slot=i % 2)
        for kc in range(8):
            pj = pw[kc % 2]
            P.op("pe", lambda e, st=st, kc=kc, pj=pj: e.transpose(
                pj[:, :], st[:, kc * 128:(kc + 1) * 128], cst["C_ID"][:, :]),
                reads=[st, cst["C_ID"], pj], writes=[pj])
            P.op("dve", lambda e, i=i, kc=kc, pj=pj: e.tensor_copy(
                w_inT[:, (kc * NT_A + i) * 128:(kc * NT_A + i + 1) * 128], pj[:, :]),
                reads=[pj], writes=[w_inT])
    WA = lambda jt_i, kc: w_inT[:, (kc * NT_A + jt_i) * 128:(kc * NT_A + jt_i + 1) * 128]

    xt = sb("xt", [128, 1024], F32)
    sqv = sb("sqv", [128, 1024], BF16)
    stat = sb("stat", [128, 8], F32)
    xnb = sb("xnb", [128, 1024], BF16)
    c1 = sb("c1", [128, 1], F32)
    P.op("dve", lambda e: e.memset(c1[:, :], 1.0), writes=[c1])
    pxn = ps("pxn", [128, 128], BF16)
    pp = [ps(f"pp{i}", [128, 512], F32) for i in range(4)]
    xh_tm = sb("xh_tm", [128, 2048], BF16)

    for ch in range(32):
        g2 = ch % 2
        P.dma("sp", xt[:, :], inp["xloc"][ch * 128:(ch + 1) * 128, :],
              reads=[xt], writes=[xt], slot=g2)
        P.op("dve", lambda e: e.tensor_reduce(stat[:, 0:1], xt[:, :], AX.X, ALU.add),
             reads=[xt], writes=[stat])
        P.op("act", lambda e: e.activation(sqv[:, :], xt[:, :], AF.Square, accum_out=stat[:, 1:2]),
             reads=[xt], writes=[sqv, stat])
        P.op("dve", lambda e: e.tensor_scalar(stat[:, 0:1], stat[:, 0:1], sc[:, 0:1], None, ALU.mult),
             reads=[stat, sc], writes=[stat])
        P.op("dve", lambda e: e.tensor_scalar(stat[:, 1:2], stat[:, 1:2], sc[:, 0:1], None, ALU.mult),
             reads=[stat, sc], writes=[stat])
        P.op("dve", lambda e: e.tensor_tensor(stat[:, 2:3], stat[:, 0:1], stat[:, 0:1], ALU.mult),
             reads=[stat], writes=[stat])
        P.op("dve", lambda e: e.tensor_tensor(stat[:, 1:2], stat[:, 1:2], stat[:, 2:3], ALU.subtract),
             reads=[stat], writes=[stat])
        P.op("dve", lambda e: e.tensor_scalar(stat[:, 1:2], stat[:, 1:2], sc[:, 2:3], None, ALU.add),
             reads=[stat, sc], writes=[stat])
        P.op("act", lambda e: e.activation(stat[:, 2:3], stat[:, 1:2], AF.Ln),
             reads=[stat], writes=[stat])
        P.op("act", lambda e: e.activation(stat[:, 3:4], stat[:, 2:3], AF.Exp, scale=sc[:, 1:2]),
             reads=[stat, sc], writes=[stat])
        P.op("dve", lambda e: e.tensor_scalar(xnb[:, :], xt[:, :], stat[:, 0:1], stat[:, 3:4],
                                              ALU.subtract, ALU.mult),
             reads=[xt, stat], writes=[xnb])
        P.op("dve", lambda e: e.tensor_copy(stat[:, 4:5], stat[:, 0:1]), reads=[stat], writes=[stat])
        P.dma("sp", bass.AP(dr["d_st"].tensor, ch * 128 * 2, [[2, 128], [1, 2]]), stat[:, 3:5],
              reads=[stat], writes=[("d_st", ch)], slot=2 + g2)
        P.dma("sp", dr["d_xn16"][ch * 128:(ch + 1) * 128, :], xnb[:, :],
              reads=[xnb], writes=[("d_xn16", ch)], slot=2 + g2)
        for kc in range(8):
            pj = pw[kc % 2]
            P.op("pe", lambda e, kc=kc, pj=pj: e.transpose(
                pj[:, :], xnb[:, kc * 128:(kc + 1) * 128], cst["C_ID"][:, :]),
                reads=[xnb, cst["C_ID"], pj], writes=[pj])
            P.op("dve", lambda e, ch=ch, kc=kc, pj=pj: e.tensor_copy(
                xnfm[:, kc * 4096 + ch * 128: kc * 4096 + (ch + 1) * 128], pj[:, :]),
                reads=[pj], writes=[xnfm])
        for q4 in range(4):
            P.ops("pe", [
                (lambda e, kc=kc, q4=q4, ch=ch: e.matmul(
                    pp[q4][:, :], xnfm[:, kc * 4096 + ch * 128: kc * 4096 + (ch + 1) * 128],
                    bass.AP(w_inT.ap().tensor if False else w_inT, (kc * NT_A + q4 * 4) * 128,
                            [[8 * NT_A * 128, 128], [128, 4], [1, 128]]),
                    start=(kc == 0), stop=(kc == 7)))
                for kc in range(8)], reads=[xnfm, w_inT], writes=[pp[q4]])
            P.op("dve", lambda e, q4=q4: e.tensor_copy(xh_tm[:, q4 * 512:(q4 + 1) * 512], pp[q4][:, :]),
                 reads=[pp[q4]], writes=[xh_tm])
        P.dma("sp", dr["d_xh16"][ch * 128:(ch + 1) * 128, :], xh_tm[:, :],
              reads=[xh_tm], writes=[("d_xh16", ch)], slot=2 + g2)
        P.ops("pe", [
            (lambda e, kc=kc, ch=ch: e.matmul(
                pp[0][:, 0:128], xnfm[:, kc * 4096 + ch * 128: kc * 4096 + (ch + 1) * 128],
                WA(16, kc), start=(kc == 0), stop=(kc == 7)))
            for kc in range(8)], reads=[xnfm, w_inT], writes=[pp[0]])
        P.op("dve", lambda e, ch=ch: e.tensor_copy(Btm[:, ch * 128:(ch + 1) * 128], pp[0][:, 0:128]),
             reads=[pp[0]], writes=[Btm])
        P.ops("pe", [
            (lambda e, kc=kc, ch=ch: e.matmul(
                pp[1][:, 0:128], xnfm[:, kc * 4096 + ch * 128: kc * 4096 + (ch + 1) * 128],
                WA(18, kc), start=(kc == 0), stop=(kc == 7)))
            for kc in range(8)], reads=[xnfm, w_inT], writes=[pp[1]])
        P.op("dve", lambda e, ch=ch: e.tensor_copy(last_tm[:, ch * 128:(ch + 1) * 128], pp[1][:, 0:128]),
             reads=[pp[1]], writes=[last_tm])
        P.ops("pe", [
            (lambda e, kc=kc, ch=ch: e.matmul(
                pp[2][:, 0:128], WA(16, kc), xnfm[:, kc * 4096 + ch * 128: kc * 4096 + (ch + 1) * 128],
                start=(kc == 0), stop=(kc == 7)))
            for kc in range(8)], reads=[xnfm, w_inT], writes=[pp[2]])
        P.op("dve", lambda e, ch=ch: e.tensor_copy(Bfm[:, ch * 128:(ch + 1) * 128], pp[2][:, 0:128]),
             reads=[pp[2]], writes=[Bfm])
        P.ops("pe", [
            (lambda e, kc=kc, ch=ch: e.matmul(
                pp[3][:, 0:128], WA(17, kc), xnfm[:, kc * 4096 + ch * 128: kc * 4096 + (ch + 1) * 128],
                start=(kc == 0), stop=(kc == 7)))
            for kc in range(8)], reads=[xnfm, w_inT], writes=[pp[3]])
        P.op("dve", lambda e, ch=ch: e.tensor_copy(Cfm[:, ch * 128:(ch + 1) * 128], pp[3][:, 0:128]),
             reads=[pp[3]], writes=[Cfm])
        P.ops("pe", [
            (lambda e, kc=kc, ch=ch: e.matmul(
                pp[2][0:96, 128:256], bass.AP(w_inT, (kc * NT_A + 16) * 128 + 32, [[8 * NT_A * 128, 128], [1, 96]]),
                xnfm[:, kc * 4096 + ch * 128: kc * 4096 + (ch + 1) * 128],
                start=(kc == 0), stop=(kc == 7)))
            for kc in range(8)], reads=[xnfm, w_inT], writes=[(pp[2], "u")])
        P.op("dve", lambda e, ch=ch: e.tensor_copy(Bfu[0:64, ch * 128:(ch + 1) * 128],
                                                   pp[2][0:64, 128:256]),
             reads=[(pp[2], "u")], writes=[Bfu])
        P.op("dve", lambda e, ch=ch: e.tensor_copy(Bfu[64:96, ch * 128:(ch + 1) * 128],
                                                   pp[2][64:96, 128:256]),
             reads=[(pp[2], "u")], writes=[Bfu])
        P.ops("pe", [
            (lambda e, kc=kc, ch=ch: e.matmul(
                pp[3][0:96, 128:256], bass.AP(w_inT, (kc * NT_A + 17) * 128 + 32, [[8 * NT_A * 128, 128], [1, 96]]),
                xnfm[:, kc * 4096 + ch * 128: kc * 4096 + (ch + 1) * 128],
                start=(kc == 0), stop=(kc == 7)))
            for kc in range(8)], reads=[xnfm, w_inT], writes=[(pp[3], "u")])
        P.op("dve", lambda e, ch=ch: e.tensor_copy(Cfu[0:64, ch * 128:(ch + 1) * 128],
                                                   pp[3][0:64, 128:256]),
             reads=[(pp[3], "u")], writes=[Cfu])
        P.op("dve", lambda e, ch=ch: e.tensor_copy(Cfu[64:96, ch * 128:(ch + 1) * 128],
                                                   pp[3][64:96, 128:256]),
             reads=[(pp[3], "u")], writes=[Cfu])

    ctx["pop_scope"]()
    ctx["push_scope"]()
    def lview(c0):
        return bass.AP(last_tm, c0, [[4096, 128], [128, 32], [1, 32]])
    sca = sb("sca", [128, 1024], F32)
    scb = sb("scb", [128, 1024], F32)
    scc = sb("scc", [128, 1024], F32)
    a_tm = sb("a_tm", [128, 1024], F32)
    c1b = sb("c1a", [128, 1], F32)
    P.op("dve", lambda e: e.memset(c1b[:, :], 1.0), writes=[c1b])
    dtbv = lambda: bass.AP(dtb128, 0, [[32, 128], [0, 32], [1, 32]])

    def softplus(dst, dkey, src_ap, rkeys, tmp1, tmp2):
        P.op("act", lambda e: e.activation(tmp1[:, :], src_ap, AF.Abs), reads=rkeys, writes=[tmp1])
        P.op("act", lambda e: e.activation(tmp1[:, :], tmp1[:, :], AF.Exp, scale=-1.0),
             reads=[tmp1], writes=[tmp1])
        P.op("act", lambda e: e.activation(tmp1[:, :], tmp1[:, :], AF.Ln, bias=c1b[:, 0:1]),
             reads=[tmp1, c1b], writes=[tmp1])
        P.op("dve", lambda e: e.tensor_scalar(tmp2[:, :], src_ap, 0.0, None, ALU.max),
             reads=rkeys, writes=[tmp2])
        P.op("dve", lambda e: e.tensor_tensor(dst, tmp2[:, :], tmp1[:, :], ALU.add),
             reads=[tmp1, tmp2], writes=[dkey])

    P.op("dve", lambda e: e.tensor_tensor(sca[:, :], lview(0), dtbv(), ALU.add),
         reads=[last_tm, dtb128], writes=[sca])
    softplus(dt_tm[:, :], dt_tm, sca[:, :], [sca], scb, scc)
    softplus(sca[:, :], sca, lview(32), [last_tm], scb, scc)
    P.op("dve", lambda e: e.tensor_scalar(sca[:, :], sca[:, :], 1e-4, -1.0, ALU.max, ALU.mult),
         reads=[sca], writes=[sca])
    P.op("dve", lambda e: e.tensor_tensor(a_tm[:, :], sca[:, :], dt_tm[:, :], ALU.mult),
         reads=[sca, dt_tm], writes=[a_tm])
    P.op("act", lambda e: e.activation(scb[:, :], a_tm[:, :], AF.Exp), reads=[a_tm], writes=[scb])
    P.op("act", lambda e: e.activation(scc[:, :], lview(64), AF.Exp, scale=-1.0),
         reads=[last_tm], writes=[scc])
    P.op("dve", lambda e: e.tensor_scalar(scc[:, :], scc[:, :], 1.0, None, ALU.add),
         reads=[scc], writes=[scc])
    P.op("dve", lambda e: e.reciprocal(scc[:, :], scc[:, :]), reads=[scc], writes=[scc])
    P.op("dve", lambda e: e.tensor_tensor(gam_tm[:, :], scc[:, :], dt_tm[:, :], ALU.mult),
         reads=[scc, dt_tm], writes=[gam_tm])
    P.op("dve", lambda e: e.tensor_scalar(scc[:, :], scc[:, :], -1.0, -1.0, ALU.add, ALU.mult),
         reads=[scc], writes=[scc])
    P.op("dve", lambda e: e.tensor_tensor(scc[:, :], scc[:, :], dt_tm[:, :], ALU.mult),
         reads=[scc, dt_tm], writes=[scc])
    P.op("dve", lambda e: e.tensor_tensor(beta_tm[:, :], scc[:, :], scb[:, :], ALU.mult),
         reads=[scc, scb], writes=[beta_tm])
    P.op("dve", lambda e: e.tensor_copy(angles_tm[:, :], lview(96)), reads=[last_tm], writes=[angles_tm])
    if "d_dbgsc" in ctx:
        P.dma("gp", ctx["d_dbgsc"][0:128, 0:1024], beta_tm[:, :], reads=[beta_tm], writes=["dbgsc"])
        P.dma("gp", ctx["d_dbgsc"][0:128, 1024:2048], gam_tm[:, :], reads=[gam_tm], writes=["dbgsc"])
        P.dma("sp", ctx["d_dbgsc"][0:128, 2048:3072], dt_tm[:, :], reads=[dt_tm], writes=["dbgsc"])
        P.dma("sp", ctx["d_dbgsc"][0:128, 3072:4096], a_tm[:, :], reads=[a_tm], writes=["dbgsc"])

    a_fm = sb("a_fm", [32, 4096], F32)
    lamf = sb("lamf", [32, 4096], F32)
    p32 = ps("p32", [32, 128], F32)
    p32b = ps("p32b", [128, 32], F32)
    for ch in range(32):
        P.op("pe", lambda e, ch=ch: e.transpose(p32[:, :], a_tm[:, ch * 32:(ch + 1) * 32],
                                                cst["C_IDF"][:, :]),
             reads=[a_tm, cst["C_IDF"], p32], writes=[p32])
        P.op("dve", lambda e, ch=ch: e.tensor_copy(a_fm[:, ch * 128:(ch + 1) * 128], p32[:, :]),
             reads=[p32], writes=[a_fm])
    zc32 = sb("zc32", [32, 1], F32)
    P.op("dve", lambda e: e.memset(zc32[:, :], 0.0), writes=[zc32])
    for s in range(2):
        P.op("dve", lambda e, s=s: e.tensor_tensor_scan(
            lamf[:, s * 2048:(s + 1) * 2048], a_fm[:, s * 2048:(s + 1) * 2048],
            zc32[:, 0:1].to_broadcast([32, 2048]), 0.0, ALU.add, ALU.add),
            reads=[a_fm, zc32], writes=[lamf])
    for it in reversed(range(NIT)):
        if it % 16 != 0:
            P.op("dve", lambda e, it=it: e.tensor_scalar(
                lamf[:, it * 128:(it + 1) * 128], lamf[:, it * 128:(it + 1) * 128],
                bass.AP(lamf, it * 128 - 1, [[4096, 32], [1, 1]]), None, ALU.subtract),
                reads=[lamf], writes=[lamf])


    for it in range(NIT):
        P.dma("sp", bass.AP(dr["d_lam"].tensor, it * 4096, [[128, 32], [1, 128]]),
              lamf[:, it * 128:(it + 1) * 128], reads=[lamf], writes=[("d_lam", it)],
              slot=it % 2)
    for it in range(NIT):
        P.op("pe", lambda e, it=it: e.transpose(p32b[:, :], lamf[:, it * 128:(it + 1) * 128],
                                                cst["C_IDF"][0:32, 0:32]),
             reads=[lamf, cst["C_IDF"], p32b], writes=[p32b])
        P.op("dve", lambda e, it=it: e.tensor_copy(lam_tm[:, it * 32:(it + 1) * 32], p32b[:, :]),
             reads=[p32b], writes=[lam_tm])
    P.op("act", lambda e: e.activation(Ettm[:, :], lam_tm[:, :], AF.Exp),
         reads=[lam_tm], writes=[Ettm])
    e1f = a_fm
    P.op("dve", lambda e: e.tensor_tensor(
        e1f[:, :], bass.AP(lamf, 127, [[4096, 32], [128, 32], [0, 128]]), lamf[:, :],
        ALU.subtract), reads=[lamf], writes=[e1f])
    P.op("act", lambda e: e.activation(e1f[:, :], e1f[:, :], AF.Exp), reads=[e1f], writes=[e1f])
    for it in range(NIT):
        P.op("pe", lambda e, it=it: e.transpose(p32b[:, :], e1f[:, it * 128:(it + 1) * 128],
                                                cst["C_IDF"][0:32, 0:32]),
             reads=[e1f, cst["C_IDF"], p32b], writes=[p32b])
        P.op("dve", lambda e, it=it: e.tensor_copy(E1tm[:, it * 32:(it + 1) * 32], p32b[:, :]),
             reads=[p32b], writes=[E1tm])
    P.op("act", lambda e: e.activation(exlam[:, :],
                                       bass.AP(lamf, 127, [[4096, 32], [128, 32]]), AF.Exp),
         reads=[lamf], writes=[exlam])
    P.op("dve", lambda e: e.tensor_scalar(sca[:, :], dt_tm[:, :], 0.5, None, ALU.mult),
         reads=[dt_tm], writes=[sca])
    for ch in range(32):
        P.op("pe", lambda e, ch=ch: e.transpose(p32[:, :], sca[:, ch * 32:(ch + 1) * 32],
                                                cst["C_IDF"][:, :]),
             reads=[sca, cst["C_IDF"], p32], writes=[p32])
        P.op("dve", lambda e, ch=ch: e.tensor_copy(a_fm[:, ch * 128:(ch + 1) * 128], p32[:, :]),
             reads=[p32], writes=[a_fm])
    P.dma("sp", dr["d_dtf"][:, :], a_fm[:, :], reads=[a_fm], writes=[("d_dtf",)])


    ctx["pop_scope"]()
    ctx["pop_scope"]()


def phase_mamba_b(ctx):
    nc, P, sb, ps, cst, inp = ctx["nc"], ctx["P"], ctx["sb"], ctx["ps"], ctx["cst"], ctx["inp"]
    dr = ctx["drams"]
    xnfm = ctx["xnfm"]
    ctx["push_scope"]()
    ctx["push_scope"]()

    tant = sb("tant", [128, 1024], F32)
    tanf = sb("tanf", [32, 4096], F32)
    p32 = ps("p32b_", [32, 128], F32)
    P.op("act", lambda e: e.activation(tant[:, :], ctx["angles_tm"][:, :], AF.Tanh),
         reads=[ctx["angles_tm"]], writes=[tant])
    for ch in range(32):
        P.op("pe", lambda e, ch=ch: e.transpose(p32[:, :], tant[:, ch * 32:(ch + 1) * 32],
                                                cst["C_IDF"][:, :]),
             reads=[tant, cst["C_IDF"], p32], writes=[p32])
        P.op("dve", lambda e, ch=ch: e.tensor_copy(tanf[:, ch * 128:(ch + 1) * 128], p32[:, :]),
             reads=[p32], writes=[tanf])
    P.dma("sp", dr["d_tanf"][:, :], tanf[:, :], reads=[tanf], writes=[("d_tanf",)])
    ctx["pop_scope"]()
    ctx["push_scope"]()

    dtrep = sb("dtrep", [128, 2048], F32)
    tanrep = sb("tanrep", [128, 2048], F32)
    fsc = sb("fsc", [128, 2048], F32)
    f1 = sb("f1", [128, 2048], F32)
    qi = sb("qi", [128, 2048], I32)
    qf = sb("qf", [128, 2048], F32)
    trg = sb("trg", [128, 2048], BF16)
    for pair in range(16):
        for s in range(2):
            P.dma("gp", dtrep[:, :],
                  bass.AP(dr["d_dtf"].tensor, pair * 2 * T + s * 2048,
                          [[T, 2], [0, 64], [1, 2048]]),
                  reads=[("d_dtf",), dtrep], writes=[dtrep], slot=0)
            P.dma("gp", tanrep[:, :],
                  bass.AP(dr["d_tanf"].tensor, s * 2048,
                          [[0, 4], [T, 32], [1, 2048]]),
                  reads=[("d_tanf",), tanrep], writes=[tanrep], slot=1)
            P.op("dve", lambda e: e.tensor_tensor(f1[:, :], dtrep[:, :], tanrep[:, :], ALU.mult),
                 reads=[dtrep, tanrep], writes=[f1])
            P.op("dve", lambda e: e.tensor_tensor_scan(
                fsc[:, :], f1[:, :], ctx["zc128"][:, 0:1].to_broadcast([128, 2048]),
                0.0, ALU.add, ALU.add), reads=[f1, ctx["zc128"]], writes=[fsc])
            P.op("dve", lambda e: e.tensor_scalar(fsc[:, :], fsc[:, :], cst["C_QCOL"][:, 0:1],
                                                  None, ALU.add),
                 reads=[fsc, cst["C_QCOL"]], writes=[fsc])
            P.op("dve", lambda e: e.tensor_copy(qi[:, :], fsc[:, :]), reads=[fsc], writes=[qi])
            P.op("dve", lambda e: e.tensor_copy(qf[:, :], qi[:, :]), reads=[qi], writes=[qf])
            P.op("dve", lambda e: e.tensor_tensor(f1[:, :], fsc[:, :], qf[:, :], ALU.subtract),
                 reads=[fsc, qf], writes=[f1])
            P.op("dve", lambda e: e.tensor_scalar(qf[:, :], f1[:, :], 0.5, None, ALU.is_ge),
                 reads=[f1], writes=[qf])
            P.op("dve", lambda e: e.tensor_tensor(f1[:, :], f1[:, :], qf[:, :], ALU.subtract),
                 reads=[f1, qf], writes=[f1])
            P.op("dve", lambda e: e.tensor_scalar(qf[:, :], f1[:, :], -0.5, None, ALU.is_lt),
                 reads=[f1], writes=[qf])
            P.op("dve", lambda e: e.tensor_tensor(f1[:, :], f1[:, :], qf[:, :], ALU.add),
                 reads=[f1, qf], writes=[f1])
            P.op("act", lambda e: e.activation(trg[:, :], f1[:, :], AF.Sin, scale=cst["C_PI2"][:, 0:1]),
                 reads=[f1, cst["C_PI2"]], writes=[trg])
            P.dma("sp", dr["d_trigF"][s, pair], trg[:, :],
                  reads=[trg], writes=[("d_trigF", s, pair)], slot=2 + s)

    ctx["pop_scope"]()
    ctx["push_scope"]()
    NT_Z = 16
    w_inT_z = sb("w_inT_z", [128, 8 * NT_Z * 128], BF16)
    wstg = [sb(f"wstgz{i}", [128, 1024], BF16) for i in range(2)]
    pwz = [ps(f"pwz{i}", [128, 128], BF16) for i in range(2)]
    for i in range(NT_Z):
        st = wstg[i % 2]
        P.dma("gp", st[:, :], inp["w_in"][i * 128:(i + 1) * 128, :],
              reads=[st], writes=[st], slot=i % 2)
        for kc in range(8):
            pj = pwz[kc % 2]
            P.op("pe", lambda e, st=st, kc=kc, pj=pj: e.transpose(
                pj[:, :], st[:, kc * 128:(kc + 1) * 128], cst["C_ID"][:, :]),
                reads=[st, cst["C_ID"], pj], writes=[pj])
            P.op("dve", lambda e, i=i, kc=kc, pj=pj: e.tensor_copy(
                w_inT_z[:, (kc * NT_Z + i) * 128:(kc * NT_Z + i + 1) * 128], pj[:, :]),
                reads=[pj], writes=[w_inT_z])
    ppz = [ps(f"ppz{i}", [128, 512], F32) for i in range(2)]
    zsil = sb("zsil", [128, 2048], BF16)
    xh_t = sb("xh_t", [128, 2048], BF16)
    xhp_t = sb("xhp_t", [128, 2048], BF16)
    vmix = sb("vmix", [128, 2048], BF16)
    psh = ps("psh", [128, 512], F32)
    for ch in range(32):
        for q4 in range(4):
            pz = ppz[q4 % 2]
            P.ops("pe", [
                (lambda e, kc=kc, q4=q4, ch=ch, pz=pz: e.matmul(
                    pz[:, :], xnfm[:, kc * 4096 + ch * 128: kc * 4096 + (ch + 1) * 128],
                    bass.AP(w_inT_z, (kc * NT_Z + q4 * 4) * 128,
                            [[8 * NT_Z * 128, 128], [128, 4], [1, 128]]),
                    start=(kc == 0), stop=(kc == 7)))
                for kc in range(8)], reads=[xnfm, w_inT_z, pz], writes=[pz])
            P.op("act", lambda e, q4=q4, pz=pz: e.activation(zsil[:, q4 * 512:(q4 + 1) * 512],
                                                             pz[:, :], AF.Silu),
                 reads=[pz], writes=[zsil])
        P.dma("sp", dr["d_zsil"][ch * 128:(ch + 1) * 128, :], zsil[:, :],
              reads=[zsil], writes=[("d_zsil", ch)], slot=2 + ch % 2)
        P.dma("gp", xh_t[:, :], dr["d_xh16"][ch * 128:(ch + 1) * 128, :],
              reads=[("d_xh16", ch), xh_t], writes=[xh_t], slot=2)
        if ch % 16 == 0:
            for q4 in range(4):
                P.ops("pe", [(lambda e, q4=q4: e.matmul(
                    psh[:, :], cst["C_SHIFT"][:, :], xh_t[:, q4 * 512:(q4 + 1) * 512],
                    start=True, stop=True))], reads=[cst["C_SHIFT"], xh_t, psh], writes=[psh])
                P.op("dve", lambda e, q4=q4: e.tensor_copy(xhp_t[:, q4 * 512:(q4 + 1) * 512],
                                                           psh[:, :]),
                     reads=[psh], writes=[xhp_t])
        else:
            P.dma("gp", xhp_t[:, :],
                  bass.AP(dr["d_xh16"].tensor, (ch * 128 - 1) * DIN, [[DIN, 128], [1, DIN]]),
                  reads=[("d_xh16", ch), ("d_xh16", ch - 1), xhp_t], writes=[xhp_t], slot=3)
        bview = lambda ch=ch: bass.AP(ctx["beta_tm"], ch * 32, [[1024, 128], [1, 32], [0, 64]])
        gview = lambda ch=ch: bass.AP(ctx["gam_tm"], ch * 32, [[1024, 128], [1, 32], [0, 64]])
        P.op("dve", lambda e, ch=ch: e.tensor_tensor(xhp_t[:, :], xhp_t[:, :], bview(ch), ALU.mult),
             reads=[xhp_t, ctx["beta_tm"]], writes=[xhp_t])
        P.op("dve", lambda e, ch=ch: e.tensor_tensor(vmix[:, :], xh_t[:, :], gview(ch), ALU.mult),
             reads=[xh_t, ctx["gam_tm"]], writes=[vmix])
        P.op("dve", lambda e: e.tensor_tensor(vmix[:, :], vmix[:, :], xhp_t[:, :], ALU.add),
             reads=[vmix, xhp_t], writes=[vmix])
        P.dma("sp", dr["d_vmix"][ch * 128:(ch + 1) * 128, :], vmix[:, :],
              reads=[vmix], writes=[("d_vmix", ch)], slot=4)
    ctx["pop_scope"]()
    ctx["pop_scope"]()
    ctx["pop_scope"]()


def phase_scan(ctx):
    nc, P, sb, ps, cst, inp = ctx["nc"], ctx["P"], ctx["sb"], ctx["ps"], ctx["cst"], ctx["inp"]
    dr = ctx["drams"]
    Bfm, Btm, Cfm = ctx["Bfm"], ctx["Btm"], ctx["Cfm"]
    Bfu, Cfu = ctx["Bfu"], ctx["Cfu"]
    lam_tm, E1tm, Ettm, Ddup = ctx["lam_tm"], ctx["E1tm"], ctx["Ettm"], ctx["Ddup"]
    ctx["push_scope"]()

    w_outT = sb("w_outT", [128, 16 * 1024], BF16)
    ctx["push_scope"]()
    wstg = [sb(f"wstgo{i}", [128, 2048], BF16) for i in range(2)]
    pwo = [ps(f"pwo{i}", [128, 128], BF16) for i in range(2)]
    for i in range(8):
        st = wstg[i % 2]
        P.dma("gp", st[:, :], inp["w_out"][i * 128:(i + 1) * 128, :],
              reads=[st], writes=[st], slot=i % 2)
        for fc in range(16):
            pj = pwo[fc % 2]
            P.op("pe", lambda e, st=st, fc=fc, pj=pj: e.transpose(
                pj[:, :], st[:, fc * 128:(fc + 1) * 128], cst["C_ID"][:, :]),
                reads=[st, cst["C_ID"], pj], writes=[pj])
            P.op("dve", lambda e, i=i, fc=fc, pj=pj: e.tensor_copy(
                w_outT[:, fc * 1024 + i * 128: fc * 1024 + (i + 1) * 128], pj[:, :]),
                reads=[pj], writes=[w_outT])

    ctx["pop_scope"]()
    S_cs = sb("S_cs", [128, 1024], BF16)
    Sun_t = sb("Sun_t", [128, 2048], BF16)
    P.op("dve", lambda e: e.memset(S_cs[:, :], 0.0), writes=[S_cs])
    P.op("dve", lambda e: e.memset(Sun_t[:, :], 0.0), writes=[Sun_t])

    trg = [sb(f"s_trg{i}", [128, 2048], BF16) for i in range(2)]
    vm = [sb(f"s_vm{i}", [128, 2048], BF16) for i in range(2)]
    zs = [sb("s_zs0", [128, 2048], BF16)] * 2
    xh = [sb("s_xh0", [128, 2048], BF16)] * 2
    lrep = [sb("s_lrep0", [128, 4096], F32)] * 2
    exrow = [sb(f"s_exrow{i}", [1, 32], F32) for i in range(2)]
    ttm = sb("s_ttm", [128, 2048], BF16)
    kcs = sb("s_kcs", [128, 2048], BF16)
    qcs = sb("s_qcs", [128, 2048], BF16)
    ktc = sb("s_ktc", [128, 2048], BF16)
    ktu = sb("s_ktu", [128, 4096], BF16)
    P.op("dve", lambda e: e.memset(ktu[:, :], 0.0), writes=[ktu])
    dmm = sb("s_dmm", [128, 4096], BF16)
    dm1 = dmm
    gm = sb("s_gm", [128, 4096], BF16)
    bdup = sb("s_bdup", [128, 128], BF16)
    cdup = sb("s_cdup", [128, 128], BF16)
    exh = sb("s_exh", [128, 32], BF16)
    expair = sb("s_expair", [128, 16], BF16)
    xmask = sb("s_xmask", [32, 16], BF16)
    ygt = ttm
    ygf = kcs
    scics = sb("s_scics", [128, 2048], BF16)
    sciun = sb("s_sciun", [128, 4096], BF16)
    sot = sb("s_sot", [128, 1024], BF16)
    exl32 = sb("s_exl32", [32, 1], F32)
    ptrig = ps("s_ptrig", [128, 1024], BF16)
    pG = ps("s_pG", [128, 1024], F32)
    pyA = ps("s_pyA", [128, 512], F32)
    pyB = ps("s_pyB", [128, 512], F32)
    pSc = ps("s_pSc", [128, 256], F32)
    pSu = ps("s_pSu", [128, 512], F32)

    d_exl = nc.dram_tensor("d_exl", [NIT, 32], F32, kind="Internal").ap()
    exl_f = ctx["exlam"]
    P.op("pe", lambda e: e.transpose(pSc[0:32, 0:32], exl_f[:, :], cst["C_IDF"][0:32, 0:32]),
         reads=[exl_f, cst["C_IDF"], pSc], writes=[pSc])
    exlT = sb("s_exlT", [32, 32], F32)
    P.op("dve", lambda e: e.tensor_copy(exlT[:, :], pSc[0:32, 0:32]), reads=[pSc], writes=[exlT])
    P.dma("sp", d_exl[:, :], exlT[:, :], reads=[exlT], writes=["d_exl"])

    for it in range(NIT):
        sq = it // 16
        c = it % 16
        bb = it % 2
        tk = slice(it * 128, (it + 1) * 128)
        if it == 16:
            P.op("dve", lambda e: e.memset(S_cs[:, :], 0.0), reads=[S_cs], writes=[S_cs])
            P.op("dve", lambda e: e.memset(Sun_t[:, :], 0.0), reads=[Sun_t], writes=[Sun_t])
        P.dma("sp", trg[bb][:, :].rearrange("p (pr t) -> p pr t", pr=16),
              bass.AP(dr["d_trigF"].tensor, sq * 16 * 128 * SEQ + c * 128,
                      [[SEQ, 128], [128 * SEQ, 16], [1, 128]]),
              reads=[("d_trigF", sq), trg[bb]], writes=[trg[bb]], slot=0)
        P.dma("sp", vm[bb][:, :], dr["d_vmix"][it * 128:(it + 1) * 128, :],
              reads=[("d_vmix", it), vm[bb]], writes=[vm[bb]], slot=1)
        P.dma("sp", zs[bb][:, :], dr["d_zsil"][it * 128:(it + 1) * 128, :],
              reads=[("d_zsil", it), zs[bb]], writes=[zs[bb]], slot=2)
        P.dma("sp", xh[bb][:, :], dr["d_xh16"][it * 128:(it + 1) * 128, :],
              reads=[("d_xh16", it), xh[bb]], writes=[xh[bb]], slot=3)
        P.dma("gp", lrep[bb][:, :],
              bass.AP(dr["d_lam"].tensor, it * 4096, [[0, 128], [1, 4096]]),
              reads=[("d_lam", it), lrep[bb]], writes=[lrep[bb]], slot=4)
        P.dma("sp", exrow[bb][:, :], d_exl[it:it + 1, :],
              reads=["d_exl", exrow[bb]], writes=[exrow[bb]], slot=4)
        for hf in range(2):
            for p8 in range(8):
                p = hf * 8 + p8
                P.op("pe", lambda e, p=p, p8=p8, bb=bb: e.transpose(
                    ptrig[:, p8 * 128:(p8 + 1) * 128], trg[bb][:, p * 128:(p + 1) * 128],
                    cst["C_ID"][:, :]),
                    reads=[trg[bb], cst["C_ID"], ptrig], writes=[ptrig])
            for q2 in range(2):
                P.op("act", lambda e, hf=hf, q2=q2: e.activation(
                    ttm[:, hf * 1024 + q2 * 512: hf * 1024 + (q2 + 1) * 512],
                    ptrig[:, q2 * 512:(q2 + 1) * 512], AF.Copy),
                    reads=[ptrig], writes=[ttm])
        P.op("pe", lambda e, tk=tk: e.matmul(pG[:, 0:128], cst["C_PDUP"][:, :], Bfm[0:32, tk],
                                      start=True, stop=True), reads=[Bfm, cst["C_PDUP"], pG], writes=[pG])
        P.op("pe", lambda e, tk=tk: e.matmul(pG[:, 128:256], cst["C_PDUP"][:, :], Cfm[0:32, tk],
                                      start=True, stop=True), reads=[Cfm, cst["C_PDUP"], pG], writes=[pG])
        P.op("dve", lambda e: e.tensor_copy(bdup[:, :], pG[:, 0:128]), reads=[pG], writes=[bdup])
        P.op("dve", lambda e: e.tensor_copy(cdup[:, :], pG[:, 128:256]), reads=[pG], writes=[cdup])
        P.op("dve", lambda e, bb=bb: e.tensor_copy(exl32[:, :],
                                                   exrow[bb][0:1, :].transpose([1, 0]) if False else exrow[bb][0:1, :]),
             reads=[exrow[bb]], writes=[exl32]) if False else None
        P.op("pe", lambda e, bb=bb: e.matmul(pyA[:, 0:32], cst["C_ONES1F"][0:1, :], exrow[bb][0:1, :],
                                             start=True, stop=True),
             reads=[exrow[bb], cst["C_ONES1F"], pyA], writes=[pyA])
        P.op("dve", lambda e: e.tensor_copy(exh[:, :], pyA[:, 0:32]), reads=[pyA], writes=[exh])
        P.op("dve", lambda e: e.tensor_tensor(xmask[:, :], cst["C_HALF"][:, :],
                                              bass.AP(exh, 0, [[32, 32], [0, 16]]) if False else exh[0:32, 0:16],
                                              ALU.mult),
             reads=[cst["C_HALF"], exh], writes=[xmask]) if False else None
        P.op("pe", lambda e, bb=bb: e.transpose(pSc[0:32, 0:1], exrow[bb][0:1, :],
                                                cst["C_IDF"][0:1, 0:1]),
             reads=[exrow[bb], cst["C_IDF"], pSc], writes=[pSc])
        P.op("dve", lambda e: e.tensor_copy(exl32[:, :], pSc[0:32, 0:1]), reads=[pSc], writes=[exl32])
        exl32b = exl32
        P.op("dve", lambda e: e.tensor_tensor(xmask[:, :], cst["C_HALF"][:, :],
                                              exl32b[:, 0:1].to_broadcast([32, 16]), ALU.mult),
             reads=[cst["C_HALF"], exl32], writes=[xmask])
        P.op("pe", lambda e: e.matmul(pyB[:, 0:16], cst["C_PR"][:, :], xmask[:, :],
                                      start=True, stop=True), reads=[cst["C_PR"], xmask, pyB], writes=[pyB])
        P.op("dve", lambda e: e.tensor_copy(expair[:, :], pyB[:, 0:16]), reads=[pyB], writes=[expair])
        P.op("dve", lambda e: e.tensor_tensor(
            scics[:, :], bass.AP(cst["C_ID"], 0, [[128, 128], [0, 16], [1, 128]]),
            bass.AP(expair, 0, [[16, 128], [1, 16], [0, 128]]), ALU.mult),
            reads=[cst["C_ID"], expair], writes=[scics])
        P.op("dve", lambda e: e.tensor_tensor(
            sciun[:, :], bass.AP(cst["C_ID"], 0, [[128, 128], [0, 32], [1, 128]]),
            bass.AP(exh, 0, [[32, 128], [1, 32], [0, 128]]), ALU.mult),
            reads=[cst["C_ID"], exh], writes=[sciun])
        P.op("dve", lambda e, bb=bb: e.tensor_tensor(
            kcs[:, :], trg[bb][:, :], bass.AP(bdup, 0, [[128, 128], [0, 16], [1, 128]]), ALU.mult),
            reads=[trg[bb], bdup], writes=[kcs])
        P.op("dve", lambda e, bb=bb: e.tensor_tensor(
            qcs[:, :], trg[bb][:, :], bass.AP(cdup, 0, [[128, 128], [0, 16], [1, 128]]), ALU.mult),
            reads=[trg[bb], cdup], writes=[qcs])
        P.op("dve", lambda e, it=it: e.tensor_tensor(
            ktc[:, :], ttm[:, :], bass.AP(Btm, it * 128, [[4096, 128], [0, 64], [1, 32]]), ALU.mult),
            reads=[ttm, Btm], writes=[ktc])
        P.op("dve", lambda e, it=it: e.tensor_tensor(
            ktc[:, :], ktc[:, :], bass.AP(E1tm, it * 32, [[1024, 128], [1, 32], [0, 64]]), ALU.mult),
            reads=[ktc, E1tm], writes=[ktc])
        P.op("dve", lambda e, it=it: e.tensor_tensor(
            bass.AP(ktu, 0, [[4096, 128], [128, 32], [1, 96]]),
            bass.AP(Btm, it * 128 + 32, [[4096, 128], [0, 32], [1, 96]]),
            bass.AP(E1tm, it * 32, [[1024, 128], [1, 32], [0, 96]]), ALU.mult),
            reads=[Btm, E1tm], writes=[ktu])
        P.op("dve", lambda e, bb=bb, it=it: e.tensor_tensor(
            dm1[:, :], lrep[bb][:, :], bass.AP(lam_tm, it * 32, [[1024, 128], [1, 32], [0, 128]]),
            ALU.subtract), reads=[lrep[bb], lam_tm], writes=[dm1])
        P.op("dve", lambda e: e.tensor_tensor(
            dmm[:, :], dm1[:, :], bass.AP(cst["C_TRINEG"], 0, [[128, 128], [0, 32], [1, 128]]),
            ALU.add), reads=[dm1, cst["C_TRINEG"]], writes=[dmm])
        P.op("act", lambda e: e.activation(dmm[:, :], dmm[:, :], AF.Exp), reads=[dmm], writes=[dmm])
        if it == 2 and "d_dbgs" in ctx:
            P.dma("gp", ctx["d_dbgs2"][0:128, 0:1024], S_cs[:, :], reads=[S_cs], writes=["dbgs2"])
            P.dma("gp", ctx["d_dbgs2"][0:128, 1024:3072], Sun_t[:, :], reads=[Sun_t], writes=["dbgs2"])
            P.dma("gp", ctx["d_dbgs2"][0:128, 3072:5120], ktc[:, :], reads=[ktc], writes=["dbgs2"])
        if it == 1 and "d_dbgs" in ctx:
            P.dma("gp", ctx["d_dbgs"][0:128, 0:16], expair[:, :], reads=[expair], writes=["dbgs1"])
            P.dma("gp", ctx["d_dbgs"][0:128, 16:2064], scics[:, :], reads=[scics], writes=["dbgs1"])
            P.dma("gp", ctx["d_dbgs"][0:128, 2064:2096], exh[:, :], reads=[exh], writes=["dbgs1"])
            P.dma("sp", ctx["d_dbgs"][0:128, 2096:3120], lrep[bb][0:1, 0:1024].to_broadcast([128, 1024]) if False else lrep[bb][:, 0:1024], reads=[lrep[bb]], writes=["dbgs1"])
        vmb = vm[it % 2]
        for b in range(4):
            hs = range(b * 8, (b + 1) * 8)
            gsl = slice(b * 1024, (b + 1) * 1024)
            fns = []
            for j, h in enumerate(hs):
                pr = h // 2
                h2 = h % 2
                fns.append((lambda e, j=j, h=h, pr=pr, h2=h2: e.matmul(
                    pG[:, j * 128:(j + 1) * 128],
                    kcs[h2 * 64:(h2 + 1) * 64, pr * 128:(pr + 1) * 128],
                    qcs[h2 * 64:(h2 + 1) * 64, pr * 128:(pr + 1) * 128],
                    start=True, stop=False)))
                fns.append((lambda e, j=j, it=it: e.matmul(
                    pG[:, j * 128:(j + 1) * 128], Bfu[:, it * 128:(it + 1) * 128],
                    Cfu[:, it * 128:(it + 1) * 128], start=False, stop=True)))
            P.ops("pe", fns, reads=[kcs, qcs, Bfu, Cfu, pG], writes=[pG])
            P.op("act", lambda e, gsl=gsl: e.activation(gm[:, gsl], pG[:, 0:1024], AF.Copy),
                 reads=[pG], writes=[gm])
            P.op("dve", lambda e, gsl=gsl: e.tensor_tensor(gm[:, gsl], gm[:, gsl], dmm[:, gsl],
                                                           ALU.mult),
                 reads=[gm, dmm], writes=[gm])
            fnsA, fnsB = [], []
            for j, h in enumerate(hs):
                pr = h // 2
                h2 = h % 2
                fnsA.append((lambda e, j=j, h=h, gsl=gsl, vmb=vmb: e.matmul(
                    pyA[:, j * 64:(j + 1) * 64], gm[:, gsl.start + j * 128: gsl.start + (j + 1) * 128],
                    vmb[:, h * 64:(h + 1) * 64], start=True, stop=True)))
                fnsB.append((lambda e, j=j, h=h, pr=pr, h2=h2: e.matmul(
                    pyB[:, j * 64:(j + 1) * 64],
                    qcs[h2 * 64:(h2 + 1) * 64, pr * 128:(pr + 1) * 128],
                    S_cs[h2 * 64:(h2 + 1) * 64, pr * 64:(pr + 1) * 64],
                    start=True, stop=False)))
                fnsB.append((lambda e, j=j, h=h, it=it: e.matmul(
                    pyB[:, j * 64:(j + 1) * 64], Cfu[:, it * 128:(it + 1) * 128],
                    Sun_t[:, h * 64:(h + 1) * 64], start=False, stop=True)))
            P.ops("pe", fnsA, reads=[gm, vmb, pyA], writes=[pyA])
            P.ops("pe", fnsB, reads=[qcs, Cfu, S_cs, Sun_t, pyB], writes=[pyB])
            fnsS = []
            for j, h in enumerate(hs):
                pr = h // 2
                h2 = h % 2
                fnsS.append((lambda e, h=h, pr=pr, h2=h2: e.matmul(
                    pSc[h2 * 64:(h2 + 1) * 64, (pr % 4) * 64:(pr % 4 + 1) * 64],
                    scics[:, pr * 128 + h2 * 64: pr * 128 + (h2 + 1) * 64],
                    S_cs[:, pr * 64:(pr + 1) * 64],
                    start=True, stop=False)))
                fnsS.append((lambda e, h=h, pr=pr, h2=h2, vmb=vmb: e.matmul(
                    pSc[h2 * 64:(h2 + 1) * 64, (pr % 4) * 64:(pr % 4 + 1) * 64],
                    ktc[:, h * 64:(h + 1) * 64], vmb[:, h * 64:(h + 1) * 64],
                    start=False, stop=True)))
            P.ops("pe", fnsS, reads=[scics, S_cs, ktc, vmb, pSc], writes=[pSc])
            fnsU = []
            for j, h in enumerate(hs):
                fnsU.append((lambda e, j=j, h=h: e.matmul(
                    pSu[:, j * 64:(j + 1) * 64], sciun[:, h * 128:(h + 1) * 128],
                    Sun_t[:, h * 64:(h + 1) * 64], start=True, stop=False)))
                fnsU.append((lambda e, j=j, h=h, vmb=vmb: e.matmul(
                    pSu[:, j * 64:(j + 1) * 64], ktu[:, h * 128:(h + 1) * 128],
                    vmb[:, h * 64:(h + 1) * 64], start=False, stop=True)))
            P.ops("pe", fnsU, reads=[sciun, Sun_t, ktu, vmb, pSu], writes=[pSu])
            P.op("act", lambda e, b=b: e.activation(S_cs[:, b * 256:(b + 1) * 256], pSc[:, 0:256],
                                                    AF.Copy),
                 reads=[pSc], writes=[S_cs])
            P.op("act", lambda e, b=b: e.activation(Sun_t[:, b * 512:(b + 1) * 512],
                                                    pSu[:, 0:512], AF.Copy),
                 reads=[pSu], writes=[Sun_t])
            ysl = slice(b * 512, (b + 1) * 512)
            P.op("dve", lambda e, b=b, it=it, ysl=ysl: e.tensor_tensor(
                ygt[:, ysl], pyB[:, 0:512],
                bass.AP(Ettm, it * 32 + b * 8, [[1024, 128], [1, 8], [0, 64]]), ALU.mult),
                reads=[pyB, Ettm], writes=[ygt])
            P.op("dve", lambda e, ysl=ysl: e.tensor_tensor(ygt[:, ysl], ygt[:, ysl], pyA[:, 0:512],
                                                           ALU.add),
                 reads=[ygt, pyA], writes=[ygt])
        P.op("dve", lambda e, bb=bb: e.tensor_tensor(ygf[:, :], xh[bb][:, :], Ddup[:, :], ALU.mult),
             reads=[xh[bb], Ddup], writes=[ygf])
        P.op("dve", lambda e: e.tensor_tensor(ygt[:, :], ygt[:, :], ygf[:, :], ALU.add),
             reads=[ygt, ygf], writes=[ygt])
        P.op("dve", lambda e, bb=bb: e.tensor_tensor(ygt[:, :], ygt[:, :], zs[bb][:, :], ALU.mult),
             reads=[ygt, zs[bb]], writes=[ygt])
        for hf in range(2):
            for p8 in range(8):
                p = hf * 8 + p8
                P.op("pe", lambda e, p=p, p8=p8: e.transpose(
                    ptrig[:, p8 * 128:(p8 + 1) * 128], ygt[:, p * 128:(p + 1) * 128],
                    cst["C_ID"][:, :]),
                    reads=[ygt, cst["C_ID"], ptrig], writes=[ptrig])
            for q2 in range(2):
                P.op("dve", lambda e, hf=hf, q2=q2: e.tensor_copy(
                    ygf[:, hf * 1024 + q2 * 512: hf * 1024 + (q2 + 1) * 512],
                    ptrig[:, q2 * 512:(q2 + 1) * 512]),
                    reads=[ptrig], writes=[ygf])
        fnsO = []
        for half in range(2):
            for fc in range(16):
                fnsO.append((lambda e, fc=fc, half=half: e.matmul(
                    pG[:, half * 512:(half + 1) * 512], ygf[:, fc * 128:(fc + 1) * 128],
                    w_outT[:, fc * 1024 + half * 512: fc * 1024 + (half + 1) * 512],
                    start=(fc == 0), stop=(fc == 15))))
        P.ops("pe", fnsO, reads=[ygf, w_outT, pG], writes=[pG])
        P.op("dve", lambda e: e.tensor_copy(sot[:, :], pG[:, 0:1024]), reads=[pG], writes=[sot])
        P.dma("sp", dr["d_s16"][it * 128:(it + 1) * 128, :], sot[:, :],
              reads=[sot], writes=[("d_s16", it)], slot=5)
    ctx["pop_scope"]()
    ctx["pop_scope"]()
    ctx["pop_scope"]()


def phase_scatter(ctx):
    nc, P, cst = ctx["nc"], ctx["P"], ctx["cst"]
    sbroot = ctx["sbroot"]
    dr = ctx["drams"]
    zti = sbroot("zti", [128, 8], I32)
    tki8 = sbroot("tki8", [128, 256], I32)
    sidx_sc = [sbroot(f"sidxsc{i}", [128, 32], I32) for i in range(2)]
    sidx_gb = [sbroot(f"sidxgb{i}", [128, 32], I32) for i in range(2)]
    ncl = sbroot("ncl", [128, 1], F32)
    P.op("dve", lambda e: e.tensor_scalar(ncl[:, :], cst["C_IOTA128"][:, 0:1], float(NSLOT),
                                          None, ALU.add),
         reads=[cst["C_IOTA128"]], writes=[ncl])
    for k in range(2):
        P.op("dve", lambda e, k=k: e.tensor_scalar(sidx_sc[k][:, :], ctx["sidx_tm"][k][:, :],
                                                   ncl[:, 0:1], None, ALU.min),
             reads=[ctx["sidx_tm"][k], ncl], writes=[sidx_sc[k]])
        P.op("dve", lambda e, k=k: e.tensor_scalar(sidx_gb[k][:, :], ctx["sidx_tm"][k][:, :],
                                                   float(NSLOT - 1), None, ALU.min),
             reads=[ctx["sidx_tm"][k]], writes=[sidx_gb[k]])
    ctx["sidx_sc"] = sidx_sc
    ctx["sidx_gb"] = sidx_gb
    P.op("dve", lambda e: e.memset(zti[:, :], 0.0), writes=[zti])
    P.dma("sp", bass.AP(dr["d_tokid"].tensor, 0, [[8, NSLOT], [1, 8]]),
          bass.AP(zti, 0, [[8, 128], [0, 128], [1, 8]]),
          reads=[zti], writes=[("d_tokid", "z")], slot=4)
    for j in range(32):
        P.op("dve", lambda e, j=j: e.tensor_scalar(
            tki8[:, j * 8:(j + 1) * 8], cst["C_IOTA128"][:, 0:1].to_broadcast([128, 8]),
            float(j * 128), None, ALU.add),
            reads=[cst["C_IOTA128"]], writes=[tki8])
    for j in range(32):
        for k in range(2):
            P.dma_f("gp", lambda e, j=j, k=k: e.indirect_dma_start(
                dr["d_tokid"], IndirectOffsetOnAxis(ctx["sidx_sc"][k][:, j:j + 1], 0),
                tki8[:, j * 8:(j + 1) * 8], None),
                reads=[tki8, ctx["sidx_sc"][k], ("d_tokid", "z")],
                writes=[("d_tokid", k)], slot=2 + k)


def phase_moe(ctx):
    nc, P, sb, ps, cst, inp = ctx["nc"], ctx["P"], ctx["sb"], ctx["ps"], ctx["cst"], ctx["inp"]
    dr = ctx["drams"]
    widx = ctx["widx"]
    TOKD = [("d_tokid", "z"), ("d_tokid", 0), ("d_tokid", 1)]
    XNKS = [("d_xn16", c) for c in range(32)]
    ctx["push_scope"]()
    wfall = sb("m_wfall", [128, NMT], F32)
    itmp = sb("m_itmp", [128, 2], F32)
    w8 = sb("m_w8", [128, NMT], F32)
    w11 = sb("m_w11", [128, NMT], F32)
    widxp_all = sb("m_widxp", [128, NMT * 8], I32)
    widxd_all = sb("m_widxd", [128, NMT * 11], I32)
    P.op("dve", lambda e: e.tensor_copy(wfall[:, :], widx[:, :]), reads=[widx], writes=[wfall])
    P.op("dve", lambda e: e.tensor_scalar(itmp[:, 0:1], cst["C_IOTA128"][:, 0:1],
                                          float((E - 1) * 128), None, ALU.add),
         reads=[cst["C_IOTA128"]], writes=[itmp])
    P.op("dve", lambda e: e.tensor_scalar(wfall[:, :], wfall[:, :], itmp[:, 0:1], None,
                                          ALU.min),
         reads=[wfall, itmp], writes=[wfall])
    P.op("dve", lambda e: e.tensor_scalar(itmp[:, 0:1], cst["C_IOTA128"][:, 0:1], 7.0, None, ALU.mult),
         reads=[cst["C_IOTA128"]], writes=[itmp])
    P.op("dve", lambda e: e.tensor_scalar(itmp[:, 1:2], cst["C_IOTA128"][:, 0:1], 10.0, None, ALU.mult),
         reads=[cst["C_IOTA128"]], writes=[itmp])
    P.op("dve", lambda e: e.scalar_tensor_tensor(w8[:, :], wfall[:, :], 8.0,
                                                 itmp[:, 0:1].to_broadcast([128, NMT]),
                                                 ALU.mult, ALU.subtract),
         reads=[wfall, itmp], writes=[w8])
    P.op("dve", lambda e: e.scalar_tensor_tensor(w11[:, :], wfall[:, :], 11.0,
                                                 itmp[:, 1:2].to_broadcast([128, NMT]),
                                                 ALU.mult, ALU.subtract),
         reads=[wfall, itmp], writes=[w11])
    for kc in range(8):
        P.op("dve", lambda e, kc=kc: e.tensor_scalar(
            bass.AP(widxp_all, kc, [[NMT * 8, 128], [8, NMT]]), w8[:, :],
            float(kc * 128), None, ALU.add),
            reads=[w8], writes=[widxp_all])
    for fc in range(11):
        P.op("dve", lambda e, fc=fc: e.tensor_scalar(
            bass.AP(widxd_all, fc, [[NMT * 11, 128], [11, NMT]]), w11[:, :],
            float(fc * 128), None, ALU.add),
            reads=[w11], writes=[widxd_all])

    wgs = [sb(f"m_wg{i}", [128, 8 * FF], BF16) for i in range(2)]
    wus = [sb(f"m_wu{i}", [128, 8 * FF], BF16) for i in range(2)]
    wds = [sb(f"m_wd{i}", [128, 11 * 1024], BF16) for i in range(2)]
    xin = [sb(f"m_xin{i}", [128, 1024], BF16) for i in range(2)]
    tkld = [sb(f"m_tk{i}", [128, 8], I32) for i in range(2)]
    xgfm = sb("m_xgfm", [128, 8 * 512], BF16)
    hg = sb("m_hg", [128, 512], BF16)
    hfm = sb("m_hfm", [128, 11 * 512], BF16)
    eo = sb("m_eo", [128, 1024], BF16)
    ptile = ps("m_ptile", [128, 1024], BF16)
    pga = [ps(f"m_pga{i}", [128, 512], F32) for i in range(2)]
    pup = [ps(f"m_pup{i}", [128, 512], F32) for i in range(2)]
    pdn = ps("m_pdn", [128, 1024], F32)
    wgR = inp["wg"].rearrange("e d f -> (e d) f")
    wuR = inp["wu"].rearrange("e d f -> (e d) f")
    wdR = inp["wd"].rearrange("e f d -> (e f) d")

    for t in range(NMT):
        b = t % 2
        wg_t, wu_t, wd_t = wgs[b], wus[b], wds[b]
        for kc in range(8):
            P.dma_f("gp", lambda e, t=t, kc=kc, wg_t=wg_t: e.indirect_dma_start(
                wg_t[:, kc * FF:(kc + 1) * FF], None, wgR,
                IndirectOffsetOnAxis(widxp_all[:, t * 8 + kc: t * 8 + kc + 1], 0)),
                reads=[widxp_all], writes=[wg_t], slot=0)
            P.dma_f("gp", lambda e, t=t, kc=kc, wu_t=wu_t: e.indirect_dma_start(
                wu_t[:, kc * FF:(kc + 1) * FF], None, wuR,
                IndirectOffsetOnAxis(widxp_all[:, t * 8 + kc: t * 8 + kc + 1], 0)),
                reads=[widxp_all], writes=[wu_t], slot=1)
        for fc in range(11):
            P.dma_f("gp", lambda e, t=t, fc=fc, wd_t=wd_t: e.indirect_dma_start(
                wd_t[:, fc * 1024:(fc + 1) * 1024], None, wdR,
                IndirectOffsetOnAxis(widxd_all[:, t * 11 + fc: t * 11 + fc + 1], 0)),
                reads=[widxd_all], writes=[wd_t], slot=2)
        for j in range(4):
            xi = xin[j % 2]
            tk = tkld[j % 2]
            P.dma("sp", tk[:, :],
                  bass.AP(dr["d_tokid"].tensor, (t * MT + j * 128) * 8, [[8, 128], [1, 8]]),
                  reads=TOKD, writes=[tk], slot=j % 2)
            P.dma_f("gp", lambda e, xi=xi, tk=tk: e.indirect_dma_start(
                xi[:, :], None, dr["d_xn16"], IndirectOffsetOnAxis(tk[:, 0:1], 0)),
                reads=[tk] + XNKS, writes=[xi], slot=3)
            for kc in range(8):
                P.op("pe", lambda e, kc=kc, xi=xi: e.transpose(
                    ptile[:, kc * 128:(kc + 1) * 128], xi[:, kc * 128:(kc + 1) * 128],
                    cst["C_ID"][:, :]),
                    reads=[xi, cst["C_ID"], ptile], writes=[ptile])
            P.op("dve", lambda e, j=j: e.tensor_copy(
                bass.AP(xgfm, j * 128, [[4096, 128], [512, 8], [1, 128]]),
                bass.AP(ptile, 0, [[1024, 128], [128, 8], [1, 128]])),
                reads=[ptile], writes=[xgfm])
        for fc in range(11):
            pa = pga[fc % 2]
            pu = pup[fc % 2]
            P.ops("pe", [(lambda e, kc=kc, fc=fc, wg_t=wg_t, pa=pa: e.matmul(
                pa[:, :], wg_t[:, kc * FF + fc * 128: kc * FF + (fc + 1) * 128],
                xgfm[:, kc * 512:(kc + 1) * 512], start=(kc == 0), stop=(kc == 7)))
                for kc in range(8)], reads=[wg_t, xgfm, pa], writes=[pa])
            P.op("act", lambda e, pa=pa: e.activation(hg[:, :], pa[:, :], AF.Silu),
                 reads=[pa], writes=[hg])
            P.ops("pe", [(lambda e, kc=kc, fc=fc, wu_t=wu_t, pu=pu: e.matmul(
                pu[:, :], wu_t[:, kc * FF + fc * 128: kc * FF + (fc + 1) * 128],
                xgfm[:, kc * 512:(kc + 1) * 512], start=(kc == 0), stop=(kc == 7)))
                for kc in range(8)], reads=[wu_t, xgfm, pu], writes=[pu])
            P.op("dve", lambda e, fc=fc, pu=pu: e.tensor_tensor(
                hfm[:, fc * 512:(fc + 1) * 512], hg[:, :], pu[:, :], ALU.mult),
                reads=[hg, pu], writes=[hfm])
        for j in range(4):
            fnsD = []
            for half in range(2):
                for fc in range(11):
                    fnsD.append((lambda e, j=j, fc=fc, half=half, wd_t=wd_t: e.matmul(
                        pdn[:, half * 512:(half + 1) * 512],
                        hfm[:, fc * 512 + j * 128: fc * 512 + (j + 1) * 128],
                        wd_t[:, fc * 1024 + half * 512: fc * 1024 + (half + 1) * 512],
                        start=(fc == 0), stop=(fc == 10))))
            P.ops("pe", fnsD, reads=[hfm, wd_t, pdn], writes=[pdn])
            P.op("dve", lambda e: e.tensor_copy(eo[:, :], pdn[:, 0:1024]), reads=[pdn], writes=[eo])
            P.dma("sp", dr["d_eout"][t * MT + j * 128: t * MT + (j + 1) * 128, :], eo[:, :],
                  reads=[eo], writes=["d_eout"], slot=4)
    ctx["pop_scope"]()


def phase_final(ctx):
    nc, P, sb, ps, cst, inp = ctx["nc"], ctx["P"], ctx["sb"], ctx["ps"], ctx["cst"], ctx["inp"]
    dr = ctx["drams"]
    OUT = ctx["OUT"]
    ctx["push_scope"]()
    sg = sb("f_sg", [128, 8 * FF], BF16)
    su = sb("f_su", [128, 8 * FF], BF16)
    sd = sb("f_sd", [128, 11 * 1024], BF16)
    P.dma("gp", sg[:, :], inp["swg"].rearrange("(kc p) f -> p kc f", p=128), writes=[sg], slot=0)
    P.dma("gp", su[:, :], inp["swu"].rearrange("(kc p) f -> p kc f", p=128), writes=[su], slot=1)
    P.dma("gp", sd[:, :], inp["swd"].rearrange("(fc p) d -> p fc d", p=128), writes=[sd], slot=2)

    xnt = [sb(f"f_xn{i}", [128, 2048], BF16) for i in range(2)]
    xlt = [sb(f"f_xl{i}", [128, 2048], F32) for i in range(2)]
    st2 = [sb(f"f_st{i}", [128, 4], F32) for i in range(2)]
    s_t = [sb(f"f_s{i}", [128, 2048], BF16) for i in range(2)]
    g0 = sb("f_g0", [128, 2048], BF16)
    g1 = sb("f_g1", [128, 2048], BF16)
    xn32 = sb("f_xn32", [128, 2048], F32)
    xgfm = sb("f_xgfm", [128, 8 * 256], BF16)
    hg = sb("f_hg", [128, 256], BF16)
    hfm = sb("f_hfm", [128, 11 * 256], BF16)
    ffn = sb("f_ffn", [128, 2048], F32)
    outt = sb("f_out", [128, 2048], F32)
    ptile = ps("f_ptile", [128, 1024], BF16)
    pga = [ps(f"f_pga{i}", [128, 256], F32) for i in range(2)]
    pup = [ps(f"f_pup{i}", [128, 256], F32) for i in range(2)]
    pdn = ps("f_pdn", [128, 1024], F32)

    for pr in range(16):
        b = pr % 2
        xn2, xl2, stp, stt = xnt[b], xlt[b], st2[b], s_t[b]
        P.dma("sp", xn2[:, :],
              bass.AP(dr["d_xn16"].tensor, pr * 256 * D, [[D, 128], [128 * D, 2], [1, D]]),
              reads=[("d_xn16", 2 * pr), ("d_xn16", 2 * pr + 1)], writes=[xn2], slot=0)
        P.dma("sp", xl2[:, :],
              bass.AP(inp["xloc"].tensor, pr * 256 * D, [[D, 128], [128 * D, 2], [1, D]]),
              writes=[xl2], slot=1)
        P.dma("sp", stp[:, :],
              bass.AP(dr["d_st"].tensor, pr * 256 * 2, [[2, 128], [256, 2], [1, 2]]),
              reads=[("d_st", 2 * pr), ("d_st", 2 * pr + 1)], writes=[stp], slot=2)
        P.dma("sp", stt[:, :],
              bass.AP(dr["d_s16"].tensor, pr * 256 * D, [[D, 128], [128 * D, 2], [1, D]]),
              reads=[("d_s16", 2 * pr), ("d_s16", 2 * pr + 1)], writes=[stt], slot=3)
        for jj in range(2):
            j = pr * 2 + jj
            for k in range(2):
                gb = g0 if k == 0 else g1
                P.dma_f("gp", lambda e, j=j, jj=jj, k=k, gb=gb: e.indirect_dma_start(
                    gb[:, jj * 1024:(jj + 1) * 1024], None, dr["d_eout"],
                    IndirectOffsetOnAxis(ctx["sidx_gb"][k][:, j:j + 1], 0)),
                    reads=[ctx["sidx_gb"][k], "d_eout"], writes=[gb], slot=3 + k)
        for jj in range(2):
            for kc in range(8):
                P.op("pe", lambda e, jj=jj, kc=kc, xn2=xn2: e.transpose(
                    ptile[:, kc * 128:(kc + 1) * 128],
                    xn2[:, jj * 1024 + kc * 128: jj * 1024 + (kc + 1) * 128],
                    cst["C_ID"][:, :]),
                    reads=[xn2, cst["C_ID"], ptile], writes=[ptile])
            P.op("dve", lambda e, jj=jj: e.tensor_copy(
                bass.AP(xgfm, jj * 128, [[2048, 128], [256, 8], [1, 128]]),
                bass.AP(ptile, 0, [[1024, 128], [128, 8], [1, 128]])),
                reads=[ptile], writes=[xgfm])
        for fc in range(11):
            pa = pga[fc % 2]
            pu = pup[fc % 2]
            P.ops("pe", [(lambda e, kc=kc, fc=fc, pa=pa: e.matmul(
                pa[:, :], sg[:, kc * FF + fc * 128: kc * FF + (fc + 1) * 128],
                xgfm[:, kc * 256:(kc + 1) * 256], start=(kc == 0), stop=(kc == 7)))
                for kc in range(8)], reads=[sg, xgfm, pa], writes=[pa])
            P.op("act", lambda e, pa=pa: e.activation(hg[:, :], pa[:, :], AF.Silu),
                 reads=[pa], writes=[hg])
            P.ops("pe", [(lambda e, kc=kc, fc=fc, pu=pu: e.matmul(
                pu[:, :], su[:, kc * FF + fc * 128: kc * FF + (fc + 1) * 128],
                xgfm[:, kc * 256:(kc + 1) * 256], start=(kc == 0), stop=(kc == 7)))
                for kc in range(8)], reads=[su, xgfm, pu], writes=[pu])
            P.op("dve", lambda e, fc=fc, pu=pu: e.tensor_tensor(
                hfm[:, fc * 256:(fc + 1) * 256], hg[:, :], pu[:, :], ALU.mult),
                reads=[hg, pu], writes=[hfm])
        for jj in range(2):
            j = pr * 2 + jj
            sl = slice(jj * 1024, (jj + 1) * 1024)
            fnsD = []
            for half in range(2):
                for fc in range(11):
                    fnsD.append((lambda e, jj=jj, fc=fc, half=half: e.matmul(
                        pdn[:, half * 512:(half + 1) * 512],
                        hfm[:, fc * 256 + jj * 128: fc * 256 + (jj + 1) * 128],
                        sd[:, fc * 1024 + half * 512: fc * 1024 + (half + 1) * 512],
                        start=(fc == 0), stop=(fc == 10))))
            P.ops("pe", fnsD, reads=[hfm, sd, pdn], writes=[pdn])
            P.op("dve", lambda e, sl=sl, jj=jj, xl2=xl2, stp=stp: e.tensor_scalar(
                xn32[:, sl], xl2[:, sl], stp[:, jj * 2 + 1: jj * 2 + 2],
                stp[:, jj * 2: jj * 2 + 1], ALU.subtract, ALU.mult),
                reads=[xl2, stp], writes=[xn32])
            P.op("dve", lambda e, j=j, sl=sl: e.tensor_scalar(
                ffn[:, sl], g0[:, sl], ctx["val_tm"][0][:, j:j + 1], None, ALU.mult),
                reads=[g0, ctx["val_tm"][0]], writes=[ffn])
            P.op("dve", lambda e, j=j, sl=sl: e.tensor_scalar(
                outt[:, sl], g1[:, sl], ctx["val_tm"][1][:, j:j + 1], None, ALU.mult),
                reads=[g1, ctx["val_tm"][1]], writes=[outt])
            P.op("dve", lambda e, sl=sl: e.tensor_tensor(ffn[:, sl], ffn[:, sl], outt[:, sl], ALU.add),
                 reads=[ffn, outt], writes=[ffn])
            P.op("dve", lambda e, sl=sl: e.tensor_tensor(ffn[:, sl], ffn[:, sl], pdn[:, 0:1024],
                                                         ALU.add),
                 reads=[ffn, pdn], writes=[ffn])
            P.op("dve", lambda e, sl=sl: e.tensor_tensor(ffn[:, sl], ffn[:, sl], xn32[:, sl],
                                                         ALU.add),
                 reads=[ffn, xn32], writes=[ffn])
        P.op("act", lambda e: e.activation(ffn[:, :], ffn[:, :], AF.Silu), reads=[ffn], writes=[ffn])
        P.op("dve", lambda e, stt=stt: e.tensor_tensor(outt[:, :], ffn[:, :], stt[:, :], ALU.mult),
             reads=[ffn, stt], writes=[outt])
        P.op("dve", lambda e, xl2=xl2: e.tensor_tensor(outt[:, :], outt[:, :], xl2[:, :], ALU.add),
             reads=[outt, xl2], writes=[outt])
        P.dma("sp", bass.AP(OUT.tensor, pr * 256 * D, [[D, 128], [128 * D, 2], [1, D]]),
              outt[:, :], reads=[outt], writes=["OUT"], slot=5)
    ctx["pop_scope"]()


def kernel(**inputs):
    cfg = {}
    ctx = build(cfg)
    phase_routing(ctx)
    phase_routing2(ctx)
    phase_mamba_a(ctx)
    phase_mamba_b(ctx)
    phase_scan(ctx)
    phase_scatter(ctx)
    phase_moe(ctx)
    phase_final(ctx)
    finish(ctx)
    in_maps = make_inputs(inputs, cfg)
    res = run_bass_kernel_spmd(ctx["nc"], in_maps, core_ids=list(range(NCORES)))
    out = np.concatenate([np.asarray(res.results[c]["out"]) for c in range(NCORES)], 0)
    return out.reshape(16, 2048, 1024).astype(np.float32)
